# version 03
# baseline (speedup 1.0000x reference)
import numpy as np
import ml_dtypes
import contextlib
import concourse.bass as bass
import concourse.mybir as mybir
from concourse.bass_utils import run_bass_kernel_spmd

F32 = mybir.dt.float32
BF16 = mybir.dt.bfloat16
AF = mybir.ActivationFunctionType
ALU = mybir.AluOpType
BF = ml_dtypes.bfloat16

D = 2048
DCH = 16
TOK = 4096
TSEQ = 2048
NSEQ = 2
NK, DK = 16, 128
NV, DV = 32, 128
QD, VD = NK * DK, NV * DV
QKVF = 2 * QD + VD
KC = 4
CH = 128
NCH = TOK // CH
CHSEQ = TSEQ // CH
INTER = 8192
EPS_LN = 1e-5
EPS_RMS = 1e-6
EPS_NORM = 1e-12

ENGINES = ("sync", "scalar", "vector", "tensor", "gpsimd")


class Prog:

    def __init__(self):
        self.steps = []
        self.cnt = {}

    def val(self, sem):
        return self.cnt.get(sem, 0)

    def emit(self, engine, fn, waits=(), incs=()):
        for s, amt in incs:
            self.cnt[s] = self.cnt.get(s, 0) + amt
        self.steps.append((engine, fn, tuple(w for w in waits if w), tuple(incs)))
        return tuple(self.cnt[s] for s, _ in incs) if incs else None

    def dma(self, engine, sem, out_ap, in_ap, waits=(), **kw):
        fn = (lambda e, o=out_ap, i=in_ap, kw=dict(kw):
              e.dma_start(out=o, in_=i, **kw))
        self.emit(engine, fn, waits, ((sem, 16),))
        return self.cnt[sem]

    def op(self, engine, fn, waits=(), sem=None, fence=False):
        sem = sem or {"tensor": "PE", "vector": "DV", "scalar": "AC",
                      "gpsimd": "PO"}[engine]
        self.emit(engine, fn, waits, ((sem, 1),))
        if fence:
            self.emit(engine, fence, (), ((sem, 1),))
        return self.cnt[sem]

    def barrier(self, exclude=("PL", "SC")):
        vals = [(s, v) for s, v in self.cnt.items() if v > 0 and s not in exclude]
        for en in ENGINES:
            self.emit(en, lambda e: e.nop(), vals, ())


class Banks:

    def __init__(self, slots):
        self.slots = [{"t": t, "w": None} for t in slots]
        self.i = 0

    def acquire(self):
        s = self.slots[self.i % len(self.slots)]
        self.i += 1
        w = s["w"]
        s["w"] = None
        return s, (w,) if w else ()

    @staticmethod
    def release(slot, sem, val):
        slot["w"] = (sem, val)


def _r(h, pat, **kw):
    return h.rearrange(pat, **kw)


def _copy(e, out, in_):
    if hasattr(e, "tensor_copy"):
        return e.tensor_copy(out, in_)
    return e.copy(out, in_)


DBG_WATCH = None
DBG_B1W = None
B0_PAIRING = True
B0_REVERSE = False
B0_SERIAL = False
B1_MODE = "full"


def build_program(phases=("A", "B0", "B1", "C", "D"), debug=False):
    nc = bass.Bass("TRN2", target_bir_lowering=False)
    P = Prog()

    scratch_kind = "ExternalOutput" if debug else "Internal"
    dbg_watch = DBG_WATCH
    dbg_b1w = DBG_B1W
    dbg_tensor = (nc.dram_tensor("DBGT", [12 * 128, 512], F32, kind="ExternalOutput")
                  if (debug and (dbg_watch or dbg_b1w)) else None)
    dbg_b1 = (nc.dram_tensor("DBGB1", [2 * 128, NK * 256], F32, kind="ExternalOutput")
              if debug else None)

    xt = nc.dram_tensor("xt", [D, TOK], F32, kind="ExternalInput")
    xr = nc.dram_tensor("xr", [TOK, D], F32, kind="ExternalInput")
    wqkvt = nc.dram_tensor("wqkvt", [D, QKVF], F32, kind="ExternalInput")
    wzt = nc.dram_tensor("wzt", [D, VD], F32, kind="ExternalInput")
    wbat = nc.dram_tensor("wbat", [D, 64], F32, kind="ExternalInput")
    convw = nc.dram_tensor("convw", [QKVF, KC], F32, kind="ExternalInput")
    vecs = nc.dram_tensor("vecs", [1, 192], F32, kind="ExternalInput")
    woutt = nc.dram_tensor("woutt", [VD, D], F32, kind="ExternalInput")
    wgut = nc.dram_tensor("wgut", [D, 2 * INTER], F32, kind="ExternalInput")
    wdot = nc.dram_tensor("wdot", [INTER, D], F32, kind="ExternalInput")

    out = nc.dram_tensor("out", [TOK, D], F32, kind="ExternalOutput")

    QS = nc.dram_tensor("QS", [QD, TOK], BF16, kind=scratch_kind)
    KS = nc.dram_tensor("KS", [QD, TOK], BF16, kind=scratch_kind)
    KTS = nc.dram_tensor("KTS", [TOK, QD], BF16, kind=scratch_kind)
    VS = nc.dram_tensor("VS", [TOK, VD], BF16, kind=scratch_kind)
    ZS = nc.dram_tensor("ZS", [TOK, VD], BF16, kind=scratch_kind)
    ABAD = nc.dram_tensor("ABAD", [128, 32 * 64], F32, kind=scratch_kind)
    OS = nc.dram_tensor("OS", [TOK, VD], BF16, kind=scratch_kind)
    H1F = nc.dram_tensor("H1F", [D, TOK], F32, kind=scratch_kind)
    H1B = nc.dram_tensor("H1B", [D, TOK], BF16, kind=scratch_kind)
    B0M = nc.dram_tensor("B0M", [NCH * NK * 128, 1024], BF16, kind=scratch_kind)
    FFS = nc.dram_tensor("FFS", [INTER, TOK], BF16, kind=scratch_kind)
    woutb = nc.dram_tensor("woutb", [VD, D], BF16, kind="Internal")
    wgutb = nc.dram_tensor("wgutb", [D, 2 * INTER], BF16, kind="Internal")
    wdotb = nc.dram_tensor("wdotb", [INTER, D], BF16, kind=scratch_kind)

    incl_np = np.triu(np.ones((128, 128), dtype=np.float32))
    strictlow_np = np.tril(np.ones((128, 128), dtype=np.float32), -1)
    negstrict_np = -np.triu(np.ones((128, 128), dtype=np.float32), 1)
    e127_np = np.zeros((128, 1), dtype=np.float32); e127_np[127, 0] = 1.0
    C_INCL = nc.inline_tensor(incl_np, "c_incl")
    C_SLOW = nc.inline_tensor(strictlow_np, "c_slow")
    C_NEGS = nc.inline_tensor(negstrict_np, "c_negs")
    C_E127 = nc.inline_tensor(e127_np, "c_e127")
    C_ONES1 = nc.inline_tensor(np.ones((1, 128), dtype=np.float32), "c_ones1")
    C_ONESC = nc.inline_tensor(np.ones((128, 1), dtype=np.float32), "c_onesc")
    C_IDBF = nc.inline_tensor(np.eye(128, dtype=np.float32).astype(BF), "c_idbf")
    C_IDF = nc.inline_tensor(np.eye(128, dtype=np.float32), "c_idf")

    es = contextlib.ExitStack()
    sb = lambda name, shape, dt: es.enter_context(nc.sbuf_tensor(name, shape, dt))
    ps = lambda name, shape, dt: es.enter_context(nc.psum_tensor(name, shape, dt))

    s_incl = sb("s_incl", [128, 128], F32)
    s_slow = sb("s_slow", [128, 128], F32)
    s_negs = sb("s_negs", [128, 128], F32)
    s_e127 = sb("s_e127", [128, 1], F32)
    s_ones1 = sb("s_ones1", [1, 128], F32)
    s_onesc = sb("s_onesc", [128, 1], F32)
    s_idbf = sb("s_idbf", [128, 128], BF16)
    s_idf = sb("s_idf", [128, 128], F32)
    s_vec = sb("s_vec", [1, 192], F32)
    s_dtb = sb("s_dtb", [128, 32], F32)
    s_nega = sb("s_nega", [128, 32], F32)
    s_nrmw = sb("s_nrmw", [128, 128], F32)

    PB = [ps(f"pb{i}", [128, 512], F32) for i in range(6)]
    PT = [ps(f"pt{i}", [128, 1024], BF16) for i in range(2)]

    for dst, src in ((s_incl, C_INCL), (s_slow, C_SLOW), (s_negs, C_NEGS),
                     (s_e127, C_E127), (s_ones1, C_ONES1), (s_onesc, C_ONESC),
                     (s_idbf, C_IDBF), (s_idf, C_IDF), (s_vec, vecs)):
        P.dma("sync", "LW", dst[:, :], src[:, :])
    lw0 = P.val("LW")
    pe = P.op("tensor", lambda e: e.matmul(PB[0][:, 0:32], s_ones1[:, :],
                                           s_vec[0:1, 0:32], start=True, stop=True),
              waits=[("LW", lw0)])
    dv = P.op("vector", lambda e: e.tensor_copy(s_dtb[:, :], PB[0][:, 0:32]),
              waits=[("PE", pe)])
    pe = P.op("tensor", lambda e: e.matmul(PB[1][:, 0:32], s_ones1[:, :],
                                           s_vec[0:1, 32:64], start=True, stop=True),
              waits=[("LW", lw0)])
    ac = P.op("scalar", lambda e: e.activation(s_nega[:, :], PB[1][:, 0:32], AF.Exp),
              waits=[("PE", pe)])
    dv = P.op("vector", lambda e: e.tensor_scalar_mul(s_nega[:, :], s_nega[:, :], -1.0),
              waits=[("AC", ac)])
    pe = P.op("tensor", lambda e: e.matmul(PB[2][:, 0:128], s_ones1[:, :],
                                           s_vec[0:1, 64:192], start=True, stop=True),
              waits=[("LW", lw0)])
    dv_nrm = P.op("vector", lambda e: e.tensor_copy(s_nrmw[:, :], PB[2][:, 0:128]),
                  waits=[("PE", pe)])
    init_dv = P.val("DV")
    init_ac = P.val("AC")

    if "A" in phases:
        emit_phase_a(nc, P, es, locals())
    if "C" in phases or "D" in phases:
        emit_precast(nc, P, es, locals())
    if "B0" in phases or "B1" in phases:
        st = {}
        if "B0" in phases:
            st = emit_phase_b0(nc, P, es, locals())
        if "B1" in phases:
            emit_phase_b1(nc, P, es, locals(), st)
    if "C" in phases:
        emit_phase_c(nc, P, es, locals())
    if "D" in phases:
        emit_phase_d(nc, P, es, locals())

    final_waits = [(s, P.val(s)) for s in ("SW", "SS", "SG") if P.val(s)]
    P.emit("gpsimd", lambda e: e.nop(), final_waits, ())

    realize(nc, P, es)
    es.close()
    return nc


def realize(nc, P, es):
    sems = {}
    for name in P.cnt:
        sems[name] = es.enter_context(nc.semaphore(name))
    with nc.Block() as block:
        for eng_name in ENGINES:
            def body(eng, _en=eng_name):
                seen = {}
                for engine, fn, waits, incs in P.steps:
                    if engine != _en:
                        continue
                    for s, v in waits:
                        if v > 0 and seen.get(s, -1) < v:
                            eng.wait_ge(sems[s], v)
                            seen[s] = v
                    inst = fn(eng)
                    for s, amt in incs:
                        inst = inst.then_inc(sems[s], amt)
            getattr(block, eng_name)(body)


def emit_phase_a(nc, P, es, g):
    PB, PT = g["PB"], g["PT"]
    xt, wqkvt, wzt, wbat, convw = g["xt"], g["wqkvt"], g["wzt"], g["wbat"], g["convw"]
    QS, KS, KTS, VS, ZS, ABAD = g["QS"], g["KS"], g["KTS"], g["VS"], g["ZS"], g["ABAD"]
    s_onesc, s_ones1, s_idbf, s_idf = g["s_onesc"], g["s_ones1"], g["s_idbf"], g["s_idf"]
    init_dv, init_ac = g["init_dv"], g["init_ac"]

    esA = contextlib.ExitStack()
    sb = lambda name, shape, dt: esA.enter_context(nc.sbuf_tensor(name, shape, dt))

    XTB = sb("XTB", [128, DCH * TOK], BF16)
    esA1 = contextlib.ExitStack()
    sb1 = lambda name, shape, dt: esA1.enter_context(nc.sbuf_tensor(name, shape, dt))
    WQ = [sb1(f"WQ{i}", [128, DCH * 128], BF16) for i in range(2)]
    QKVP = sb1("QKVP", [128, NSEQ * (TSEQ + 3)], F32)
    CONVOUT = sb1("CONVOUT", [128, TOK], F32)
    OUTBF = sb1("OUTBF", [128, TOK], BF16)
    TSTG = sb1("TSTG", [128, TOK], BF16)
    RROW = sb1("RROW", [1, 512], F32)
    SQT = sb1("SQT", [128, 512], F32)
    FCA = sb1("FCA", [128, 4], F32)
    PADG1 = sb1("PADG1", [128, 64], F32)
    CW = [sb1(f"CW{i}", [128, KC], F32) for i in range(2)]
    PADG2 = sb1("PADG2", [128, 64], F32)

    xt_r = _r(xt, "(k p) t -> k p t", p=128)
    xl = 0
    for k in range(DCH):
        xl = P.dma("gpsimd", "LG", XTB[:, k * TOK:(k + 1) * TOK], xt_r[k], )
    dv0 = P.op("vector", lambda e: e.memset(QKVP[:, 0:3], 0.0))
    dv0 = P.op("vector", lambda e: e.memset(QKVP[:, TSEQ + 3:TSEQ + 6], 0.0))

    wq_r = _r(wqkvt, "(k p) f -> p k f", p=128)
    cwr = convw

    qkv_banks = Banks([PB[0], PB[1]])
    ss_bank = Banks([PB[2]])
    bc_bank = Banks([PB[3]])
    tr_banks = Banks([PT[0], PT[1]])
    trf_banks = Banks([PB[4], PB[5]])

    wq_done = [0, 0]
    cw_done = [0, 0]
    G = {}
    buf = 0
    wq_done[0] = P.dma("gpsimd", "LG",
                       _r(WQ[0], "p (k f) -> p k f", f=128),
                       wq_r[:, :, 0:128])
    cw_done[0] = P.dma("sync", "LW", CW[0][:, :], cwr[0:128, :])

    for warm, m in [(True, 0)] + [(False, mm) for mm in range(64)]:
        buf = m % 2
        nbuf = 1 - buf
        if m + 1 < 64 and not warm:
            wq_done[nbuf] = P.dma(
                "gpsimd", "LG", _r(WQ[nbuf], "p (k f) -> p k f", f=128),
                wq_r[:, :, (m + 1) * 128:(m + 2) * 128],
                waits=G.pop(f"WQ{nbuf}", ()))
            cw_done[nbuf] = P.dma(
                "sync", "LW", CW[nbuf][:, :], cwr[(m + 1) * 128:(m + 2) * 128, :],
                waits=G.pop(f"CW{nbuf}", ()))

        evac_vals = []
        qkvp_guard = G.pop("QKVP", [])
        for n in range(8):
            slot, bw = qkv_banks.acquire()
            pe = None
            for kk in range(DCH):
                w = list(bw) + ([("LG", max(wq_done[buf], xl))] if kk == 0 else [])
                pe = P.op("tensor", (lambda e, b=slot["t"], l=WQ[buf], kk=kk, n=n:
                                     e.matmul(b[:, 0:512],
                                              l[:, kk * 128:(kk + 1) * 128],
                                              XTB[:, kk * TOK + n * 512:kk * TOK + (n + 1) * 512],
                                              start=(kk == 0), stop=(kk == 15))),
                          waits=w)
                bw = ()
            b_, t_ = divmod(n, 4)
            col = 3 + b_ * (TSEQ + 3) + t_ * 512
            eng = "vector" if n % 2 == 0 else "scalar"
            sem = "DV" if n % 2 == 0 else "AC"
            ev = P.op(eng, (lambda e, b=slot["t"], col=col:
                            _copy(e, QKVP[:, col:col + 512], b[:, 0:512])),
                      waits=[("PE", pe)] + qkvp_guard, sem=sem,
                      fence=(lambda e, col=col:
                             e.copy(FCA[:, 1:2], QKVP[:, col + 511:col + 512]))
                      if eng == "scalar" else False)
            Banks.release(slot, sem, ev)
            evac_vals.append((sem, ev))

        conv_vals = []
        conv_guard = G.pop("CONVOUT", [])
        for b_ in range(NSEQ):
            base = b_ * (TSEQ + 3)
            o0 = b_ * TSEQ
            w = evac_vals + [("LW", cw_done[buf])] + conv_guard
            dvv = P.op("vector", (lambda e, base=base, o0=o0, cb=CW[buf]:
                                  e.tensor_scalar(CONVOUT[:, o0:o0 + TSEQ],
                                                  QKVP[:, base:base + TSEQ],
                                                  cb[:, 0:1], None, ALU.mult)),
                       waits=w)
            for j in range(1, KC):
                dvv = P.op("vector", (lambda e, base=base, o0=o0, j=j, cb=CW[buf]:
                                      e.scalar_tensor_tensor(
                                          CONVOUT[:, o0:o0 + TSEQ],
                                          QKVP[:, base + j:base + j + TSEQ],
                                          cb[:, j:j + 1],
                                          CONVOUT[:, o0:o0 + TSEQ],
                                          ALU.mult, ALU.add)),
                           fence=(lambda e, o0=o0: e.tensor_copy(
                               FCA[:, 0:1], CONVOUT[:, o0 + TSEQ - 1:o0 + TSEQ]))
                           if (j == KC - 1 and m >= 32) else False)
            conv_vals.append(("DV", dvv))

        if m < 32:
            mul_vals = []
            outbf_guard = G.pop("OUTBF", [])
            rrow_pe = None
            sqt_pe = None
            for n in range(8):
                sq = P.op("vector", (lambda e, n=n: e.tensor_tensor(
                    SQT[:, :], CONVOUT[:, n * 512:(n + 1) * 512],
                    CONVOUT[:, n * 512:(n + 1) * 512], ALU.mult)),
                    waits=conv_vals + ([("PE", sqt_pe)] if sqt_pe else []),
                    fence=lambda e: e.tensor_copy(FCA[:, 0:1], SQT[:, 511:512]))
                slot, bw = ss_bank.acquire()
                pe = P.op("tensor", (lambda e, b=slot["t"]:
                                     e.matmul(b[0:1, 0:512], s_onesc[:, :],
                                              SQT[:, :],
                                              start=True, stop=True)),
                          waits=[("DV", sq)] + list(bw))
                sqt_pe = pe
                wv = [("PE", pe)] + ([("PE", rrow_pe)] if rrow_pe else [])
                ac = P.op("scalar", (lambda e, b=slot["t"]:
                                     e.activation(RROW[0:1, :], b[0:1, 0:512], AF.Sqrt)),
                          waits=wv,
                          fence=lambda e: e.copy(FCA[0:1, 1:2], RROW[0:1, 511:512]))
                Banks.release(slot, "AC", ac)
                dvv = P.op("vector", lambda e: e.tensor_scalar_max(
                    RROW[0:1, :], RROW[0:1, :], 1e-12), waits=[("AC", ac)])
                dvv = P.op("vector", lambda e: e.reciprocal(RROW[0:1, :], RROW[0:1, :]),
                           fence=lambda e: e.tensor_copy(FCA[0:1, 0:1],
                                                         RROW[0:1, 511:512]))
                bslot, bbw = bc_bank.acquire()
                pe2 = P.op("tensor", (lambda e, b=bslot["t"]:
                                      e.matmul(b[:, 0:512], s_ones1[:, :],
                                               RROW[0:1, :], start=True, stop=True)),
                           waits=[("DV", dvv)] + list(bbw))
                rrow_pe = pe2
                mv = P.op("vector", (lambda e, b=bslot["t"], n=n:
                                     e.tensor_tensor(OUTBF[:, n * 512:(n + 1) * 512],
                                                     CONVOUT[:, n * 512:(n + 1) * 512],
                                                     b[:, 0:512], ALU.mult)),
                          waits=[("PE", pe2)] + (outbf_guard if n == 0 else []),
                          fence=(lambda e, n=n: e.tensor_copy(
                              FCA[:, 2:3], OUTBF[:, (n + 1) * 512 - 1:(n + 1) * 512]))
                          if n == 7 else False)
                Banks.release(bslot, "DV", mv)
                mul_vals.append(("DV", mv))
            dst = QS if m < 16 else KS
            row = (m if m < 16 else m - 16) * 128
            if not warm:
                stv = P.dma("sync", "SW", dst[row:row + 128, :], OUTBF[:, :],
                            waits=mul_vals)
                G["OUTBF"] = [("SW", stv)]
            if m >= 16:
                tr_vals = emit_transpose_block(nc, P, tr_banks, OUTBF, TSTG,
                                               s_idbf, mul_vals,
                                               G.pop("TSTG", []), fence_t=FCA)
                stv2 = P.dma("sync", "SW",
                             _r(KTS, "(c p) f -> p c f", p=128)[:, :, row:row + 128],
                             _r(TSTG, "p (c f) -> p c f", f=128),
                             waits=tr_vals)
                G["TSTG"] = [("SW", stv2)]
                G["OUTBF"].append(("PE", P.val("PE")))
            G["QKVP"] = [("DV", P.val("DV")), ("PE", P.val("PE"))]
        else:
            tr_vals = emit_transpose_block(nc, P, trf_banks, CONVOUT, TSTG,
                                           s_idf, conv_vals, G.pop("TSTG", []),
                                           fence_t=FCA)
            row = (m - 32) * 128
            stv2 = P.dma("sync", "SW",
                         _r(VS, "(c p) f -> p c f", p=128)[:, :, row:row + 128],
                         _r(TSTG, "p (c f) -> p c f", f=128),
                         waits=tr_vals)
            G["TSTG"] = [("SW", stv2)]
            G["CONVOUT"] = [("PE", P.val("PE"))]
            G["QKVP"] = [("DV", P.val("DV")), ("AC", P.val("AC"))]
        G[f"WQ{buf}"] = [("PE", P.val("PE"))]
        G[f"CW{buf}"] = [("DV", P.val("DV"))]

    P.barrier()
    esA1.close()
    WZ = [sb(f"WZ{i}", [128, DCH * 512], BF16) for i in range(2)]
    ZSTG = [sb(f"ZSTG{i}", [128, 512], BF16) for i in range(2)]
    FCA2 = sb("FCA2", [128, 2], F32)
    wz_r = _r(wzt, "(k p) f -> p k f", p=128)
    z_banks = Banks([PB[0], PB[1]])
    wz_done = [0, 0]
    wz_done[0] = P.dma("gpsimd", "LG", _r(WZ[0], "p (k f) -> p k f", f=512),
                       wz_r[:, :, 0:512])
    zst_prev = [None, None]
    wz_guard = [None, None]
    for s in range(8):
        buf = s % 2
        nbuf = 1 - buf
        if s + 1 < 8:
            wz_done[nbuf] = P.dma("gpsimd", "LG",
                                  _r(WZ[nbuf], "p (k f) -> p k f", f=512),
                                  wz_r[:, :, (s + 1) * 512:(s + 2) * 512],
                                  waits=[wz_guard[nbuf]] if wz_guard[nbuf] else ())
        for tb in range(32):
            slot, bw = z_banks.acquire()
            pe = None
            for kk in range(DCH):
                w = list(bw) + ([("LG", wz_done[buf])] if kk == 0 else [])
                pe = P.op("tensor", (lambda e, b=slot["t"], kk=kk, tb=tb, l=WZ[buf]:
                                     e.matmul(b[:, 0:512],
                                              XTB[:, kk * TOK + tb * 128:kk * TOK + tb * 128 + 128],
                                              l[:, kk * 512:(kk + 1) * 512],
                                              start=(kk == 0), stop=(kk == 15))),
                          waits=w)
                bw = ()
            zb = tb % 2
            w = [("PE", pe)] + ([zst_prev[zb]] if zst_prev[zb] else [])
            ac = P.op("scalar", (lambda e, b=slot["t"], zb=zb:
                                 e.activation(ZSTG[zb][:, :], b[:, 0:512], AF.Silu)),
                      waits=w,
                      fence=(lambda e, zb=zb:
                             e.copy(FCA2[:, 0:1], ZSTG[zb][:, 511:512])))
            Banks.release(slot, "AC", ac)
            stv = P.dma("scalar", "SS", ZS[tb * 128:(tb + 1) * 128, s * 512:(s + 1) * 512],
                        ZSTG[zb][:, :], waits=[("AC", ac)])
            zst_prev[zb] = ("SS", stv)
        wz_guard[buf] = ("PE", P.val("PE"))

    WBA = sb("WBA", [128, DCH * 64], BF16)
    ABAS = sb("ABAS", [128, 32 * 64], F32)
    wba_done = P.dma("gpsimd", "LG", _r(WBA, "p (k f) -> p k f", f=64),
                     _r(wbat, "(k p) f -> p k f", p=128))
    ba_banks = Banks([PB[2], PB[3]])
    aba_vals = []
    for tb in range(32):
        slot, bw = ba_banks.acquire()
        pe = None
        for kk in range(DCH):
            w = list(bw) + ([("LG", wba_done)] if kk == 0 else [])
            pe = P.op("tensor", (lambda e, b=slot["t"], kk=kk, tb=tb:
                                 e.matmul(b[:, 0:64],
                                          XTB[:, kk * TOK + tb * 128:kk * TOK + tb * 128 + 128],
                                          WBA[:, kk * 64:(kk + 1) * 64],
                                          start=(kk == 0), stop=(kk == 15))),
                      waits=w)
            bw = ()
        dvv = P.op("vector", (lambda e, b=slot["t"], tb=tb:
                              e.tensor_copy(ABAS[:, tb * 64:(tb + 1) * 64], b[:, 0:64])),
                   waits=[("PE", pe)],
                   fence=(lambda e, tb=tb: e.tensor_copy(
                       FCA2[:, 1:2], ABAS[:, (tb + 1) * 64 - 1:(tb + 1) * 64]))
                   if tb == 31 else False)
        Banks.release(slot, "DV", dvv)
        aba_vals.append(("DV", dvv))
    P.dma("sync", "SW", ABAD[:, :], ABAS[:, :], waits=aba_vals)

    P.barrier()
    esA.close()


def emit_transpose_block(nc, P, banks, src, dststg, ident, in_vals, stg_guard,
                         fence_t=None):
    vals = []
    for grp in range(8):
        slot, bw = banks.acquire()
        pe = None
        for q in range(4):
            c = grp * 4 + q
            w = list(bw) + (list(in_vals) if q == 0 else [])
            pe = P.op("tensor", (lambda e, b=slot["t"], c=c, q=q:
                                 e.transpose(b[:, q * 128:(q + 1) * 128],
                                             src[:, c * 128:(c + 1) * 128],
                                             ident[:, :])),
                      waits=w)
            bw = ()
        eng, sem = ("vector", "DV") if grp % 2 == 0 else ("scalar", "AC")
        fc = False
        if fence_t is not None:
            fc = (lambda e, grp=grp, col=(2 if grp % 2 == 0 else 3):
                  _copy(e, fence_t[:, col:col + 1],
                        dststg[:, (grp + 1) * 512 - 1:(grp + 1) * 512]))
        ev = P.op(eng, (lambda e, b=slot["t"], grp=grp:
                        _copy(e, dststg[:, grp * 512:(grp + 1) * 512],
                              b[:, 0:512])),
                  waits=[("PE", pe)] + (list(stg_guard) if grp == 0 else []),
                  sem=sem, fence=fc)
        Banks.release(slot, sem, ev)
        vals.append((sem, ev))
    return vals


def emit_phase_b0(nc, P, es, g):
    PB, PT = g["PB"], g["PT"]
    QS, KS, KTS, VS, ABAD, B0M = (g["QS"], g["KS"], g["KTS"], g["VS"],
                                  g["ABAD"], g["B0M"])
    s_incl, s_slow, s_negs = g["s_incl"], g["s_slow"], g["s_negs"]
    s_e127, s_ones1, s_idbf, s_idf = (g["s_e127"], g["s_ones1"], g["s_idbf"],
                                      g["s_idf"])
    s_dtb, s_nega = g["s_dtb"], g["s_nega"]

    P.barrier()
    ECUMR = es.enter_context(nc.sbuf_tensor("ECUMR", [128, NCH * 32], F32))
    ECCR = es.enter_context(nc.sbuf_tensor("ECCR", [128, NCH * 32], F32))

    esB = contextlib.ExitStack()
    sb = lambda name, shape, dt: esB.enter_context(nc.sbuf_tensor(name, shape, dt))

    ABR = sb("ABR", [128, 32 * 64], F32)
    QFB = [sb(f"QFB{i}", [128, 2048], BF16) for i in range(2)]
    KFB = [sb(f"KFB{i}", [128, 2048], BF16) for i in range(2)]
    KTB = [sb(f"KTB{i}", [128, 2048], BF16) for i in range(2)]
    VTB = [sb(f"VTB{i}", [128, 4096], BF16) for i in range(2)]
    SM = []
    for i in range(2):
        SM.append(dict(
            BB=sb(f"BB{i}", [128, 32], F32),
            LNB=sb(f"LNB{i}", [128, 32], F32),
            T1=sb(f"T1_{i}", [128, 32], F32),
            T2=sb(f"T2_{i}", [128, 32], F32),
            GG=sb(f"GG{i}", [128, 32], F32),
            TRIN=sb(f"TRIN{i}", [128, 64], F32),
            CPRV=sb(f"CPRV{i}", [128, 32], F32),
            ESCC=sb(f"ESCC{i}", [128, 32], F32),
            EBC=sb(f"EBC{i}", [128, 32], F32),
            ECCW=sb(f"ECCW{i}", [1, 32], F32),
            FSM=sb(f"FSM{i}", [128, 4], F32),
            done=[],
        ))
    HS = []
    for i in range(2):
        HS.append(dict(
            MG=sb(f"MG{i}", [128, 128], F32),
            MKQ=sb(f"MKQ{i}", [128, 128], F32),
            EG=sb(f"EG{i}", [128, 256], F32),
            EXP1=sb(f"EXP1_{i}", [128, 256], F32),
            EXP2=sb(f"EXP2_{i}", [128, 256], F32),
            NP=[sb(f"NP{i}_{j}", [128, 256], BF16) for j in range(7)],
            TPS=[sb(f"TPS{i}_{j}", [128, 256], BF16) for j in range(7)],
            AT=sb(f"AT{i}", [128, 256], BF16),
            QQ=sb(f"QQ{i}", [128, 256], BF16),
            WN=sb(f"WN{i}", [128, 256], BF16),
            BV=sb(f"BV{i}", [128, 256], BF16),
            KP=sb(f"KP{i}", [128, 256], BF16),
            TW=sb(f"TW{i}", [128, 256], BF16),
            U0=sb(f"U0{i}", [128, 256], BF16),
            REC=sb(f"REC{i}", [128, 1024], BF16),
            ROWT=sb(f"ROWT{i}", [1, 512], F32),
            FDB=sb(f"FDB{i}", [128, 8], BF16),
            guard=[], rec_guard=[],
        ))

    fb = Banks(PB)
    tb = Banks(PT)

    abr_done = P.dma("sync", "LW", ABR[:, :], ABAD[:, :])
    qs_r = _r(QS, "(h p) t -> p h t", p=128)
    ks_r = _r(KS, "(h p) t -> p h t", p=128)
    ld_done = [None, None]

    def load_chunk(c, guards):
        buf = c % 2
        P.dma("sync", "LW", _r(QFB[buf], "p (h t) -> p h t", t=128),
              qs_r[:, :, c * 128:(c + 1) * 128], waits=list(guards))
        P.dma("sync", "LW", _r(KFB[buf], "p (h t) -> p h t", t=128),
              ks_r[:, :, c * 128:(c + 1) * 128])
        P.dma("sync", "LW", KTB[buf][:, :], KTS[c * 128:(c + 1) * 128, :])
        v4 = P.dma("sync", "LW", VTB[buf][:, :], VS[c * 128:(c + 1) * 128, :])
        ld_done[buf] = ("LW", v4)

    def chunk_smalls(c, guards):
        par = c % 2
        s = SM[par]
        w = [("LW", abr_done)] + list(guards)
        ab = ABR[:, c * 64:c * 64 + 32]
        aa = ABR[:, c * 64 + 32:c * 64 + 64]
        vBB = P.op("scalar", lambda e, ab=ab: e.activation(s["BB"][:, :], ab, AF.Sigmoid),
                   waits=w)
        vabs = P.op("scalar", lambda e, ab=ab: e.activation(s["T1"][:, :], ab, AF.Abs),
                    waits=w)
        P.op("scalar", lambda e: e.activation(s["T1"][:, :], s["T1"][:, :], AF.Exp,
                                              scale=-1.0))
        vL = P.op("scalar", lambda e: e.activation(s["T1"][:, :], s["T1"][:, :], AF.Ln,
                                                   bias=1.0),
                  fence=lambda e: e.copy(s["FSM"][:, 1:2], s["T1"][:, 31:32]))
        P.op("vector", (lambda e, ab=ab: e.tensor_scalar(s["LNB"][:, :], ab, 0.0, None,
                                                         ALU.min)), waits=w)
        P.op("vector", (lambda e: e.tensor_tensor(s["LNB"][:, :], s["LNB"][:, :],
                                                  s["T1"][:, :], ALU.subtract)),
             waits=[("AC", vL)])
        vdt = P.op("vector", (lambda e, aa=aa: e.tensor_tensor(s["T2"][:, :], aa,
                                                               s_dtb[:, :], ALU.add)),
                   waits=w,
                   fence=lambda e: e.tensor_copy(s["FSM"][:, 0:1], s["T2"][:, 31:32]))
        vabs2 = P.op("scalar", lambda e: e.activation(s["GG"][:, :], s["T2"][:, :], AF.Abs),
                     waits=[("DV", vdt)])
        P.op("scalar", lambda e: e.activation(s["GG"][:, :], s["GG"][:, :], AF.Exp,
                                              scale=-1.0))
        vLa = P.op("scalar", lambda e: e.activation(s["GG"][:, :], s["GG"][:, :], AF.Ln,
                                                    bias=1.0),
                   fence=lambda e: e.copy(s["FSM"][:, 1:2], s["GG"][:, 31:32]))
        P.op("vector", (lambda e: e.tensor_scalar(s["T2"][:, :], s["T2"][:, :], 0.0,
                                                  None, ALU.max)),
             waits=[("AC", vabs2)])
        P.op("vector", (lambda e: e.tensor_tensor(s["T2"][:, :], s["T2"][:, :],
                                                  s["GG"][:, :], ALU.add)),
             waits=[("AC", vLa)])
        vgg = P.op("vector", (lambda e: e.tensor_tensor(s["GG"][:, :], s["T2"][:, :],
                                                        s_nega[:, :], ALU.mult)),
                   fence=lambda e: e.tensor_copy(s["FSM"][:, 0:1], s["GG"][:, 31:32]))
        slot, bw = fb.acquire()
        pcum = P.op("tensor", (lambda e, b=slot["t"]:
                               e.matmul(b[:, 0:32], s_incl[:, :], s["GG"][:, :],
                                        start=True, stop=True)),
                    waits=[("DV", vgg)] + list(bw))
        vcum = P.op("vector", (lambda e, b=slot["t"]:
                               e.tensor_copy(s["TRIN"][:, 32:64], b[:, 0:32])),
                    waits=[("PE", pcum)],
                    fence=lambda e: e.tensor_copy(s["FSM"][:, 0:1],
                                                  s["TRIN"][:, 63:64]))
        Banks.release(slot, "DV", vcum)
        vcp = P.op("vector", (lambda e: e.tensor_tensor(s["CPRV"][:, :],
                                                        s["TRIN"][:, 32:64],
                                                        s["GG"][:, :], ALU.subtract)),
                   fence=lambda e: e.tensor_copy(s["FSM"][:, 0:1],
                                                 s["CPRV"][:, 31:32]))
        vlbc = P.op("vector", (lambda e: e.tensor_tensor(s["TRIN"][:, 0:32],
                                                         s["LNB"][:, :],
                                                         s["CPRV"][:, :], ALU.add)))
        slot, bw = fb.acquire()
        prev = P.op("tensor", (lambda e, b=slot["t"]:
                               e.matmul(b[:, 0:32], s_slow[:, :], s["GG"][:, :],
                                        start=True, stop=True)),
                    waits=[("DV", vgg)] + list(bw))
        vesc = P.op("scalar", (lambda e, b=slot["t"]:
                               e.activation(s["ESCC"][:, :], b[:, 0:32], AF.Exp)),
                    waits=[("PE", prev)])
        Banks.release(slot, "AC", vesc)
        vecum = P.op("scalar", (lambda e, c=c: e.activation(
            ECUMR[:, c * 32:(c + 1) * 32], s["TRIN"][:, 32:64], AF.Exp)),
            waits=[("DV", vcum)])
        vecp = P.op("scalar", (lambda e: e.activation(s["EBC"][:, :], s["CPRV"][:, :],
                                                      AF.Exp)), waits=[("DV", vcp)],
                    fence=lambda e: e.copy(s["FSM"][:, 1:2], s["EBC"][:, 31:32]))
        veb = P.op("vector", (lambda e: e.tensor_tensor(s["EBC"][:, :], s["EBC"][:, :],
                                                        s["BB"][:, :], ALU.mult)),
                   waits=[("AC", vecp), ("AC", vBB)])
        P.op("vector", lambda e: e.tensor_scalar_mul(s["EBC"][:, :], s["EBC"][:, :],
                                                     -1.0))
        slot, bw = fb.acquire()
        pec = P.op("tensor", (lambda e, b=slot["t"]:
                              e.matmul(b[0:1, 0:32], s_e127[:, :], s["TRIN"][:, 32:64],
                                       start=True, stop=True)),
                   waits=[("DV", vcum)] + list(bw))
        vecc = P.op("scalar", (lambda e, b=slot["t"]:
                               e.activation(s["ECCW"][0:1, :], b[0:1, 0:32], AF.Exp)),
                    waits=[("PE", pec)],
                    fence=lambda e: e.copy(s["FSM"][0:1, 1:2], s["ECCW"][0:1, 31:32]))
        Banks.release(slot, "AC", vecc)
        slot, bw = fb.acquire()
        pbc = P.op("tensor", (lambda e, b=slot["t"]:
                              e.matmul(b[:, 0:32], s_ones1[:, :], s["ECCW"][0:1, :],
                                       start=True, stop=True)),
                   waits=[("AC", vecc)] + list(bw))
        vc2 = P.op("vector", (lambda e, b=slot["t"], c=c:
                              e.tensor_copy(ECCR[:, c * 32:(c + 1) * 32], b[:, 0:32])),
                   waits=[("PE", pbc)])
        Banks.release(slot, "DV", vc2)
        fv = P.op("vector", (lambda e, c=c: e.tensor_tensor(
            s["FSM"][:, 0:1], s["TRIN"][:, 63:64],
            ECCR[:, c * 32 + 31:c * 32 + 32], ALU.add)))
        P.op("scalar", lambda e: e.copy(s["FSM"][0:1, 1:2], s["ECCW"][0:1, 31:32]))
        fa = P.op("scalar", lambda e: e.copy(s["FSM"][:, 2:3], s["ESCC"][:, 31:32]))
        s["done"] = [("DV", fv), ("AC", fa)]
        return s["done"]

    def head_stages(c, h, hs, flat_done):
        par = c % 2
        s = SM[par]
        buf = c % 2
        QF, KF, KT, VT = QFB[buf], KFB[buf], KTB[buf], VTB[buf]
        st = {}
        vh0 = 2 * h
        g0 = list(hs["guard"])
        hs["guard"] = []

        def col(tile, j):
            return tile[:, j:j + 1]

        def colrep(tile, j, n):
            return bass.AP(tile, j, [[tile.shape[1], 128], [0, n]])

        def rep2(t):
            return bass.AP(t, 0, [[t.shape[1], 128], [0, 2], [1, 128]])

        def stage_grams():
            w = [ld_done[buf]] + g0
            slot, bw = fb.acquire()
            p1 = P.op("tensor", (lambda e, b=slot["t"]:
                                 e.matmul(b[:, 0:128], KF[:, h * 128:(h + 1) * 128],
                                          KF[:, h * 128:(h + 1) * 128],
                                          start=True, stop=True)),
                      waits=w + list(bw))
            v1 = P.op("vector", (lambda e, b=slot["t"]:
                                 e.tensor_tensor(hs["MG"][:, :], b[:, 0:128],
                                                 s_negs[:, :], ALU.mult)),
                      waits=[("PE", p1)] + g0)
            Banks.release(slot, "DV", v1)
            slot, bw = fb.acquire()
            p2 = P.op("tensor", (lambda e, b=slot["t"]:
                                 e.matmul(b[:, 0:128], KF[:, h * 128:(h + 1) * 128],
                                          QF[:, h * 128:(h + 1) * 128],
                                          start=True, stop=True)),
                      waits=list(bw))
            v2 = P.op("vector", (lambda e, b=slot["t"]:
                                 e.tensor_tensor(hs["MKQ"][:, :], b[:, 0:128],
                                                 s_incl[:, :], ALU.mult)),
                      waits=[("PE", p2)])
            Banks.release(slot, "DV", v2)
            st["mg"] = ("DV", v1)
            st["mkq"] = ("DV", v2)

        def stage_rows():
            slot, bw = fb.acquire()
            pe = None
            for qq, ci in enumerate((vh0, vh0 + 1, 32 + vh0, 32 + vh0 + 1)):
                pe = P.op("tensor", (lambda e, b=slot["t"], qq=qq, ci=ci:
                                     e.transpose(b[0:1, qq * 128:(qq + 1) * 128],
                                                 s["TRIN"][:, ci:ci + 1],
                                                 s_idf[:, :])),
                          waits=(s["done"] + list(bw) + g0) if qq == 0 else [])
            ev = P.op("vector", (lambda e, b=slot["t"]:
                                 e.tensor_copy(hs["ROWT"][0:1, 0:512],
                                               b[0:1, 0:512])),
                      waits=[("PE", pe)] + g0)
            Banks.release(slot, "DV", ev)
            st["rows"] = ("DV", ev)

        def mkexp(which):
            def fn():
                fcol = 0 if which == 1 else 256
                slot, bw = fb.acquire()
                pb_ = P.op("tensor", (lambda e, b=slot["t"]:
                                      e.matmul(b[:, 0:256], s_ones1[:, :],
                                               hs["ROWT"][0:1, fcol:fcol + 256],
                                               start=True, stop=True)),
                          waits=[st["rows"]] + list(bw) + g0)
                ev = None
                for p in range(2):
                    ww = [("PE", pb_)] + s["done"] + g0
                    if which == 2:
                        ww.append(st["exp1"])
                    ev = P.op("vector", (lambda e, b=slot["t"], p=p:
                                         e.tensor_tensor(
                                             hs["EG"][:, p * 128:(p + 1) * 128],
                                             b[:, p * 128:(p + 1) * 128],
                                             colrep(s["TRIN"], 32 + vh0 + p, 128),
                                             ALU.subtract)),
                              waits=ww)
                ev = P.op("vector", (lambda e: e.tensor_scalar(
                    hs["EG"][:, :], hs["EG"][:, :], 0.0, None, ALU.min)))
                Banks.release(slot, "DV", ev)
                dst = hs["EXP1"] if which == 1 else hs["EXP2"]
                ex = P.op("scalar", (lambda e, d=dst: e.activation(d[:, :],
                                                                   hs["EG"][:, :],
                                                                   AF.Exp)),
                          waits=[("DV", ev)] + g0)
                st[f"exp{which}"] = ("AC", ex)
            return fn

        def stage_ntat():
            v1 = P.op("vector", (lambda e: e.tensor_tensor(
                _r(hs["NP"][0], "p (r t) -> p r t", t=128),
                _r(hs["EXP1"], "p (r t) -> p r t", t=128),
                rep2(hs["MG"]), ALU.mult)),
                waits=[st["exp1"], st["mg"]] + g0)
            v2 = P.op("vector", (lambda e: e.tensor_tensor(
                _r(hs["AT"], "p (r t) -> p r t", t=128),
                _r(hs["EXP2"], "p (r t) -> p r t", t=128),
                rep2(hs["MKQ"]), ALU.mult)),
                waits=[st["exp2"], st["mkq"]])
            st["np0"] = ("DV", v1)
            st["at"] = ("DV", v2)

        def mklevel(j):
            def fn():
                src = hs["NP"][j - 1]
                srcv = st[f"np{j - 1}"]
                slot, bw = tb.acquire()
                pe = None
                for p in range(2):
                    pe = P.op("tensor", (lambda e, b=slot["t"], p=p:
                                         e.transpose(b[:, p * 128:(p + 1) * 128],
                                                     src[:, p * 128:(p + 1) * 128],
                                                     s_idbf[:, :])),
                              waits=([srcv] + list(bw) + g0) if p == 0 else [])
                eng, sem = ("scalar", "AC") if j % 2 else ("vector", "DV")
                ev = P.op(eng, (lambda e, b=slot["t"], j=j:
                                _copy(e, hs["TPS"][j - 1][:, :], b[:, 0:256])),
                          waits=[("PE", pe)] + g0, sem=sem)
                Banks.release(slot, sem, ev)
                st[f"tps{j - 1}"] = (sem, ev)
                if j <= 6:
                    slot2, bw2 = fb.acquire()
                    pe2 = None
                    for p in range(2):
                        pe2 = P.op("tensor", (lambda e, b=slot2["t"], p=p, j=j:
                                              e.matmul(b[:, p * 128:(p + 1) * 128],
                                                       hs["TPS"][j - 1][:, p * 128:(p + 1) * 128],
                                                       src[:, p * 128:(p + 1) * 128],
                                                       start=(p == 0), stop=(p == 1))),
                                   waits=([(sem, ev)] + list(bw2)) if p == 0 else [])
                    eng2, sem2 = ("vector", "DV") if j % 2 else ("scalar", "AC")
                    ev2 = P.op(eng2, (lambda e, b=slot2["t"], j=j:
                                      _copy(e, hs["NP"][j][:, :], b[:, 0:256])),
                               waits=[("PE", pe2)] + g0, sem=sem2)
                    Banks.release(slot2, sem2, ev2)
                    st[f"np{j}"] = (sem2, ev2)
            return fn

        def stage_qq_init():
            v = P.op("vector", (lambda e: e.tensor_tensor(
                _r(hs["QQ"], "p (r t) -> p r t", t=128),
                _r(hs["NP"][0], "p (r t) -> p r t", t=128),
                rep2(s_idbf), ALU.add)),
                waits=[st["np0"]] + g0)
            st["qq"] = ("DV", v)

        def mkprod(j):
            def fn():
                slot, bw = fb.acquire()
                pe = None
                for p in range(2):
                    pe = P.op("tensor", (lambda e, b=slot["t"], p=p, j=j:
                                         e.matmul(b[:, p * 128:(p + 1) * 128],
                                                  hs["TPS"][j][:, p * 128:(p + 1) * 128],
                                                  hs["QQ"][:, p * 128:(p + 1) * 128],
                                                  start=(p == 0), stop=(p == 1))),
                              waits=([st["qq"], st[f"tps{j}"]] + list(bw)) if p == 0 else [])
                v = P.op("vector", (lambda e, b=slot["t"]:
                                    e.tensor_tensor(hs["QQ"][:, :], hs["QQ"][:, :],
                                                    b[:, 0:256], ALU.add)),
                         waits=[("PE", pe)])
                Banks.release(slot, "DV", v)
                st["qq"] = ("DV", v)
            return fn

        def stage_prep():
            w = [ld_done[buf]] + s["done"] + g0
            P.op("vector", (lambda e: e.tensor_tensor(
                hs["WN"][:, 0:128], KT[:, h * 128:(h + 1) * 128],
                colrep(s["EBC"], vh0, 128), ALU.mult)), waits=w)
            P.op("vector", (lambda e: e.tensor_tensor(
                hs["WN"][:, 128:256], KT[:, h * 128:(h + 1) * 128],
                colrep(s["EBC"], vh0 + 1, 128), ALU.mult)))
            P.op("vector", (lambda e: e.tensor_tensor(
                hs["BV"][:, 0:128], VT[:, vh0 * 128:vh0 * 128 + 128],
                colrep(s["BB"], vh0, 128), ALU.mult)))
            P.op("vector", (lambda e: e.tensor_tensor(
                hs["BV"][:, 128:256], VT[:, vh0 * 128 + 128:vh0 * 128 + 256],
                colrep(s["BB"], vh0 + 1, 128), ALU.mult)))
            P.op("vector", (lambda e: e.tensor_tensor(
                hs["KP"][:, 0:128], KT[:, h * 128:(h + 1) * 128],
                colrep(s["ESCC"], vh0, 128), ALU.mult)))
            v3 = P.op("vector", (lambda e: e.tensor_tensor(
                hs["KP"][:, 128:256], KT[:, h * 128:(h + 1) * 128],
                colrep(s["ESCC"], vh0 + 1, 128), ALU.mult)))
            st["prep"] = ("DV", v3)

        def mkfinal(name, lhs_key, rhs_key, wkeys, outcol):
            def fn():
                lhs, rhs = hs[lhs_key], hs[rhs_key]
                wk = [st[k] for k in wkeys]
                slot, bw = fb.acquire()
                pe = None
                for p in range(2):
                    pe = P.op("tensor", (lambda e, b=slot["t"], p=p:
                                         e.matmul(b[:, p * 128:(p + 1) * 128],
                                                  lhs[:, p * 128:(p + 1) * 128],
                                                  rhs[:, p * 128:(p + 1) * 128],
                                                  start=(p == 0), stop=(p == 1))),
                              waits=(wk + list(bw)) if p == 0 else [])
                if outcol is None:
                    dst = hs["TW"] if name == "tw" else hs["U0"]
                    ev = P.op("scalar", (lambda e, b=slot["t"], d=dst:
                                         _copy(e, d[:, :], b[:, 0:256])),
                              waits=[("PE", pe)] + g0, sem="AC")
                else:
                    ev = P.op("scalar", (lambda e, b=slot["t"], oc=outcol:
                                         _copy(e, hs["REC"][:, oc:oc + 256],
                                               b[:, 0:256])),
                              waits=[("PE", pe)] + g0 + hs["rec_guard"],
                              sem="AC",
                              fence=(lambda e, oc=outcol:
                                     e.copy(hs["FDB"][:, 0:1],
                                            hs["REC"][:, oc + 255:oc + 256])))
                st[f"{name}v"] = ("AC", ev)
                Banks.release(slot, "AC", ev)
            return fn

        def stage_store():
            w = [st[k] for k in ("motv", "mstv", "oindv", "kindv")]
            if g.get("dbg_watch") == (c, h):
                dbgt = g["dbg_tensor"]
                for i, key in enumerate(("AT", "TW", "U0", "KP", "QQ", "EXP2",
                                         "MKQ", "WN")):
                    wid = min(256, hs[key].shape[1])
                    P.dma("gpsimd", "SG", dbgt[i * 128:(i + 1) * 128, 0:wid],
                          hs[key][:, 0:wid], waits=w)
                P.dma("gpsimd", "SG", dbgt[8 * 128:9 * 128, 0:64],
                      s["TRIN"][:, 0:64], waits=w)
                P.dma("gpsimd", "SG", dbgt[9 * 128:9 * 128 + 1, 0:512],
                      hs["ROWT"][0:1, 0:512], waits=w)
                P.dma("gpsimd", "SG", dbgt[10 * 128:11 * 128, 0:32],
                      s["EBC"][:, 0:32], waits=w)
                P.dma("gpsimd", "SG", dbgt[11 * 128:12 * 128, 0:256],
                      hs["EG"][:, 0:256], waits=w)
            r = (c * NK + h) * 128
            v = P.dma("sync", "SW", B0M[r:r + 128, :], hs["REC"][:, :], waits=w)
            hs["rec_guard"] = [("SW", v)]
            hs["guard"] = [("PE", P.val("PE")), ("DV", P.val("DV")),
                           ("AC", P.val("AC")), ("PO", P.val("PO"))]
            if g.get("dbg_watch") == (c, h):
                hs["guard"].append(("SG", P.val("SG")))

        stages = [stage_grams, stage_rows, mkexp(1), mkexp(2), stage_ntat]
        stages += [mklevel(j) for j in range(1, 8)]
        stages += [stage_qq_init]
        stages += [mkprod(j) for j in range(1, 7)]
        stages += [stage_prep]
        stages += [mkfinal("tw", "QQ", "WN", ["qq", "prep"], None)]
        stages += [mkfinal("u0", "QQ", "BV", ["qq", "prep"], None)]
        stages += [mkfinal("mst", "TW", "KP", ["twv", "prep"], 256)]
        stages += [mkfinal("oind", "AT", "U0", ["at", "u0v"], 512)]
        stages += [mkfinal("kind", "KP", "U0", ["prep", "u0v"], 768)]
        stages += [mkfinal("mot", "TW", "AT", ["twv", "at"], 0)]
        stages += [stage_store]
        return stages

    if B0_SERIAL:
        head_guard = []
        for c in range(NCH):
            P.barrier()
            load_chunk(c, [])
            flat_done = chunk_smalls(c, [])
            for h in range(NK):
                st0 = head_stages(c, h, HS[0], flat_done)
                for i in range(len(st0)):
                    st0[i]()
        P.barrier()
        if g["debug"]:
            ECUMD = nc.dram_tensor("ECUMD", [128, NCH * 32], F32, kind="ExternalOutput")
            ECCD = nc.dram_tensor("ECCD", [128, NCH * 32], F32, kind="ExternalOutput")
            P.dma("sync", "SW", ECUMD[:, :], ECUMR[:, :])
            P.dma("sync", "SW", ECCD[:, :], ECCR[:, :])
        esB.close()
        return dict(ECUMR=ECUMR, ECCR=ECCR)

    load_chunk(0, [])
    head_guard = []
    for c in range(NCH):
        if c + 1 < NCH:
            load_chunk(c + 1, head_guard)
        flat_done = chunk_smalls(c, head_guard)
        head_guard = []
        if B0_PAIRING:
            for hp in range(NK // 2):
                st0 = head_stages(c, 2 * hp, HS[0], flat_done)
                st1 = head_stages(c, 2 * hp + 1, HS[1], flat_done)
                for i in range(len(st0)):
                    st0[i]()
                    st1[i]()
        else:
            order = range(NK - 1, -1, -1) if B0_REVERSE else range(NK)
            for h in order:
                st0 = head_stages(c, h, HS[0], flat_done)
                for i in range(len(st0)):
                    st0[i]()
        head_guard = [("PE", P.val("PE")), ("DV", P.val("DV")),
                      ("AC", P.val("AC")), ("PO", P.val("PO"))]

    P.barrier()
    if g["debug"]:
        ECUMD = nc.dram_tensor("ECUMD", [128, NCH * 32], F32, kind="ExternalOutput")
        ECCD = nc.dram_tensor("ECCD", [128, NCH * 32], F32, kind="ExternalOutput")
        P.dma("sync", "SW", ECUMD[:, :], ECUMR[:, :])
        P.dma("sync", "SW", ECCD[:, :], ECCR[:, :])
    esB.close()
    return dict(ECUMR=ECUMR, ECCR=ECCR)


def emit_phase_b1(nc, P, es, g, st):
    PB = g["PB"]
    QS, B0M, OS = g["QS"], g["B0M"], g["OS"]
    ECUMR, ECCR = st["ECUMR"], st["ECCR"]

    P.barrier()
    esB1 = contextlib.ExitStack()
    sb = lambda name, shape, dt: esB1.enter_context(nc.sbuf_tensor(name, shape, dt))

    S0ALL = sb("S0ALL", [128, NK * 256], F32)
    S0B = sb("S0B", [128, NK * 256], BF16)
    FD1 = sb("FD1", [128, 8], F32)
    RB = [sb(f"RB{i}", [128, NK * 1024], BF16) for i in range(2)]
    QF1 = [sb(f"QF1_{i}", [128, 2048], BF16) for i in range(2)]
    QSCT = sb("QSCT", [128, 256], F32)
    OT1 = sb("OT1", [128, 256], F32)
    STMP = sb("STMP", [128, 256], F32)
    DMP = sb("DMP", [128, 512], F32)
    ECU32 = sb("ECU32", [128, 32], F32)
    ECC32 = sb("ECC32", [128, 32], F32)
    OSTG = [sb(f"OSTG{i}", [128, NK * 256], BF16) for i in range(2)]

    fb = Banks(PB)
    qs_r = _r(QS, "(h p) t -> p h t", p=128)
    b0_r = _r(B0M, "(r p) m -> p r m", p=128)

    def colrep(tile, j, n):
        return bass.AP(tile, j, [[tile.shape[1], 128], [0, n]])

    ld_done = [None, None]
    ld_guard = [[], []]

    def load_chunk(c):
        buf = c % 2
        P.dma("sync", "LW", _r(RB[buf], "p (h m) -> p h m", m=1024),
              b0_r[:, c * NK:(c + 1) * NK, :], waits=ld_guard[buf])
        v2 = P.dma("sync", "LW", _r(QF1[buf], "p (h t) -> p h t", t=128),
                   qs_r[:, :, c * 128:(c + 1) * 128])
        ld_done[buf] = ("LW", v2)

    load_chunk(0)
    ost_guard = [[], []]
    sup_dv = 0
    for c in range(NCH):
        buf = c % 2
        if c + 1 < NCH:
            load_chunk(c + 1)
        if c % CHSEQ == 0:
            sup_dv = P.op("vector", lambda e: e.memset(S0ALL[:, :], 0.0),
                          waits=[("AC", P.val("AC"))])
        sbc = None
        sb_w = [("DV", sup_dv), ("PE", P.val("PE"))]
        for i in range(4):
            sbc = P.op("scalar", (lambda e, i=i: e.copy(
                S0B[:, i * 1024:(i + 1) * 1024], S0ALL[:, i * 1024:(i + 1) * 1024])),
                waits=sb_w if i == 0 else [],
                fence=(lambda e: e.copy(FD1[:, 0:1], S0B[:, 4095:4096]))
                if i == 3 else False)
        P.op("vector", (lambda e, c=c: e.tensor_copy(ECU32[:, :],
                                                     ECUMR[:, c * 32:(c + 1) * 32])),
             waits=[("DV", P.val("DV"))])
        P.op("vector", (lambda e, c=c: e.tensor_copy(ECC32[:, :],
                                                     ECCR[:, c * 32:(c + 1) * 32])))
        ovals = []
        if B1_MODE in ("oindpure", "oindmm", "oindq", "oinds"):
            for h in range(NK):
                rbase = h * 1024
                if B1_MODE in ("oindmm", "oindq", "oinds"):
                    slotQ, bwQ = fb.acquire()
                    pq = P.op("tensor", (lambda e, b=slotQ["t"], h=h, buf=buf:
                                         e.matmul(b[:, 0:256],
                                                  QF1[buf][:, h * 128:(h + 1) * 128],
                                                  S0B[:, h * 256:(h + 1) * 256],
                                                  start=True, stop=True)),
                              waits=[ld_done[buf], ("AC", sbc)] + list(bwQ))
                    slotO, bwO = fb.acquire()
                    slotS, bwS = fb.acquire()
                    po = ps_ = None
                    for p in range(2):
                        po = P.op("tensor", (lambda e, b=slotO["t"], h=h, p=p, buf=buf, rbase=rbase:
                                             e.matmul(b[:, p * 128:(p + 1) * 128],
                                                      RB[buf][:, rbase + p * 128:rbase + (p + 1) * 128],
                                                      S0B[:, h * 256 + p * 128:h * 256 + (p + 1) * 128],
                                                      start=(p == 0), stop=(p == 1))),
                                  waits=list(bwO) if p == 0 else [])
                    for p in range(2):
                        ps_ = P.op("tensor", (lambda e, b=slotS["t"], h=h, p=p, buf=buf, rbase=rbase:
                                              e.matmul(b[:, p * 128:(p + 1) * 128],
                                                       RB[buf][:, rbase + 256 + p * 128:rbase + 256 + (p + 1) * 128],
                                                       S0B[:, h * 256 + p * 128:h * 256 + (p + 1) * 128],
                                                       start=(p == 0), stop=(p == 1))),
                                   waits=list(bwS) if p == 0 else [])
                    if B1_MODE in ("oindq", "oinds"):
                        vq = None
                        for p in range(2):
                            vq = P.op("vector", (lambda e, b=slotQ["t"], p=p, h=h, c=c:
                                                 e.tensor_tensor(QSCT[:, p * 128:(p + 1) * 128],
                                                                 b[:, p * 128:(p + 1) * 128],
                                                                 colrep(ECU32, 2 * h + p, 128),
                                                                 ALU.mult)),
                                      waits=[("PE", pq)])
                        Banks.release(slotQ, "DV", vq)
                        v1 = P.op("vector", (lambda e, b=slotO["t"], h=h, buf=buf, rbase=rbase:
                                             e.tensor_tensor(OT1[:, :],
                                                             RB[buf][:, rbase + 512:rbase + 768],
                                                             b[:, 0:256], ALU.add)),
                                  waits=[("PE", po)])
                        Banks.release(slotO, "DV", v1)
                    else:
                        vq = P.op("vector", (lambda e, b=slotQ["t"]:
                                             e.tensor_copy(QSCT[:, :], b[:, 0:256])),
                                  waits=[("PE", pq)])
                        Banks.release(slotQ, "DV", vq)
                        v1 = P.op("vector", (lambda e, b=slotO["t"]:
                                             e.tensor_copy(OT1[:, :], b[:, 0:256])),
                                  waits=[("PE", po)])
                        Banks.release(slotO, "DV", v1)
                    if B1_MODE == "oinds":
                        vs1 = None
                        for p in range(2):
                            vs1 = P.op("vector", (lambda e, p=p, h=h, c=c:
                                                  e.tensor_tensor(STMP[:, p * 128:(p + 1) * 128],
                                                                  S0ALL[:, h * 256 + p * 128:h * 256 + (p + 1) * 128],
                                                                  colrep(ECC32, 2 * h + p, 128),
                                                                  ALU.mult)))
                        vs2 = P.op("vector", (lambda e, b=slotS["t"]:
                                              e.tensor_tensor(STMP[:, :], STMP[:, :],
                                                              b[:, 0:256], ALU.add)),
                                   waits=[("PE", ps_)])
                        Banks.release(slotS, "DV", vs2)
                        vs3 = P.op("vector", (lambda e, h=h, buf=buf, rbase=rbase:
                                              e.tensor_tensor(S0ALL[:, h * 256:(h + 1) * 256],
                                                              STMP[:, :],
                                                              RB[buf][:, rbase + 768:rbase + 1024],
                                                              ALU.add)))
                        sup_dv = vs3
                    else:
                        vs2 = P.op("vector", (lambda e, b=slotS["t"]:
                                              e.tensor_copy(STMP[:, :], b[:, 0:256])),
                                   waits=[("PE", ps_)])
                        Banks.release(slotS, "DV", vs2)
                v2 = P.op("vector", (lambda e, h=h, buf=buf, rbase=rbase:
                                     e.tensor_copy(OSTG[buf][:, h * 256:(h + 1) * 256],
                                                   RB[buf][:, rbase + 512:rbase + 768])),
                          waits=(ost_guard[buf] + [ld_done[buf]]) if h == 0 else [])
                ovals.append(("DV", v2))
            vfence = P.op("vector", (lambda e, buf=buf: e.tensor_copy(
                FD1[:, 1:2], OSTG[buf][:, 4095:4096])))
            stv = P.dma("sync", "SW", OS[c * 128:(c + 1) * 128, :], OSTG[buf][:, :],
                        waits=ovals + [("DV", vfence)])
            ost_guard[buf] = [("SW", stv)]
            ld_guard[buf] = [("PE", P.val("PE")), ("DV", P.val("DV"))]
            continue
        for h in range(NK):
            rbase = h * 1024
            slotQ, bwQ = fb.acquire()
            pq = P.op("tensor", (lambda e, b=slotQ["t"], h=h, buf=buf:
                                 e.matmul(b[:, 0:256], QF1[buf][:, h * 128:(h + 1) * 128],
                                          S0B[:, h * 256:(h + 1) * 256],
                                          start=True, stop=True)),
                      waits=[ld_done[buf], ("AC", sbc)] + list(bwQ))
            slotO, bwO = fb.acquire()
            slotS, bwS = fb.acquire()
            po = ps_ = None
            for p in range(2):
                po = P.op("tensor", (lambda e, b=slotO["t"], h=h, p=p, buf=buf, rbase=rbase:
                                     e.matmul(b[:, p * 128:(p + 1) * 128],
                                              RB[buf][:, rbase + p * 128:rbase + (p + 1) * 128],
                                              S0B[:, h * 256 + p * 128:h * 256 + (p + 1) * 128],
                                              start=(p == 0), stop=(p == 1))),
                          waits=list(bwO) if p == 0 else [])
            for p in range(2):
                ps_ = P.op("tensor", (lambda e, b=slotS["t"], h=h, p=p, buf=buf, rbase=rbase:
                                      e.matmul(b[:, p * 128:(p + 1) * 128],
                                               RB[buf][:, rbase + 256 + p * 128:rbase + 256 + (p + 1) * 128],
                                               S0B[:, h * 256 + p * 128:h * 256 + (p + 1) * 128],
                                               start=(p == 0), stop=(p == 1))),
                           waits=list(bwS) if p == 0 else [])
            vq = None
            for p in range(2):
                vq = P.op("vector", (lambda e, b=slotQ["t"], p=p, h=h, c=c:
                                     e.tensor_tensor(QSCT[:, p * 128:(p + 1) * 128],
                                                     b[:, p * 128:(p + 1) * 128],
                                                     colrep(ECU32, 2 * h + p, 128),
                                                     ALU.mult)),
                          waits=[("PE", pq)])
            Banks.release(slotQ, "DV", vq)
            v1 = P.op("vector", (lambda e, b=slotO["t"], h=h, buf=buf, rbase=rbase:
                                 e.tensor_tensor(OT1[:, :],
                                                 RB[buf][:, rbase + 512:rbase + 768],
                                                 b[:, 0:256], ALU.add)),
                      waits=[("PE", po)])
            Banks.release(slotO, "DV", v1)
            if B1_MODE == "oind":
                v2 = P.op("vector", (lambda e, h=h, buf=buf, rbase=rbase:
                                     e.tensor_copy(OSTG[buf][:, h * 256:(h + 1) * 256],
                                                   RB[buf][:, rbase + 512:rbase + 768])),
                          waits=(ost_guard[buf] + [ld_done[buf]]) if h == 0 else [])
            elif B1_MODE == "noq":
                v2 = P.op("vector", (lambda e, h=h, buf=buf:
                                     e.tensor_copy(OSTG[buf][:, h * 256:(h + 1) * 256],
                                                   OT1[:, :])),
                          waits=ost_guard[buf] if h == 0 else [])
            else:
                v2 = P.op("vector", (lambda e, h=h, buf=buf:
                                     e.tensor_tensor(OSTG[buf][:, h * 256:(h + 1) * 256],
                                                     OT1[:, :], QSCT[:, :], ALU.add)),
                          waits=ost_guard[buf] if h == 0 else [])
            ovals.append(("DV", v2))
            if g["debug"] and (c, h) == g.get("dbg_b1w", (None, None)):
                dt_ = g["dbg_tensor"]
                vdmp = P.op("vector", lambda e: e.tensor_copy(DMP[:, 0:256], QSCT[:, :]))
                vdmp = P.op("vector", lambda e: e.tensor_copy(DMP[:, 256:512], OT1[:, :]))
                P.dma("gpsimd", "SG", dt_[0:128, 0:512], DMP[:, :],
                      waits=[("DV", vdmp)])
                vdmp = P.op("vector", (lambda e, h=h: e.tensor_copy(
                    DMP[:, 0:256], S0B[:, h * 256:(h + 1) * 256])))
                vdmp = P.op("vector", (lambda e, buf=buf, rbase=rbase: e.tensor_copy(
                    DMP[:, 256:512], RB[buf][:, rbase + 512:rbase + 768])),
                    waits=[("SG", P.val("SG"))])
                P.dma("gpsimd", "SG", dt_[128:256, 0:512], DMP[:, :],
                      waits=[("DV", vdmp)])
            if B1_MODE == "fullq":
                vs2 = P.op("vector", (lambda e, b=slotS["t"]:
                                      e.tensor_copy(STMP[:, :], b[:, 0:256])),
                           waits=[("PE", ps_)])
                Banks.release(slotS, "DV", vs2)
                sup_dv = vs2
                continue
            vs1 = None
            for p in range(2):
                vs1 = P.op("vector", (lambda e, p=p, h=h, c=c:
                                      e.tensor_tensor(STMP[:, p * 128:(p + 1) * 128],
                                                      S0ALL[:, h * 256 + p * 128:h * 256 + (p + 1) * 128],
                                                      colrep(ECC32, 2 * h + p, 128),
                                                      ALU.mult)))
            vs2 = P.op("vector", (lambda e, b=slotS["t"]:
                                  e.tensor_tensor(STMP[:, :], STMP[:, :],
                                                  b[:, 0:256], ALU.add)),
                       waits=[("PE", ps_)])
            Banks.release(slotS, "DV", vs2)
            vs3 = P.op("vector", (lambda e, h=h, buf=buf, rbase=rbase:
                                  e.tensor_tensor(S0ALL[:, h * 256:(h + 1) * 256],
                                                  STMP[:, :],
                                                  RB[buf][:, rbase + 768:rbase + 1024],
                                                  ALU.add)))
            sup_dv = vs3
        vfence = P.op("vector", (lambda e, buf=buf: e.tensor_copy(
            FD1[:, 1:2], OSTG[buf][:, 4095:4096])))
        stv = P.dma("sync", "SW", OS[c * 128:(c + 1) * 128, :], OSTG[buf][:, :],
                    waits=ovals + [("DV", vfence)])
        ost_guard[buf] = [("SW", stv)]
        ld_guard[buf] = [("PE", P.val("PE")), ("DV", P.val("DV"))]

    P.barrier()
    esB1.close()


def emit_precast(nc, P, es, g):
    woutt, wgut, wdot = g["woutt"], g["wgut"], g["wdot"]
    woutb, wgutb, wdotb = g["woutb"], g["wgutb"], g["wdotb"]
    PCS = [es.enter_context(nc.sbuf_tensor(f"PCS{i}", [128, 4096], BF16))
           for i in range(2)]
    pieces = []
    for rc in range(16):
        for fc in range(4):
            pieces.append((wgut, wgutb, rc * 128, fc * 4096, 4096))
    for rc in range(64):
        pieces.append((wdot, wdotb, rc * 128, 0, 2048))
    for rc in range(32):
        pieces.append((woutt, woutb, rc * 128, 0, 2048))
    st_done = [0, 0]
    for i, (src, dst, r0, c0, wid) in enumerate(pieces):
        buf = i % 2
        ld = P.dma("gpsimd", "PL", PCS[buf][:, 0:wid],
                   src[r0:r0 + 128, c0:c0 + wid],
                   waits=[("SC", st_done[buf])] if st_done[buf] else ())
        st_done[buf] = P.dma("gpsimd", "SC", dst[r0:r0 + 128, c0:c0 + wid],
                             PCS[buf][:, 0:wid], waits=[("PL", ld)])
    P.pc_done = P.val("SC")


def emit_phase_c(nc, P, es, g):
    PB = g["PB"]
    OS, ZS, H1F, H1B = g["OS"], g["ZS"], g["H1F"], g["H1B"]
    xt, woutb = g["xt"], g["woutb"]
    s_onesc, s_ones1, s_idf, s_vec = (g["s_onesc"], g["s_ones1"], g["s_idf"],
                                      g["s_vec"])

    P.barrier()
    esC = contextlib.ExitStack()
    sb = lambda name, shape, dt: esC.enter_context(nc.sbuf_tensor(name, shape, dt))

    TT = 512
    OTB = sb("OTB", [128, NV * TT], BF16)
    ZTB = sb("ZTB", [128, NV * TT], BF16)
    WOC = [sb(f"WOC{i}", [128, NV * 128], BF16) for i in range(2)]
    HC = sb("HC", [128, DCH * TT], F32)
    HBC = sb("HBC", [128, DCH * TT], BF16)
    SQC = sb("SQC", [128, TT], F32)
    RR = sb("RR", [1, TT], F32)
    LR = sb("LR", [1, 4 * TT], F32)
    NWC = sb("NWC", [128, 1], F32)
    FCC = sb("FCC", [128, 4], F32)
    DMPC = sb("DMPC", [128, 3 * TT], F32)

    fb = Banks(PB)

    def colrep(tile, j, n):
        return bass.AP(tile, j, [[tile.shape[1], 128], [0, n]])

    slot, bw = fb.acquire()
    pe = P.op("tensor", (lambda e, b=slot["t"]:
                         e.transpose(b[:, 0:1], s_vec[0:1, 64:192],
                                     s_idf[0:1, 0:1])),
              waits=list(bw))
    nwv = P.op("vector", (lambda e, b=slot["t"]: e.tensor_copy(NWC[:, :], b[:, 0:1])),
               waits=[("PE", pe)])
    Banks.release(slot, "DV", nwv)

    Gd = {}
    for tt in range(TOK // TT):
        t0 = tt * TT
        otld = None
        w0 = Gd.pop("OTZT", [])
        for v in range(NV):
            P.dma("sync", "LW", OTB[:, v * TT:(v + 1) * TT],
                  OS[t0:t0 + TT, v * 128:(v + 1) * 128], transpose=True,
                  waits=w0 if v == 0 else ())
            otld = P.dma("sync", "LW", ZTB[:, v * TT:(v + 1) * TT],
                         ZS[t0:t0 + TT, v * 128:(v + 1) * 128], transpose=True)
        ld_done = ("LW", otld)
        gvals = []
        sq_guard = Gd.pop("SQC", [])
        rr_guard = Gd.pop("RR", [])
        for v in range(NV):
            sq = P.op("vector", (lambda e, v=v: e.tensor_tensor(
                SQC[:, :], OTB[:, v * TT:(v + 1) * TT],
                OTB[:, v * TT:(v + 1) * TT], ALU.mult)),
                waits=[ld_done] + (sq_guard if v == 0 else []),
                fence=lambda e: e.tensor_copy(FCC[:, 0:1], SQC[:, TT - 1:TT]))
            slot, bw = fb.acquire()
            pe = P.op("tensor", (lambda e, b=slot["t"]:
                                 e.matmul(b[0:1, 0:TT], s_onesc[:, :], SQC[:, :],
                                          start=True, stop=True)),
                      waits=[("DV", sq)] + list(bw))
            rp = P.op("vector", (lambda e, b=slot["t"]:
                                 e.tensor_scalar(RR[0:1, :], b[0:1, 0:TT],
                                                 1.0 / 128.0, EPS_RMS,
                                                 ALU.mult, ALU.add)),
                      waits=[("PE", pe)] + (rr_guard if v == 0 else []),
                      fence=lambda e: e.tensor_copy(FCC[0:1, 0:1],
                                                    RR[0:1, TT - 1:TT]))
            Banks.release(slot, "DV", rp)
            ac = P.op("scalar", lambda e: e.activation(RR[0:1, :], RR[0:1, :],
                                                       AF.Sqrt),
                      waits=[("DV", rp)],
                      fence=lambda e: e.copy(FCC[0:1, 1:2], RR[0:1, TT - 1:TT]))
            rv = P.op("vector", lambda e: e.reciprocal(RR[0:1, :], RR[0:1, :]),
                      waits=[("AC", ac)],
                      fence=lambda e: e.tensor_copy(FCC[0:1, 0:1], RR[0:1, TT - 1:TT]))
            bslot, bbw = fb.acquire()
            pe2 = P.op("tensor", (lambda e, b=bslot["t"]:
                                  e.matmul(b[:, 0:TT], s_ones1[:, :], RR[0:1, :],
                                           start=True, stop=True)),
                       waits=[("DV", rv)] + list(bbw))
            if g["debug"] and tt == 0 and v == NV - 1:
                P.op("vector", (lambda e, b=bslot["t"]: e.tensor_copy(
                    DMPC[:, 0:TT], b[:, 0:TT])), waits=[("PE", pe2)])
            g1 = P.op("vector", (lambda e, b=bslot["t"], v=v: e.tensor_tensor(
                OTB[:, v * TT:(v + 1) * TT], OTB[:, v * TT:(v + 1) * TT],
                b[:, 0:TT], ALU.mult)), waits=[("PE", pe2)])
            Banks.release(bslot, "DV", g1)
            if g["debug"] and tt == 0 and v == NV - 1:
                P.op("vector", (lambda e, v=v: e.tensor_copy(
                    DMPC[:, TT:2 * TT], OTB[:, v * TT:(v + 1) * TT])))
            g2 = P.op("vector", (lambda e, v=v: e.tensor_tensor(
                OTB[:, v * TT:(v + 1) * TT], OTB[:, v * TT:(v + 1) * TT],
                colrep(NWC, 0, TT), ALU.mult)))
            if g["debug"] and tt == 0 and v == NV - 1:
                P.op("vector", (lambda e, v=v: e.tensor_copy(
                    DMPC[:, 2 * TT:3 * TT], OTB[:, v * TT:(v + 1) * TT])))
            g3 = P.op("vector", (lambda e, v=v: e.tensor_tensor(
                OTB[:, v * TT:(v + 1) * TT], OTB[:, v * TT:(v + 1) * TT],
                ZTB[:, v * TT:(v + 1) * TT], ALU.mult)),
                fence=(lambda e, v=v: e.tensor_copy(
                    FCC[:, 1:2], OTB[:, (v + 1) * TT - 1:(v + 1) * TT])))
            gvals.append(("DV", g3))
        dbgw = []
        if g["debug"] and tt == 0 and g.get("dbg_b1") is not None:
            P.dma("gpsimd", "SG", g["dbg_b1"][0:128, :], OTB[:, 0:4096],
                  waits=gvals)
            P.dma("gpsimd", "SG", g["dbg_b1"][128:129, 0:512], RR[0:1, :],
                  waits=gvals)
            P.dma("gpsimd", "SG", g["dbg_b1"][129:130, 0:512],
                  DMPC[0:1, 0:TT])
            P.dma("gpsimd", "SG", g["dbg_b1"][130:131, 0:512],
                  DMPC[0:1, TT:2 * TT])
            P.dma("gpsimd", "SG", g["dbg_b1"][131:132, 0:512],
                  OTB[0:1, 31 * TT:32 * TT])
            P.dma("gpsimd", "SG", g["dbg_b1"][132:133, 0:512],
                  DMPC[0:1, 2 * TT:3 * TT])
            dbgw = [("SG", P.val("SG"))]
        Gd["SQC"] = [("PE", P.val("PE"))]
        Gd["RR"] = [("PE", P.val("PE")), ("DV", P.val("DV"))] + dbgw
        wo_done = [0, 0]
        hvals = []
        hc_guard = Gd.pop("HC", [])
        for c in range(DCH):
            wb = c % 2
            wo_w = ([("SC", getattr(P, "pc_done", 0))]
                    if (tt == 0 and c == 0) else [])
            wo_done[wb] = P.dma("sync", "LW",
                                _r(WOC[wb], "p (v m) -> p v m", m=128),
                                _r(woutb, "(v p) m -> p v m", p=128)
                                [:, :, c * 128:(c + 1) * 128],
                                waits=wo_w + ([("PE", wo_done[wb])]
                                              if wo_done[wb] else []))
            xl = P.dma("gpsimd", "LG", HC[:, c * TT:(c + 1) * TT],
                       xt[c * 128:(c + 1) * 128, t0:t0 + TT],
                       waits=hc_guard if c < 2 else ())
            slot, bw = fb.acquire()
            pe = None
            for v in range(NV):
                w = list(bw) + (gvals + [("LW", wo_done[wb])] if v == 0 else [])
                pe = P.op("tensor", (lambda e, b=slot["t"], v=v, wb=wb:
                                     e.matmul(b[:, 0:TT],
                                              WOC[wb][:, v * 128:(v + 1) * 128],
                                              OTB[:, v * TT:(v + 1) * TT],
                                              start=(v == 0), stop=(v == NV - 1))),
                          waits=w)
                bw = ()
            wo_done[wb] = pe
            hv = P.op("vector", (lambda e, b=slot["t"], c=c: e.tensor_tensor(
                HC[:, c * TT:(c + 1) * TT], HC[:, c * TT:(c + 1) * TT],
                b[:, 0:TT], ALU.add)), waits=[("PE", pe), ("LG", xl)],
                fence=(lambda e, c=c: e.tensor_copy(
                    FCC[:, 3:4], HC[:, (c + 1) * TT - 1:(c + 1) * TT]))
                if c == DCH - 1 else False)
            Banks.release(slot, "DV", hv)
            hvals.append(("DV", hv))
        Gd["OTZT"] = [("PE", P.val("PE")), ("DV", P.val("DV"))] + dbgw
        sslot, sbw = fb.acquire()
        pe = None
        for c in range(DCH):
            pe = P.op("tensor", (lambda e, b=sslot["t"], c=c:
                                 e.matmul(b[0:1, 0:TT], s_onesc[:, :],
                                          HC[:, c * TT:(c + 1) * TT],
                                          start=(c == 0), stop=(c == DCH - 1))),
                      waits=(hvals + list(sbw)) if c == 0 else [])
        sum_pe = pe
        qslot, qbw = fb.acquire()
        sq_pe = None
        sqv = None
        for c in range(DCH):
            sqv = P.op("vector", (lambda e, c=c: e.tensor_tensor(
                SQC[:, :], HC[:, c * TT:(c + 1) * TT],
                HC[:, c * TT:(c + 1) * TT], ALU.mult)),
                waits=([("PE", sum_pe)] if c == 0 else []) +
                      ([("PE", sq_pe)] if sq_pe else []),
                fence=lambda e: e.tensor_copy(FCC[:, 0:1], SQC[:, TT - 1:TT]))
            sq_pe = P.op("tensor", (lambda e, b=qslot["t"], c=c:
                                    e.matmul(b[0:1, 0:TT], s_onesc[:, :], SQC[:, :],
                                             start=(c == 0), stop=(c == DCH - 1))),
                         waits=[("DV", sqv)] + (list(qbw) if c == 0 else []))
        Gd["SQC"].append(("PE", sq_pe))
        r1 = P.op("vector", (lambda e, b=sslot["t"]: e.tensor_scalar_mul(
            LR[0:1, 0:TT], b[0:1, 0:TT], 1.0 / D)), waits=[("PE", sum_pe)])
        Banks.release(sslot, "DV", r1)
        r2 = P.op("vector", (lambda e, b=qslot["t"]: e.tensor_scalar(
            LR[0:1, TT:2 * TT], b[0:1, 0:TT], 1.0 / D, EPS_LN,
            ALU.mult, ALU.add)), waits=[("PE", sq_pe)])
        Banks.release(qslot, "DV", r2)
        r3 = P.op("vector", lambda e: e.tensor_tensor(
            LR[0:1, 2 * TT:3 * TT], LR[0:1, 0:TT], LR[0:1, 0:TT], ALU.mult))
        r4 = P.op("vector", lambda e: e.tensor_tensor(
            LR[0:1, TT:2 * TT], LR[0:1, TT:2 * TT], LR[0:1, 2 * TT:3 * TT],
            ALU.subtract),
            fence=lambda e: e.tensor_copy(FCC[0:1, 0:1], LR[0:1, 2 * TT - 1:2 * TT]))
        r5 = P.op("scalar", lambda e: e.activation(
            LR[0:1, TT:2 * TT], LR[0:1, TT:2 * TT], AF.Sqrt),
            waits=[("DV", r4)],
            fence=lambda e: e.copy(FCC[0:1, 1:2], LR[0:1, 2 * TT - 1:2 * TT]))
        r6 = P.op("vector", lambda e: e.reciprocal(
            LR[0:1, TT:2 * TT], LR[0:1, TT:2 * TT]), waits=[("AC", r5)],
            fence=lambda e: e.tensor_copy(FCC[0:1, 0:1], LR[0:1, 2 * TT - 1:2 * TT]))
        mslot, mbw = fb.acquire()
        mpe = P.op("tensor", (lambda e, b=mslot["t"]:
                              e.matmul(b[:, 0:TT], s_ones1[:, :], LR[0:1, 0:TT],
                                       start=True, stop=True)),
                   waits=[("DV", r6)] + list(mbw))
        rslot, rbw = fb.acquire()
        rpe = P.op("tensor", (lambda e, b=rslot["t"]:
                              e.matmul(b[:, 0:TT], s_ones1[:, :],
                                       LR[0:1, TT:2 * TT], start=True, stop=True)),
                   waits=list(rbw))
        h1v = None
        hb_guard = Gd.pop("HBC", [])
        for c in range(DCH):
            n1 = P.op("vector", (lambda e, b=mslot["t"], c=c: e.tensor_tensor(
                HC[:, c * TT:(c + 1) * TT], HC[:, c * TT:(c + 1) * TT],
                b[:, 0:TT], ALU.subtract)), waits=[("PE", mpe), ("PE", rpe)]
                if c == 0 else [])
            n2 = P.op("vector", (lambda e, b=rslot["t"], c=c: e.tensor_tensor(
                HC[:, c * TT:(c + 1) * TT], HC[:, c * TT:(c + 1) * TT],
                b[:, 0:TT], ALU.mult)),
                fence=(lambda e, c=c: e.tensor_copy(
                    FCC[:, 1:2], HC[:, (c + 1) * TT - 1:(c + 1) * TT])))
            hb = P.op("scalar", (lambda e, c=c: e.copy(
                HBC[:, c * TT:(c + 1) * TT], HC[:, c * TT:(c + 1) * TT])),
                waits=[("DV", n2)] + (hb_guard if c == 0 else []),
                fence=(lambda e, c=c: e.copy(
                    FCC[:, 2:3], HBC[:, (c + 1) * TT - 1:(c + 1) * TT])))
            P.dma("sync", "SW", H1F[c * 128:(c + 1) * 128, t0:t0 + TT],
                  HC[:, c * TT:(c + 1) * TT], waits=[("DV", n2)])
            h1v = P.dma("scalar", "SS", H1B[c * 128:(c + 1) * 128, t0:t0 + TT],
                        HBC[:, c * TT:(c + 1) * TT], waits=[("AC", hb)])
        Banks.release(mslot, "DV", P.val("DV"))
        Banks.release(rslot, "DV", P.val("DV"))
        Gd["HC"] = [("SW", P.val("SW")), ("AC", P.val("AC"))]
        Gd["HBC"] = [("SS", P.val("SS"))]
    P.barrier()
    esC.close()


def emit_phase_d(nc, P, es, g):
    PB = g["PB"]
    H1F, H1B, FFS, out = g["H1F"], g["H1B"], g["FFS"], g["out"]
    wgutb, wdotb = g["wgutb"], g["wdotb"]
    s_onesc, s_ones1, s_idf = g["s_onesc"], g["s_ones1"], g["s_idf"]

    P.barrier()
    esD = contextlib.ExitStack()
    sb = lambda name, shape, dt: esD.enter_context(nc.sbuf_tensor(name, shape, dt))
    MT = 1024
    H1S = sb("H1S", [128, DCH * MT], BF16)
    WG = [sb(f"WG{i}", [128, DCH * 256], BF16) for i in range(2)]
    FFO = [sb(f"FFO{i}", [128, MT], BF16) for i in range(2)]
    FCD = sb("FCD", [128, 4], F32)
    fb = Banks(PB)
    NJ = INTER // 128

    wg_r = _r(wgutb, "(k p) f -> p k f", p=128)
    h1_r = _r(H1B, "(k p) t -> p k t", p=128)
    wg_done = [0, 0]
    ffo_guard = [[], []]
    for mt in range(TOK // MT):
        t0 = mt * MT
        h1ld = P.dma("sync", "LW", _r(H1S, "p (k t) -> p k t", t=MT),
                     h1_r[:, :, t0:t0 + MT],
                     waits=[("PE", P.val("PE"))] if mt else ())
        for j in range(NJ):
            wb = j % 2
            wg_w = ([("SC", getattr(P, "pc_done", 0))]
                    if (mt == 0 and j == 0) else [])
            wld = P.dma("sync", "LW",
                        _r(WG[wb], "p (k f) -> p k f", f=256)[:, :, 0:128],
                        wg_r[:, :, j * 128:(j + 1) * 128],
                        waits=wg_w + ([("PE", wg_done[wb])]
                                      if wg_done[wb] else []))
            wld = P.dma("sync", "LW",
                        _r(WG[wb], "p (k f) -> p k f", f=256)[:, :, 128:256],
                        wg_r[:, :, INTER + j * 128:INTER + (j + 1) * 128])
            gsl = [fb.acquire() for _ in range(2)]
            usl = [fb.acquire() for _ in range(2)]
            upe = None
            for half in range(2):
                slot, bw = gsl[half]
                for kk in range(DCH):
                    w = list(bw) + ([("LW", max(wld, h1ld))] if kk == 0 else [])
                    P.op("tensor", (lambda e, b=slot["t"], kk=kk, half=half, wb=wb:
                                    e.matmul(b[:, 0:512],
                                             WG[wb][:, kk * 256:kk * 256 + 128],
                                             H1S[:, kk * MT + half * 512:
                                                 kk * MT + (half + 1) * 512],
                                             start=(kk == 0),
                                             stop=(kk == DCH - 1))),
                         waits=w)
                    bw = ()
            for half in range(2):
                slot, bw = usl[half]
                for kk in range(DCH):
                    w = list(bw)
                    upe = P.op("tensor", (lambda e, b=slot["t"], kk=kk, half=half,
                                          wb=wb:
                                          e.matmul(b[:, 0:512],
                                                   WG[wb][:, kk * 256 + 128:
                                                          (kk + 1) * 256],
                                                   H1S[:, kk * MT + half * 512:
                                                       kk * MT + (half + 1) * 512],
                                                   start=(kk == 0),
                                                   stop=(kk == DCH - 1))),
                               waits=w)
                    bw = ()
            wg_done[wb] = upe
            fo = j % 2
            dv = None
            for half in range(2):
                ac = P.op("scalar", (lambda e, b=gsl[half][0]["t"], fo=fo,
                                     half=half:
                                     e.activation(FFO[fo][:, half * 512:
                                                          (half + 1) * 512],
                                                  b[:, 0:512], AF.Silu)),
                          waits=[("PE", upe)] + (ffo_guard[fo] if half == 0
                                                 else []),
                          fence=(lambda e, fo=fo, half=half: e.copy(
                              FCD[:, 2:3],
                              FFO[fo][:, (half + 1) * 512 - 1:(half + 1) * 512])))
                Banks.release(gsl[half][0], "AC", ac)
                dv = P.op("vector", (lambda e, b=usl[half][0]["t"], fo=fo,
                                     half=half:
                                     e.tensor_tensor(FFO[fo][:, half * 512:
                                                             (half + 1) * 512],
                                                     FFO[fo][:, half * 512:
                                                             (half + 1) * 512],
                                                     b[:, 0:512], ALU.mult)),
                          waits=[("AC", ac)],
                          fence=(lambda e, fo=fo, half=half: e.tensor_copy(
                              FCD[:, 0:1],
                              FFO[fo][:, (half + 1) * 512 - 1:(half + 1) * 512])))
                Banks.release(usl[half][0], "DV", dv)
            stv = P.dma("sync", "SW", FFS[j * 128:(j + 1) * 128, t0:t0 + MT],
                        FFO[fo][:, :], waits=[("DV", dv)])
            ffo_guard[fo] = [("SW", stv)]
    P.barrier()
    esD.close()

    esD2 = contextlib.ExitStack()
    sb = lambda name, shape, dt: esD2.enter_context(nc.sbuf_tensor(name, shape, dt))
    TT = 512
    NI = INTER // 128
    FFSB = sb("FFSB", [128, NI * TT], BF16)
    WDC = [sb(f"WDC{i}", [128, NI * 128], BF16) for i in range(2)]
    HD = sb("HD", [128, DCH * TT], F32)
    SQD = sb("SQD", [128, TT], F32)
    LRD = sb("LRD", [1, 3 * TT], F32)
    OTS = sb("OTS", [128, TT], F32)
    FCD2 = sb("FCD2", [128, 4], F32)
    fb = Banks(PB)
    ffs_r = _r(FFS, "(i p) t -> p i t", p=128)
    wd_r = _r(wdotb, "(i p) m -> p i m", p=128)
    wd_done = [0, 0]
    Gd = {}
    ovals = []
    for tt in range(TOK // TT):
        t0 = tt * TT
        fld = P.dma("sync", "LW", _r(FFSB, "p (i t) -> p i t", t=TT),
                    ffs_r[:, :, t0:t0 + TT],
                    waits=Gd.pop("FFSB", []))
        hvals = []
        hd_guard = Gd.pop("HD", [])
        for c in range(DCH):
            wb = c % 2
            wd_done[wb] = P.dma("sync", "LW",
                                _r(WDC[wb], "p (i m) -> p i m", m=128),
                                wd_r[:, :, c * 128:(c + 1) * 128],
                                waits=[("PE", wd_done[wb])] if wd_done[wb] else ())
            xl = P.dma("gpsimd", "LG", HD[:, c * TT:(c + 1) * TT],
                       H1F[c * 128:(c + 1) * 128, t0:t0 + TT],
                       waits=hd_guard if c < 2 else ())
            slot, bw = fb.acquire()
            pe = None
            for i in range(NI):
                w = (list(bw) + [("LW", max(wd_done[wb], fld))]) if i == 0 else []
                pe = P.op("tensor", (lambda e, b=slot["t"], i=i, wb=wb:
                                     e.matmul(b[:, 0:TT],
                                              WDC[wb][:, i * 128:(i + 1) * 128],
                                              FFSB[:, i * TT:(i + 1) * TT],
                                              start=(i == 0), stop=(i == NI - 1))),
                          waits=w)
                bw = ()
            wd_done[wb] = pe
            hv = P.op("vector", (lambda e, b=slot["t"], c=c: e.tensor_tensor(
                HD[:, c * TT:(c + 1) * TT], HD[:, c * TT:(c + 1) * TT],
                b[:, 0:TT], ALU.add)), waits=[("PE", pe), ("LG", xl)],
                fence=(lambda e, c=c: e.tensor_copy(
                    FCD2[:, 3:4], HD[:, (c + 1) * TT - 1:(c + 1) * TT]))
                if c == DCH - 1 else False)
            Banks.release(slot, "DV", hv)
            hvals.append(("DV", hv))
        Gd["FFSB"] = [("PE", P.val("PE"))]
        sslot, sbw = fb.acquire()
        pe = None
        for c in range(DCH):
            pe = P.op("tensor", (lambda e, b=sslot["t"], c=c:
                                 e.matmul(b[0:1, 0:TT], s_onesc[:, :],
                                          HD[:, c * TT:(c + 1) * TT],
                                          start=(c == 0), stop=(c == DCH - 1))),
                      waits=(hvals + list(sbw)) if c == 0 else [])
        sum_pe = pe
        qslot, qbw = fb.acquire()
        sq_pe = None
        for c in range(DCH):
            sqv = P.op("vector", (lambda e, c=c: e.tensor_tensor(
                SQD[:, :], HD[:, c * TT:(c + 1) * TT],
                HD[:, c * TT:(c + 1) * TT], ALU.mult)),
                waits=([("PE", sum_pe)] if c == 0 else []) +
                      ([("PE", sq_pe)] if sq_pe else []),
                fence=lambda e: e.tensor_copy(FCD2[:, 0:1], SQD[:, TT - 1:TT]))
            sq_pe = P.op("tensor", (lambda e, b=qslot["t"], c=c:
                                    e.matmul(b[0:1, 0:TT], s_onesc[:, :], SQD[:, :],
                                             start=(c == 0), stop=(c == DCH - 1))),
                         waits=[("DV", sqv)] + (list(qbw) if c == 0 else []))
        r1 = P.op("vector", (lambda e, b=sslot["t"]: e.tensor_scalar_mul(
            LRD[0:1, 0:TT], b[0:1, 0:TT], 1.0 / D)), waits=[("PE", sum_pe)])
        Banks.release(sslot, "DV", r1)
        r2 = P.op("vector", (lambda e, b=qslot["t"]: e.tensor_scalar(
            LRD[0:1, TT:2 * TT], b[0:1, 0:TT], 1.0 / D, EPS_LN,
            ALU.mult, ALU.add)), waits=[("PE", sq_pe)])
        Banks.release(qslot, "DV", r2)
        r3 = P.op("vector", lambda e: e.tensor_tensor(
            LRD[0:1, 2 * TT:3 * TT], LRD[0:1, 0:TT], LRD[0:1, 0:TT], ALU.mult))
        r4 = P.op("vector", lambda e: e.tensor_tensor(
            LRD[0:1, TT:2 * TT], LRD[0:1, TT:2 * TT], LRD[0:1, 2 * TT:3 * TT],
            ALU.subtract),
            fence=lambda e: e.tensor_copy(FCD2[0:1, 0:1],
                                          LRD[0:1, 2 * TT - 1:2 * TT]))
        r5 = P.op("scalar", lambda e: e.activation(
            LRD[0:1, TT:2 * TT], LRD[0:1, TT:2 * TT], AF.Sqrt),
            waits=[("DV", r4)],
            fence=lambda e: e.copy(FCD2[0:1, 1:2], LRD[0:1, 2 * TT - 1:2 * TT]))
        r6 = P.op("vector", lambda e: e.reciprocal(
            LRD[0:1, TT:2 * TT], LRD[0:1, TT:2 * TT]), waits=[("AC", r5)],
            fence=lambda e: e.tensor_copy(FCD2[0:1, 0:1],
                                          LRD[0:1, 2 * TT - 1:2 * TT]))
        mslot, mbw = fb.acquire()
        mpe = P.op("tensor", (lambda e, b=mslot["t"]:
                              e.matmul(b[:, 0:TT], s_ones1[:, :], LRD[0:1, 0:TT],
                                       start=True, stop=True)),
                   waits=[("DV", r6)] + list(mbw))
        rslot, rbw = fb.acquire()
        rpe = P.op("tensor", (lambda e, b=rslot["t"]:
                              e.matmul(b[:, 0:TT], s_ones1[:, :],
                                       LRD[0:1, TT:2 * TT], start=True, stop=True)),
                   waits=list(rbw))
        n2v = []
        for c in range(DCH):
            n1 = P.op("vector", (lambda e, b=mslot["t"], c=c: e.tensor_tensor(
                HD[:, c * TT:(c + 1) * TT], HD[:, c * TT:(c + 1) * TT],
                b[:, 0:TT], ALU.subtract)), waits=[("PE", mpe), ("PE", rpe)]
                if c == 0 else [])
            n2 = P.op("vector", (lambda e, b=rslot["t"], c=c: e.tensor_tensor(
                HD[:, c * TT:(c + 1) * TT], HD[:, c * TT:(c + 1) * TT],
                b[:, 0:TT], ALU.mult)),
                fence=(lambda e, c=c: e.tensor_copy(
                    FCD2[:, 1:2], HD[:, (c + 1) * TT - 1:(c + 1) * TT])))
            n2v.append(("DV", n2))
        Banks.release(mslot, "DV", P.val("DV"))
        Banks.release(rslot, "DV", P.val("DV"))
        for c in range(DCH):
            tslot, tbw = fb.acquire()
            tpe = None
            for tq in range(TT // 128):
                tpe = P.op("tensor", (lambda e, b=tslot["t"], c=c, tq=tq:
                                      e.transpose(b[:, tq * 128:(tq + 1) * 128],
                                                  HD[:, c * TT + tq * 128:
                                                     c * TT + (tq + 1) * 128],
                                                  s_idf[:, :])),
                           waits=([n2v[c]] + list(tbw) + ovals[-1:])
                           if tq == 0 else [])
            ev = P.op("vector", (lambda e, b=tslot["t"]:
                                 e.tensor_copy(OTS[:, :], b[:, 0:TT])),
                      waits=[("PE", tpe)],
                      fence=lambda e: e.tensor_copy(FCD2[:, 2:3], OTS[:, TT - 1:TT]))
            Banks.release(tslot, "DV", ev)
            sv = None
            for tq in range(TT // 128):
                sv = P.dma("sync", "SW",
                           out[t0 + tq * 128:t0 + (tq + 1) * 128,
                               c * 128:(c + 1) * 128],
                           OTS[:, tq * 128:(tq + 1) * 128],
                           waits=[("DV", ev)] if tq == 0 else ())
            ovals = [("SW", sv)]
        Gd["HD"] = [("PE", P.val("PE")), ("DV", P.val("DV")), ("SW", P.val("SW"))]
    P.barrier()
    esD2.close()


def _prep_inputs(ins):
    w = {}
    w["wqkvt"] = np.ascontiguousarray(ins["L0_attn_W_qkv"].T)
    w["wzt"] = np.ascontiguousarray(ins["L0_attn_W_z"].T)
    w["wbat"] = np.ascontiguousarray(
        np.concatenate([ins["L0_attn_W_b"], ins["L0_attn_W_a"]], axis=0).T)
    w["convw"] = np.ascontiguousarray(ins["L0_attn_conv_w"])
    vec = np.zeros((1, 192), dtype=np.float32)
    vec[0, 0:32] = ins["L0_attn_dt_bias"]
    vec[0, 32:64] = ins["L0_attn_A_log"]
    vec[0, 64:192] = ins["L0_attn_norm_w"]
    w["vecs"] = vec
    w["woutt"] = np.ascontiguousarray(ins["L0_attn_W_out"].T)
    w["wgut"] = np.ascontiguousarray(ins["L0_ffn_w_gate_up"].T)
    w["wdot"] = np.ascontiguousarray(ins["L0_ffn_w_down"].T)
    return w


def _core_inputs(x, c, w):
    xs = np.asarray(x[2 * c:2 * c + 2], dtype=np.float32)
    m = dict(w)
    m["xt"] = np.ascontiguousarray(xs.transpose(2, 0, 1).reshape(D, TOK))
    m["xr"] = np.ascontiguousarray(xs.reshape(TOK, D))
    return m


def kernel(**inputs):
    x = np.asarray(inputs["L0_attn_x"], dtype=np.float32)
    w = _prep_inputs(inputs)
    nc = build_program()
    in_maps = [_core_inputs(x, c, w) for c in range(8)]
    res = run_bass_kernel_spmd(nc, in_maps, core_ids=list(range(8)))
    outs = [np.asarray(r["out"], dtype=np.float32).reshape(2, TSEQ, D)
            for r in res.results]
    return np.concatenate(outs, axis=0)


# version 08
# speedup vs baseline: 1.1176x; 1.1176x over previous
import numpy as np
import ml_dtypes
import contextlib
import concourse.bass as bass
import concourse.mybir as mybir
from concourse.bass_utils import run_bass_kernel_spmd

F32 = mybir.dt.float32
BF16 = mybir.dt.bfloat16
AF = mybir.ActivationFunctionType
ALU = mybir.AluOpType
BF = ml_dtypes.bfloat16

D = 2048
DCH = 16
TOK = 4096
TSEQ = 2048
NSEQ = 2
NK, DK = 16, 128
NV, DV = 32, 128
QD, VD = NK * DK, NV * DV
QKVF = 2 * QD + VD
KC = 4
CH = 128
NCH = TOK // CH
CHSEQ = TSEQ // CH
INTER = 8192
EPS_LN = 1e-5
EPS_RMS = 1e-6
EPS_NORM = 1e-12

ENGINES = ("sync", "scalar", "vector", "tensor", "gpsimd")


class Prog:

    def __init__(self):
        self.steps = []
        self.cnt = {}
        self.dma_sems = set()

    def val(self, sem):
        return self.cnt.get(sem, 0)

    def emit(self, engine, fn, waits=(), incs=()):
        for s, amt in incs:
            if amt == 16:
                self.dma_sems.add(s)
            self.cnt[s] = self.cnt.get(s, 0) + amt
        fixed = []
        for w in waits:
            if not w:
                continue
            s, v = w
            if v > 0 and s in self.dma_sems:
                v = self.cnt.get(s, v)
            fixed.append((s, v))
        self.steps.append((engine, fn, tuple(fixed), tuple(incs)))
        return tuple(self.cnt[s] for s, _ in incs) if incs else None

    def dma(self, engine, sem, out_ap, in_ap, waits=(), **kw):
        fn = (lambda e, o=out_ap, i=in_ap, kw=dict(kw):
              e.dma_start(out=o, in_=i, **kw))
        self.emit(engine, fn, waits, ((sem, 16),))
        return self.cnt[sem]

    def op(self, engine, fn, waits=(), sem=None, fence=False):
        sem = sem or {"tensor": "PE", "vector": "DV", "scalar": "AC",
                      "gpsimd": "PO"}[engine]
        self.emit(engine, fn, waits, ((sem, 1),))
        if fence:
            self.emit(engine, fence, (), ((sem, 1),))
        return self.cnt[sem]

    def barrier(self, exclude=("PL", "SC")):
        vals = [(s, v) for s, v in self.cnt.items() if v > 0 and s not in exclude]
        for en in ENGINES:
            self.emit(en, lambda e: e.nop(), vals, ())


class Banks:

    def __init__(self, slots):
        self.slots = [{"t": t, "w": None} for t in slots]
        self.i = 0

    def acquire(self):
        s = self.slots[self.i % len(self.slots)]
        self.i += 1
        w = s["w"]
        s["w"] = None
        return s, (w,) if w else ()

    @staticmethod
    def release(slot, sem, val):
        slot["w"] = (sem, val)


def _r(h, pat, **kw):
    return h.rearrange(pat, **kw)


def _copy(e, out, in_):
    if hasattr(e, "tensor_copy"):
        return e.tensor_copy(out, in_)
    return e.copy(out, in_)


DBG_WATCH = None
DBG_B1W = None
B0_PAIRING = True
B0_REVERSE = False
B0_SERIAL = False
B1_MODE = "full"


def build_program(phases=("A", "B0", "B1", "C", "D"), debug=False):
    nc = bass.Bass("TRN2", target_bir_lowering=False)
    P = Prog()

    scratch_kind = "ExternalOutput" if debug else "Internal"
    dbg_watch = DBG_WATCH
    dbg_b1w = DBG_B1W
    dbg_tensor = (nc.dram_tensor("DBGT", [12 * 128, 512], F32, kind="ExternalOutput")
                  if (debug and (dbg_watch or dbg_b1w)) else None)
    dbg_b1 = (nc.dram_tensor("DBGB1", [2 * 128, NK * 256], F32, kind="ExternalOutput")
              if debug else None)

    xt = nc.dram_tensor("xt", [D, TOK], F32, kind="ExternalInput")
    xr = nc.dram_tensor("xr", [TOK, D], F32, kind="ExternalInput")
    wqkvt = nc.dram_tensor("wqkvt", [D, QKVF], F32, kind="ExternalInput")
    wzt = nc.dram_tensor("wzt", [D, VD], F32, kind="ExternalInput")
    wbat = nc.dram_tensor("wbat", [D, 64], F32, kind="ExternalInput")
    convw = nc.dram_tensor("convw", [QKVF, KC], F32, kind="ExternalInput")
    vecs = nc.dram_tensor("vecs", [1, 192], F32, kind="ExternalInput")
    woutt = nc.dram_tensor("woutt", [VD, D], F32, kind="ExternalInput")
    wgut = nc.dram_tensor("wgut", [D, 2 * INTER], F32, kind="ExternalInput")
    wdot = nc.dram_tensor("wdot", [INTER, D], F32, kind="ExternalInput")

    out = nc.dram_tensor("out", [TOK, D], F32, kind="ExternalOutput")

    QS = nc.dram_tensor("QS", [QD, TOK], BF16, kind=scratch_kind)
    KS = nc.dram_tensor("KS", [QD, TOK], BF16, kind=scratch_kind)
    KTS = nc.dram_tensor("KTS", [TOK, QD], BF16, kind=scratch_kind)
    VS = nc.dram_tensor("VS", [TOK, VD], BF16, kind=scratch_kind)
    ZS = nc.dram_tensor("ZS", [TOK, VD], BF16, kind=scratch_kind)
    ABAD = nc.dram_tensor("ABAD", [128, 32 * 64], F32, kind=scratch_kind)
    OS = nc.dram_tensor("OS", [TOK, VD], BF16, kind=scratch_kind)
    H1F = nc.dram_tensor("H1F", [D, TOK], F32, kind=scratch_kind)
    H1B = nc.dram_tensor("H1B", [D, TOK], BF16, kind=scratch_kind)
    B0M = nc.dram_tensor("B0M", [NCH * NK * 128, 1024], BF16, kind=scratch_kind)
    FFS = nc.dram_tensor("FFS", [INTER, TOK], BF16, kind=scratch_kind)
    woutb = nc.dram_tensor("woutb", [VD, D], BF16, kind="Internal")
    wgutb = nc.dram_tensor("wgutb", [D, 2 * INTER], BF16, kind="Internal")
    wdotb = nc.dram_tensor("wdotb", [INTER, D], BF16, kind=scratch_kind)

    incl_np = np.triu(np.ones((128, 128), dtype=np.float32))
    strictlow_np = np.tril(np.ones((128, 128), dtype=np.float32), -1)
    negstrict_np = -np.triu(np.ones((128, 128), dtype=np.float32), 1)
    e127_np = np.zeros((128, 1), dtype=np.float32); e127_np[127, 0] = 1.0
    C_INCL = nc.inline_tensor(incl_np, "c_incl")
    C_SLOW = nc.inline_tensor(strictlow_np, "c_slow")
    C_NEGS = nc.inline_tensor(negstrict_np, "c_negs")
    C_E127 = nc.inline_tensor(e127_np, "c_e127")
    C_ONES1 = nc.inline_tensor(np.ones((1, 128), dtype=np.float32), "c_ones1")
    C_ONESC = nc.inline_tensor(np.ones((128, 1), dtype=np.float32), "c_onesc")
    C_IDBF = nc.inline_tensor(np.eye(128, dtype=np.float32).astype(BF), "c_idbf")
    C_IDF = nc.inline_tensor(np.eye(128, dtype=np.float32), "c_idf")

    es = contextlib.ExitStack()
    sb = lambda name, shape, dt: es.enter_context(nc.sbuf_tensor(name, shape, dt))
    ps = lambda name, shape, dt: es.enter_context(nc.psum_tensor(name, shape, dt))

    s_incl = sb("s_incl", [128, 128], F32)
    s_slow = sb("s_slow", [128, 128], F32)
    s_negs = sb("s_negs", [128, 128], F32)
    s_e127 = sb("s_e127", [128, 1], F32)
    s_ones1 = sb("s_ones1", [1, 128], F32)
    s_onesc = sb("s_onesc", [128, 1], F32)
    s_idbf = sb("s_idbf", [128, 128], BF16)
    s_idf = sb("s_idf", [128, 128], F32)
    s_vec = sb("s_vec", [1, 192], F32)
    s_dtb = sb("s_dtb", [128, 32], F32)
    s_nega = sb("s_nega", [128, 32], F32)
    s_nrmw = sb("s_nrmw", [128, 128], F32)

    PB = [ps(f"pb{i}", [128, 512], F32) for i in range(6)]
    PT = [ps(f"pt{i}", [128, 1024], BF16) for i in range(2)]

    for dst, src in ((s_incl, C_INCL), (s_slow, C_SLOW), (s_negs, C_NEGS),
                     (s_e127, C_E127), (s_ones1, C_ONES1), (s_onesc, C_ONESC),
                     (s_idbf, C_IDBF), (s_idf, C_IDF), (s_vec, vecs)):
        P.dma("sync", "LW", dst[:, :], src[:, :])
    lw0 = P.val("LW")
    pe = P.op("tensor", lambda e: e.matmul(PB[0][:, 0:32], s_ones1[:, :],
                                           s_vec[0:1, 0:32], start=True, stop=True),
              waits=[("LW", lw0)])
    dv = P.op("vector", lambda e: e.tensor_copy(s_dtb[:, :], PB[0][:, 0:32]),
              waits=[("PE", pe)])
    pe = P.op("tensor", lambda e: e.matmul(PB[1][:, 0:32], s_ones1[:, :],
                                           s_vec[0:1, 32:64], start=True, stop=True),
              waits=[("LW", lw0)])
    ac = P.op("scalar", lambda e: e.activation(s_nega[:, :], PB[1][:, 0:32], AF.Exp),
              waits=[("PE", pe)])
    dv = P.op("vector", lambda e: e.tensor_scalar_mul(s_nega[:, :], s_nega[:, :], -1.0),
              waits=[("AC", ac)])
    pe = P.op("tensor", lambda e: e.matmul(PB[2][:, 0:128], s_ones1[:, :],
                                           s_vec[0:1, 64:192], start=True, stop=True),
              waits=[("LW", lw0)])
    dv_nrm = P.op("vector", lambda e: e.tensor_copy(s_nrmw[:, :], PB[2][:, 0:128]),
                  waits=[("PE", pe)])
    init_dv = P.val("DV")
    init_ac = P.val("AC")

    if "A" in phases:
        emit_phase_a(nc, P, es, locals())
    if "C" in phases or "D" in phases:
        emit_precast(nc, P, es, locals())
    if "B0" in phases or "B1" in phases:
        st = {}
        if "B0" in phases:
            st = emit_phase_b0(nc, P, es, locals())
        if "B1" in phases:
            emit_phase_b1(nc, P, es, locals(), st)
    if "C" in phases:
        emit_phase_c(nc, P, es, locals())
    if "D" in phases:
        emit_phase_d(nc, P, es, locals())

    final_waits = [(s, P.val(s)) for s in ("SW", "SS", "SG") if P.val(s)]
    P.emit("gpsimd", lambda e: e.nop(), final_waits, ())

    realize(nc, P, es)
    es.close()
    return nc


def realize(nc, P, es):
    sems = {}
    for name in P.cnt:
        sems[name] = es.enter_context(nc.semaphore(name))
    with nc.Block() as block:
        for eng_name in ENGINES:
            def body(eng, _en=eng_name):
                seen = {}
                for engine, fn, waits, incs in P.steps:
                    if engine != _en:
                        continue
                    for s, v in waits:
                        if v > 0 and seen.get(s, -1) < v:
                            eng.wait_ge(sems[s], v)
                            seen[s] = v
                    inst = fn(eng)
                    for s, amt in incs:
                        inst = inst.then_inc(sems[s], amt)
            getattr(block, eng_name)(body)


def emit_phase_a(nc, P, es, g):
    PB, PT = g["PB"], g["PT"]
    xt, wqkvt, wzt, wbat, convw = g["xt"], g["wqkvt"], g["wzt"], g["wbat"], g["convw"]
    QS, KS, KTS, VS, ZS, ABAD = g["QS"], g["KS"], g["KTS"], g["VS"], g["ZS"], g["ABAD"]
    s_onesc, s_ones1, s_idbf, s_idf = g["s_onesc"], g["s_ones1"], g["s_idbf"], g["s_idf"]
    init_dv, init_ac = g["init_dv"], g["init_ac"]

    esA = contextlib.ExitStack()
    sb = lambda name, shape, dt: esA.enter_context(nc.sbuf_tensor(name, shape, dt))

    XTB = sb("XTB", [128, DCH * TOK], BF16)
    esA1 = contextlib.ExitStack()
    sb1 = lambda name, shape, dt: esA1.enter_context(nc.sbuf_tensor(name, shape, dt))
    WQ = [sb1(f"WQ{i}", [128, DCH * 128], BF16) for i in range(2)]
    QKVP = sb1("QKVP", [128, NSEQ * (TSEQ + 3)], F32)
    CONVOUT = sb1("CONVOUT", [128, TOK], F32)
    OUTBF = sb1("OUTBF", [128, TOK], BF16)
    TSTG = sb1("TSTG", [128, TOK], BF16)
    RROW = sb1("RROW", [1, 512], F32)
    SQT = sb1("SQT", [128, 512], F32)
    FCA = sb1("FCA", [128, 4], F32)
    PADG1 = sb1("PADG1", [128, 64], F32)
    CW = [sb1(f"CW{i}", [128, KC], F32) for i in range(2)]
    PADG2 = sb1("PADG2", [128, 64], F32)

    xt_r = _r(xt, "(k p) t -> k p t", p=128)
    xl = 0
    for k in range(DCH):
        xl = P.dma("gpsimd", "LG", XTB[:, k * TOK:(k + 1) * TOK], xt_r[k], )
    dv0 = P.op("vector", lambda e: e.memset(QKVP[:, 0:3], 0.0))
    dv0 = P.op("vector", lambda e: e.memset(QKVP[:, TSEQ + 3:TSEQ + 6], 0.0))

    wq_r = _r(wqkvt, "(k p) f -> p k f", p=128)
    cwr = convw

    qkv_banks = Banks([PB[0], PB[1]])
    ss_bank = Banks([PB[2]])
    bc_bank = Banks([PB[3]])
    tr_banks = Banks([PT[0], PT[1]])
    trf_banks = Banks([PB[4], PB[5]])

    wq_done = [0, 0]
    cw_done = [0, 0]
    G = {}
    buf = 0
    wq_done[0] = P.dma("gpsimd", "LG",
                       _r(WQ[0], "p (k f) -> p k f", f=128),
                       wq_r[:, :, 0:128])
    cw_done[0] = P.dma("sync", "LW", CW[0][:, :], cwr[0:128, :])

    for warm, m in [(True, 0)] + [(False, mm) for mm in range(64)]:
        buf = m % 2
        nbuf = 1 - buf
        if m + 1 < 64 and not warm:
            wq_done[nbuf] = P.dma(
                "gpsimd", "LG", _r(WQ[nbuf], "p (k f) -> p k f", f=128),
                wq_r[:, :, (m + 1) * 128:(m + 2) * 128],
                waits=G.pop(f"WQ{nbuf}", ()))
            cw_done[nbuf] = P.dma(
                "sync", "LW", CW[nbuf][:, :], cwr[(m + 1) * 128:(m + 2) * 128, :],
                waits=G.pop(f"CW{nbuf}", ()))

        evac_vals = []
        qkvp_guard = G.pop("QKVP", [])
        for n in range(8):
            slot, bw = qkv_banks.acquire()
            pe = None
            for kk in range(DCH):
                w = list(bw) + ([("LG", max(wq_done[buf], xl))] if kk == 0 else [])
                pe = P.op("tensor", (lambda e, b=slot["t"], l=WQ[buf], kk=kk, n=n:
                                     e.matmul(b[:, 0:512],
                                              l[:, kk * 128:(kk + 1) * 128],
                                              XTB[:, kk * TOK + n * 512:kk * TOK + (n + 1) * 512],
                                              start=(kk == 0), stop=(kk == 15))),
                          waits=w)
                bw = ()
            b_, t_ = divmod(n, 4)
            col = 3 + b_ * (TSEQ + 3) + t_ * 512
            eng = "vector" if n % 2 == 0 else "scalar"
            sem = "DV" if n % 2 == 0 else "AC"
            ev = P.op(eng, (lambda e, b=slot["t"], col=col:
                            _copy(e, QKVP[:, col:col + 512], b[:, 0:512])),
                      waits=[("PE", pe)] + qkvp_guard, sem=sem,
                      fence=(lambda e, col=col:
                             e.copy(FCA[:, 1:2], QKVP[:, col + 511:col + 512]))
                      if eng == "scalar" else False)
            Banks.release(slot, sem, ev)
            evac_vals.append((sem, ev))

        conv_vals = []
        conv_guard = G.pop("CONVOUT", [])
        for b_ in range(NSEQ):
            base = b_ * (TSEQ + 3)
            o0 = b_ * TSEQ
            w = evac_vals + [("LW", cw_done[buf])] + conv_guard
            dvv = P.op("vector", (lambda e, base=base, o0=o0, cb=CW[buf]:
                                  e.tensor_scalar(CONVOUT[:, o0:o0 + TSEQ],
                                                  QKVP[:, base:base + TSEQ],
                                                  cb[:, 0:1], None, ALU.mult)),
                       waits=w)
            for j in range(1, KC):
                dvv = P.op("vector", (lambda e, base=base, o0=o0, j=j, cb=CW[buf]:
                                      e.scalar_tensor_tensor(
                                          CONVOUT[:, o0:o0 + TSEQ],
                                          QKVP[:, base + j:base + j + TSEQ],
                                          cb[:, j:j + 1],
                                          CONVOUT[:, o0:o0 + TSEQ],
                                          ALU.mult, ALU.add)),
                           fence=(lambda e, o0=o0: e.tensor_copy(
                               FCA[:, 0:1], CONVOUT[:, o0 + TSEQ - 1:o0 + TSEQ]))
                           if (j == KC - 1 and m >= 32) else False)
            conv_vals.append(("DV", dvv))

        if m < 32:
            mul_vals = []
            outbf_guard = G.pop("OUTBF", [])
            rrow_pe = None
            sqt_pe = None
            for n in range(8):
                sq = P.op("vector", (lambda e, n=n: e.tensor_tensor(
                    SQT[:, :], CONVOUT[:, n * 512:(n + 1) * 512],
                    CONVOUT[:, n * 512:(n + 1) * 512], ALU.mult)),
                    waits=conv_vals + ([("PE", sqt_pe)] if sqt_pe else []),
                    fence=lambda e: e.tensor_copy(FCA[:, 0:1], SQT[:, 511:512]))
                slot, bw = ss_bank.acquire()
                pe = P.op("tensor", (lambda e, b=slot["t"]:
                                     e.matmul(b[0:1, 0:512], s_onesc[:, :],
                                              SQT[:, :],
                                              start=True, stop=True)),
                          waits=[("DV", sq)] + list(bw))
                sqt_pe = pe
                wv = [("PE", pe)] + ([("PE", rrow_pe)] if rrow_pe else [])
                ac = P.op("scalar", (lambda e, b=slot["t"]:
                                     e.activation(RROW[0:1, :], b[0:1, 0:512], AF.Sqrt)),
                          waits=wv,
                          fence=lambda e: e.copy(FCA[0:1, 1:2], RROW[0:1, 511:512]))
                Banks.release(slot, "AC", ac)
                dvv = P.op("vector", lambda e: e.tensor_scalar_max(
                    RROW[0:1, :], RROW[0:1, :], 1e-12), waits=[("AC", ac)])
                dvv = P.op("vector", lambda e: e.reciprocal(RROW[0:1, :], RROW[0:1, :]),
                           fence=lambda e: e.tensor_copy(FCA[0:1, 0:1],
                                                         RROW[0:1, 511:512]))
                bslot, bbw = bc_bank.acquire()
                pe2 = P.op("tensor", (lambda e, b=bslot["t"]:
                                      e.matmul(b[:, 0:512], s_ones1[:, :],
                                               RROW[0:1, :], start=True, stop=True)),
                           waits=[("DV", dvv)] + list(bbw))
                rrow_pe = pe2
                mv = P.op("vector", (lambda e, b=bslot["t"], n=n:
                                     e.tensor_tensor(OUTBF[:, n * 512:(n + 1) * 512],
                                                     CONVOUT[:, n * 512:(n + 1) * 512],
                                                     b[:, 0:512], ALU.mult)),
                          waits=[("PE", pe2)] + (outbf_guard if n == 0 else []),
                          fence=(lambda e, n=n: e.tensor_copy(
                              FCA[:, 2:3], OUTBF[:, (n + 1) * 512 - 1:(n + 1) * 512]))
                          if n == 7 else False)
                Banks.release(bslot, "DV", mv)
                mul_vals.append(("DV", mv))
            dst = QS if m < 16 else KS
            row = (m if m < 16 else m - 16) * 128
            if not warm:
                stv = P.dma("sync", "SW", dst[row:row + 128, :], OUTBF[:, :],
                            waits=mul_vals)
                G["OUTBF"] = [("SW", stv)]
            if m >= 16:
                tr_vals = emit_transpose_block(nc, P, tr_banks, OUTBF, TSTG,
                                               s_idbf, mul_vals,
                                               G.pop("TSTG", []), fence_t=FCA)
                stv2 = P.dma("sync", "SW",
                             _r(KTS, "(c p) f -> p c f", p=128)[:, :, row:row + 128],
                             _r(TSTG, "p (c f) -> p c f", f=128),
                             waits=tr_vals)
                G["TSTG"] = [("SW", stv2)]
                G["OUTBF"].append(("PE", P.val("PE")))
            G["QKVP"] = [("DV", P.val("DV")), ("PE", P.val("PE"))]
        else:
            tr_vals = emit_transpose_block(nc, P, trf_banks, CONVOUT, TSTG,
                                           s_idf, conv_vals, G.pop("TSTG", []),
                                           fence_t=FCA)
            row = (m - 32) * 128
            stv2 = P.dma("sync", "SW",
                         _r(VS, "(c p) f -> p c f", p=128)[:, :, row:row + 128],
                         _r(TSTG, "p (c f) -> p c f", f=128),
                         waits=tr_vals)
            G["TSTG"] = [("SW", stv2)]
            G["CONVOUT"] = [("PE", P.val("PE"))]
            G["QKVP"] = [("DV", P.val("DV")), ("AC", P.val("AC"))]
        G[f"WQ{buf}"] = [("PE", P.val("PE"))]
        G[f"CW{buf}"] = [("DV", P.val("DV"))]

    P.barrier()
    esA1.close()
    WZ = [sb(f"WZ{i}", [128, DCH * 512], BF16) for i in range(2)]
    ZSTG = [sb(f"ZSTG{i}", [128, 512], BF16) for i in range(2)]
    FCA2 = sb("FCA2", [128, 2], F32)
    wz_r = _r(wzt, "(k p) f -> p k f", p=128)
    z_banks = Banks([PB[0], PB[1]])
    wz_done = [0, 0]
    wz_done[0] = P.dma("gpsimd", "LG", _r(WZ[0], "p (k f) -> p k f", f=512),
                       wz_r[:, :, 0:512])
    zst_prev = [None, None]
    wz_guard = [None, None]
    for s in range(8):
        buf = s % 2
        nbuf = 1 - buf
        if s + 1 < 8:
            wz_done[nbuf] = P.dma("gpsimd", "LG",
                                  _r(WZ[nbuf], "p (k f) -> p k f", f=512),
                                  wz_r[:, :, (s + 1) * 512:(s + 2) * 512],
                                  waits=[wz_guard[nbuf]] if wz_guard[nbuf] else ())
        for tb in range(32):
            slot, bw = z_banks.acquire()
            pe = None
            for kk in range(DCH):
                w = list(bw) + ([("LG", wz_done[buf])] if kk == 0 else [])
                pe = P.op("tensor", (lambda e, b=slot["t"], kk=kk, tb=tb, l=WZ[buf]:
                                     e.matmul(b[:, 0:512],
                                              XTB[:, kk * TOK + tb * 128:kk * TOK + tb * 128 + 128],
                                              l[:, kk * 512:(kk + 1) * 512],
                                              start=(kk == 0), stop=(kk == 15))),
                          waits=w)
                bw = ()
            zb = tb % 2
            w = [("PE", pe)] + ([zst_prev[zb]] if zst_prev[zb] else [])
            ac = P.op("scalar", (lambda e, b=slot["t"], zb=zb:
                                 e.activation(ZSTG[zb][:, :], b[:, 0:512], AF.Silu)),
                      waits=w,
                      fence=(lambda e, zb=zb:
                             e.copy(FCA2[:, 0:1], ZSTG[zb][:, 511:512])))
            Banks.release(slot, "AC", ac)
            stv = P.dma("scalar", "SS", ZS[tb * 128:(tb + 1) * 128, s * 512:(s + 1) * 512],
                        ZSTG[zb][:, :], waits=[("AC", ac)])
            zst_prev[zb] = ("SS", stv)
        wz_guard[buf] = ("PE", P.val("PE"))

    WBA = sb("WBA", [128, DCH * 64], BF16)
    ABAS = sb("ABAS", [128, 32 * 64], F32)
    wba_done = P.dma("gpsimd", "LG", _r(WBA, "p (k f) -> p k f", f=64),
                     _r(wbat, "(k p) f -> p k f", p=128))
    ba_banks = Banks([PB[2], PB[3]])
    aba_vals = []
    for tb in range(32):
        slot, bw = ba_banks.acquire()
        pe = None
        for kk in range(DCH):
            w = list(bw) + ([("LG", wba_done)] if kk == 0 else [])
            pe = P.op("tensor", (lambda e, b=slot["t"], kk=kk, tb=tb:
                                 e.matmul(b[:, 0:64],
                                          XTB[:, kk * TOK + tb * 128:kk * TOK + tb * 128 + 128],
                                          WBA[:, kk * 64:(kk + 1) * 64],
                                          start=(kk == 0), stop=(kk == 15))),
                      waits=w)
            bw = ()
        dvv = P.op("vector", (lambda e, b=slot["t"], tb=tb:
                              e.tensor_copy(ABAS[:, tb * 64:(tb + 1) * 64], b[:, 0:64])),
                   waits=[("PE", pe)],
                   fence=(lambda e, tb=tb: e.tensor_copy(
                       FCA2[:, 1:2], ABAS[:, (tb + 1) * 64 - 1:(tb + 1) * 64]))
                   if tb == 31 else False)
        Banks.release(slot, "DV", dvv)
        aba_vals.append(("DV", dvv))
    P.dma("sync", "SW", ABAD[:, :], ABAS[:, :], waits=aba_vals)

    P.barrier()
    esA.close()


def emit_transpose_block(nc, P, banks, src, dststg, ident, in_vals, stg_guard,
                         fence_t=None):
    vals = []
    for grp in range(8):
        slot, bw = banks.acquire()
        pe = None
        for q in range(4):
            c = grp * 4 + q
            w = list(bw) + (list(in_vals) if q == 0 else [])
            pe = P.op("tensor", (lambda e, b=slot["t"], c=c, q=q:
                                 e.transpose(b[:, q * 128:(q + 1) * 128],
                                             src[:, c * 128:(c + 1) * 128],
                                             ident[:, :])),
                      waits=w)
            bw = ()
        eng, sem = ("vector", "DV") if grp % 2 == 0 else ("scalar", "AC")
        fc = False
        if fence_t is not None:
            fc = (lambda e, grp=grp, col=(2 if grp % 2 == 0 else 3):
                  _copy(e, fence_t[:, col:col + 1],
                        dststg[:, (grp + 1) * 512 - 1:(grp + 1) * 512]))
        ev = P.op(eng, (lambda e, b=slot["t"], grp=grp:
                        _copy(e, dststg[:, grp * 512:(grp + 1) * 512],
                              b[:, 0:512])),
                  waits=[("PE", pe)] + (list(stg_guard) if grp == 0 else []),
                  sem=sem, fence=fc)
        Banks.release(slot, sem, ev)
        vals.append((sem, ev))
    return vals


def emit_phase_b0(nc, P, es, g):
    PB, PT = g["PB"], g["PT"]
    QS, KS, KTS, VS, ABAD, B0M = (g["QS"], g["KS"], g["KTS"], g["VS"],
                                  g["ABAD"], g["B0M"])
    s_incl, s_slow, s_negs = g["s_incl"], g["s_slow"], g["s_negs"]
    s_e127, s_ones1, s_idbf, s_idf = (g["s_e127"], g["s_ones1"], g["s_idbf"],
                                      g["s_idf"])
    s_dtb, s_nega = g["s_dtb"], g["s_nega"]

    P.barrier()
    ECUMR = es.enter_context(nc.sbuf_tensor("ECUMR", [128, NCH * 32], F32))
    ECCR = es.enter_context(nc.sbuf_tensor("ECCR", [128, NCH * 32], F32))

    esB = contextlib.ExitStack()
    sb = lambda name, shape, dt: esB.enter_context(nc.sbuf_tensor(name, shape, dt))

    ABR = sb("ABR", [128, 32 * 64], F32)
    QFB = [sb(f"QFB{i}", [128, 2048], BF16) for i in range(2)]
    KFB = [sb(f"KFB{i}", [128, 2048], BF16) for i in range(2)]
    KTB = [sb(f"KTB{i}", [128, 2048], BF16) for i in range(2)]
    VTB = [sb(f"VTB{i}", [128, 4096], BF16) for i in range(2)]
    SM = []
    for i in range(2):
        SM.append(dict(
            BB=sb(f"BB{i}", [128, 32], F32),
            LNB=sb(f"LNB{i}", [128, 32], F32),
            T1=sb(f"T1_{i}", [128, 32], F32),
            T2=sb(f"T2_{i}", [128, 32], F32),
            GG=sb(f"GG{i}", [128, 32], F32),
            TRIN=sb(f"TRIN{i}", [128, 64], F32),
            CPRV=sb(f"CPRV{i}", [128, 32], F32),
            ESCC=sb(f"ESCC{i}", [128, 32], F32),
            EBC=sb(f"EBC{i}", [128, 32], F32),
            ECCW=sb(f"ECCW{i}", [1, 32], F32),
            FSM=sb(f"FSM{i}", [128, 4], F32),
            done=[],
        ))
    HS = []
    for i in range(2):
        HS.append(dict(
            MG=sb(f"MG{i}", [128, 128], F32),
            MKQ=sb(f"MKQ{i}", [128, 128], F32),
            EG=sb(f"EG{i}", [128, 256], F32),
            EXP1=sb(f"EXP1_{i}", [128, 256], F32),
            EXP2=sb(f"EXP2_{i}", [128, 256], F32),
            NP=[sb(f"NP{i}_{j}", [128, 256], BF16) for j in range(7)],
            TPS=[sb(f"TPS{i}_{j}", [128, 256], BF16) for j in range(7)],
            AT=sb(f"AT{i}", [128, 256], BF16),
            QQ=sb(f"QQ{i}", [128, 256], BF16),
            WN=sb(f"WN{i}", [128, 256], BF16),
            BV=sb(f"BV{i}", [128, 256], BF16),
            KP=sb(f"KP{i}", [128, 256], BF16),
            TW=sb(f"TW{i}", [128, 256], BF16),
            U0=sb(f"U0{i}", [128, 256], BF16),
            REC=sb(f"REC{i}", [128, 1024], BF16),
            ROWT=sb(f"ROWT{i}", [1, 512], F32),
            FDB=sb(f"FDB{i}", [128, 8], BF16),
            guard=[], rec_guard=[],
        ))

    fb = Banks(PB)
    tb = Banks(PT)

    abr_done = P.dma("sync", "LW", ABR[:, :], ABAD[:, :])
    qs_r = _r(QS, "(h p) t -> p h t", p=128)
    ks_r = _r(KS, "(h p) t -> p h t", p=128)
    ld_done = [None, None]

    def load_chunk(c, guards):
        buf = c % 2
        P.dma("sync", "LW", _r(QFB[buf], "p (h t) -> p h t", t=128),
              qs_r[:, :, c * 128:(c + 1) * 128], waits=list(guards))
        P.dma("sync", "LW", _r(KFB[buf], "p (h t) -> p h t", t=128),
              ks_r[:, :, c * 128:(c + 1) * 128])
        P.dma("sync", "LW", KTB[buf][:, :], KTS[c * 128:(c + 1) * 128, :])
        v4 = P.dma("sync", "LW", VTB[buf][:, :], VS[c * 128:(c + 1) * 128, :])
        ld_done[buf] = ("LW", v4)

    def chunk_smalls(c, guards):
        par = c % 2
        s = SM[par]
        w = [("LW", abr_done)] + list(guards)
        ab = ABR[:, c * 64:c * 64 + 32]
        aa = ABR[:, c * 64 + 32:c * 64 + 64]
        vBB = P.op("scalar", lambda e, ab=ab: e.activation(s["BB"][:, :], ab, AF.Sigmoid),
                   waits=w)
        vabs = P.op("scalar", lambda e, ab=ab: e.activation(s["T1"][:, :], ab, AF.Abs),
                    waits=w)
        P.op("scalar", lambda e: e.activation(s["T1"][:, :], s["T1"][:, :], AF.Exp,
                                              scale=-1.0))
        vL = P.op("scalar", lambda e: e.activation(s["T1"][:, :], s["T1"][:, :], AF.Ln,
                                                   bias=1.0),
                  fence=lambda e: e.copy(s["FSM"][:, 1:2], s["T1"][:, 31:32]))
        P.op("vector", (lambda e, ab=ab: e.tensor_scalar(s["LNB"][:, :], ab, 0.0, None,
                                                         ALU.min)), waits=w)
        P.op("vector", (lambda e: e.tensor_tensor(s["LNB"][:, :], s["LNB"][:, :],
                                                  s["T1"][:, :], ALU.subtract)),
             waits=[("AC", vL)])
        vdt = P.op("vector", (lambda e, aa=aa: e.tensor_tensor(s["T2"][:, :], aa,
                                                               s_dtb[:, :], ALU.add)),
                   waits=w,
                   fence=lambda e: e.tensor_copy(s["FSM"][:, 0:1], s["T2"][:, 31:32]))
        vabs2 = P.op("scalar", lambda e: e.activation(s["GG"][:, :], s["T2"][:, :], AF.Abs),
                     waits=[("DV", vdt)])
        P.op("scalar", lambda e: e.activation(s["GG"][:, :], s["GG"][:, :], AF.Exp,
                                              scale=-1.0))
        vLa = P.op("scalar", lambda e: e.activation(s["GG"][:, :], s["GG"][:, :], AF.Ln,
                                                    bias=1.0),
                   fence=lambda e: e.copy(s["FSM"][:, 1:2], s["GG"][:, 31:32]))
        P.op("vector", (lambda e: e.tensor_scalar(s["T2"][:, :], s["T2"][:, :], 0.0,
                                                  None, ALU.max)),
             waits=[("AC", vabs2)])
        P.op("vector", (lambda e: e.tensor_tensor(s["T2"][:, :], s["T2"][:, :],
                                                  s["GG"][:, :], ALU.add)),
             waits=[("AC", vLa)])
        vgg = P.op("vector", (lambda e: e.tensor_tensor(s["GG"][:, :], s["T2"][:, :],
                                                        s_nega[:, :], ALU.mult)),
                   fence=lambda e: e.tensor_copy(s["FSM"][:, 0:1], s["GG"][:, 31:32]))
        slot, bw = fb.acquire()
        pcum = P.op("tensor", (lambda e, b=slot["t"]:
                               e.matmul(b[:, 0:32], s_incl[:, :], s["GG"][:, :],
                                        start=True, stop=True)),
                    waits=[("DV", vgg)] + list(bw))
        vcum = P.op("vector", (lambda e, b=slot["t"]:
                               e.tensor_copy(s["TRIN"][:, 32:64], b[:, 0:32])),
                    waits=[("PE", pcum)],
                    fence=lambda e: e.tensor_copy(s["FSM"][:, 0:1],
                                                  s["TRIN"][:, 63:64]))
        Banks.release(slot, "DV", vcum)
        vcp = P.op("vector", (lambda e: e.tensor_tensor(s["CPRV"][:, :],
                                                        s["TRIN"][:, 32:64],
                                                        s["GG"][:, :], ALU.subtract)),
                   fence=lambda e: e.tensor_copy(s["FSM"][:, 0:1],
                                                 s["CPRV"][:, 31:32]))
        vlbc = P.op("vector", (lambda e: e.tensor_tensor(s["TRIN"][:, 0:32],
                                                         s["LNB"][:, :],
                                                         s["CPRV"][:, :], ALU.add)))
        slot, bw = fb.acquire()
        prev = P.op("tensor", (lambda e, b=slot["t"]:
                               e.matmul(b[:, 0:32], s_slow[:, :], s["GG"][:, :],
                                        start=True, stop=True)),
                    waits=[("DV", vgg)] + list(bw))
        vesc = P.op("scalar", (lambda e, b=slot["t"]:
                               e.activation(s["ESCC"][:, :], b[:, 0:32], AF.Exp)),
                    waits=[("PE", prev)])
        Banks.release(slot, "AC", vesc)
        vecum = P.op("scalar", (lambda e, c=c: e.activation(
            ECUMR[:, c * 32:(c + 1) * 32], s["TRIN"][:, 32:64], AF.Exp)),
            waits=[("DV", vcum)])
        vecp = P.op("scalar", (lambda e: e.activation(s["EBC"][:, :], s["CPRV"][:, :],
                                                      AF.Exp)), waits=[("DV", vcp)],
                    fence=lambda e: e.copy(s["FSM"][:, 1:2], s["EBC"][:, 31:32]))
        veb = P.op("vector", (lambda e: e.tensor_tensor(s["EBC"][:, :], s["EBC"][:, :],
                                                        s["BB"][:, :], ALU.mult)),
                   waits=[("AC", vecp), ("AC", vBB)])
        P.op("vector", lambda e: e.tensor_scalar_mul(s["EBC"][:, :], s["EBC"][:, :],
                                                     -1.0))
        slot, bw = fb.acquire()
        pec = P.op("tensor", (lambda e, b=slot["t"]:
                              e.matmul(b[0:1, 0:32], s_e127[:, :], s["TRIN"][:, 32:64],
                                       start=True, stop=True)),
                   waits=[("DV", vcum)] + list(bw))
        vecc = P.op("scalar", (lambda e, b=slot["t"]:
                               e.activation(s["ECCW"][0:1, :], b[0:1, 0:32], AF.Exp)),
                    waits=[("PE", pec)],
                    fence=lambda e: e.copy(s["FSM"][0:1, 1:2], s["ECCW"][0:1, 31:32]))
        Banks.release(slot, "AC", vecc)
        slot, bw = fb.acquire()
        pbc = P.op("tensor", (lambda e, b=slot["t"]:
                              e.matmul(b[:, 0:32], s_ones1[:, :], s["ECCW"][0:1, :],
                                       start=True, stop=True)),
                   waits=[("AC", vecc)] + list(bw))
        vc2 = P.op("vector", (lambda e, b=slot["t"], c=c:
                              e.tensor_copy(ECCR[:, c * 32:(c + 1) * 32], b[:, 0:32])),
                   waits=[("PE", pbc)])
        Banks.release(slot, "DV", vc2)
        fv = P.op("vector", (lambda e, c=c: e.tensor_tensor(
            s["FSM"][:, 0:1], s["TRIN"][:, 63:64],
            ECCR[:, c * 32 + 31:c * 32 + 32], ALU.add)))
        P.op("scalar", lambda e: e.copy(s["FSM"][0:1, 1:2], s["ECCW"][0:1, 31:32]))
        fa = P.op("scalar", lambda e: e.copy(s["FSM"][:, 2:3], s["ESCC"][:, 31:32]))
        s["done"] = [("DV", fv), ("AC", fa)]
        return s["done"]

    def head_stages(c, h, hs, flat_done):
        par = c % 2
        s = SM[par]
        buf = c % 2
        QF, KF, KT, VT = QFB[buf], KFB[buf], KTB[buf], VTB[buf]
        st = {}
        vh0 = 2 * h
        g0 = list(hs["guard"])
        hs["guard"] = []

        def col(tile, j):
            return tile[:, j:j + 1]

        def colrep(tile, j, n):
            return bass.AP(tile, j, [[tile.shape[1], 128], [0, n]])

        def rep2(t):
            return bass.AP(t, 0, [[t.shape[1], 128], [0, 2], [1, 128]])

        def stage_grams():
            w = [ld_done[buf]] + g0
            slot, bw = fb.acquire()
            p1 = P.op("tensor", (lambda e, b=slot["t"]:
                                 e.matmul(b[:, 0:128], KF[:, h * 128:(h + 1) * 128],
                                          KF[:, h * 128:(h + 1) * 128],
                                          start=True, stop=True)),
                      waits=w + list(bw))
            v1 = P.op("vector", (lambda e, b=slot["t"]:
                                 e.tensor_tensor(hs["MG"][:, :], b[:, 0:128],
                                                 s_negs[:, :], ALU.mult)),
                      waits=[("PE", p1)] + g0)
            Banks.release(slot, "DV", v1)
            slot, bw = fb.acquire()
            p2 = P.op("tensor", (lambda e, b=slot["t"]:
                                 e.matmul(b[:, 0:128], KF[:, h * 128:(h + 1) * 128],
                                          QF[:, h * 128:(h + 1) * 128],
                                          start=True, stop=True)),
                      waits=list(bw))
            v2 = P.op("vector", (lambda e, b=slot["t"]:
                                 e.tensor_tensor(hs["MKQ"][:, :], b[:, 0:128],
                                                 s_incl[:, :], ALU.mult)),
                      waits=[("PE", p2)])
            Banks.release(slot, "DV", v2)
            st["mg"] = ("DV", v1)
            st["mkq"] = ("DV", v2)

        def stage_rows():
            slot, bw = fb.acquire()
            pe = None
            for qq, ci in enumerate((vh0, vh0 + 1, 32 + vh0, 32 + vh0 + 1)):
                pe = P.op("tensor", (lambda e, b=slot["t"], qq=qq, ci=ci:
                                     e.transpose(b[0:1, qq * 128:(qq + 1) * 128],
                                                 s["TRIN"][:, ci:ci + 1],
                                                 s_idf[:, :])),
                          waits=(s["done"] + list(bw) + g0) if qq == 0 else [])
            ev = P.op("vector", (lambda e, b=slot["t"]:
                                 e.tensor_copy(hs["ROWT"][0:1, 0:512],
                                               b[0:1, 0:512])),
                      waits=[("PE", pe)] + g0)
            Banks.release(slot, "DV", ev)
            st["rows"] = ("DV", ev)

        def mkexp(which):
            def fn():
                fcol = 0 if which == 1 else 256
                slot, bw = fb.acquire()
                pb_ = P.op("tensor", (lambda e, b=slot["t"]:
                                      e.matmul(b[:, 0:256], s_ones1[:, :],
                                               hs["ROWT"][0:1, fcol:fcol + 256],
                                               start=True, stop=True)),
                          waits=[st["rows"]] + list(bw) + g0)
                ev = None
                for p in range(2):
                    ww = [("PE", pb_)] + s["done"] + g0
                    if which == 2:
                        ww.append(st["exp1"])
                    ev = P.op("vector", (lambda e, b=slot["t"], p=p:
                                         e.tensor_tensor(
                                             hs["EG"][:, p * 128:(p + 1) * 128],
                                             b[:, p * 128:(p + 1) * 128],
                                             colrep(s["TRIN"], 32 + vh0 + p, 128),
                                             ALU.subtract)),
                              waits=ww)
                ev = P.op("vector", (lambda e: e.tensor_scalar(
                    hs["EG"][:, :], hs["EG"][:, :], 0.0, None, ALU.min)))
                Banks.release(slot, "DV", ev)
                dst = hs["EXP1"] if which == 1 else hs["EXP2"]
                ex = P.op("scalar", (lambda e, d=dst: e.activation(d[:, :],
                                                                   hs["EG"][:, :],
                                                                   AF.Exp)),
                          waits=[("DV", ev)] + g0)
                st[f"exp{which}"] = ("AC", ex)
            return fn

        def stage_ntat():
            v1 = P.op("vector", (lambda e: e.tensor_tensor(
                _r(hs["NP"][0], "p (r t) -> p r t", t=128),
                _r(hs["EXP1"], "p (r t) -> p r t", t=128),
                rep2(hs["MG"]), ALU.mult)),
                waits=[st["exp1"], st["mg"]] + g0)
            v2 = P.op("vector", (lambda e: e.tensor_tensor(
                _r(hs["AT"], "p (r t) -> p r t", t=128),
                _r(hs["EXP2"], "p (r t) -> p r t", t=128),
                rep2(hs["MKQ"]), ALU.mult)),
                waits=[st["exp2"], st["mkq"]])
            st["np0"] = ("DV", v1)
            st["at"] = ("DV", v2)

        def mklevel(j):
            def fn():
                src = hs["NP"][j - 1]
                srcv = st[f"np{j - 1}"]
                slot, bw = tb.acquire()
                pe = None
                for p in range(2):
                    pe = P.op("tensor", (lambda e, b=slot["t"], p=p:
                                         e.transpose(b[:, p * 128:(p + 1) * 128],
                                                     src[:, p * 128:(p + 1) * 128],
                                                     s_idbf[:, :])),
                              waits=([srcv] + list(bw) + g0) if p == 0 else [])
                eng, sem = ("scalar", "AC") if j % 2 else ("vector", "DV")
                ev = P.op(eng, (lambda e, b=slot["t"], j=j:
                                _copy(e, hs["TPS"][j - 1][:, :], b[:, 0:256])),
                          waits=[("PE", pe)] + g0, sem=sem)
                Banks.release(slot, sem, ev)
                st[f"tps{j - 1}"] = (sem, ev)
                if j <= 6:
                    slot2, bw2 = fb.acquire()
                    pe2 = None
                    for p in range(2):
                        pe2 = P.op("tensor", (lambda e, b=slot2["t"], p=p, j=j:
                                              e.matmul(b[:, p * 128:(p + 1) * 128],
                                                       hs["TPS"][j - 1][:, p * 128:(p + 1) * 128],
                                                       src[:, p * 128:(p + 1) * 128],
                                                       start=(p == 0), stop=(p == 1))),
                                   waits=([(sem, ev)] + list(bw2)) if p == 0 else [])
                    eng2, sem2 = ("vector", "DV") if j % 2 else ("scalar", "AC")
                    ev2 = P.op(eng2, (lambda e, b=slot2["t"], j=j:
                                      _copy(e, hs["NP"][j][:, :], b[:, 0:256])),
                               waits=[("PE", pe2)] + g0, sem=sem2)
                    Banks.release(slot2, sem2, ev2)
                    st[f"np{j}"] = (sem2, ev2)
            return fn

        def stage_qq_init():
            v = P.op("vector", (lambda e: e.tensor_tensor(
                _r(hs["QQ"], "p (r t) -> p r t", t=128),
                _r(hs["NP"][0], "p (r t) -> p r t", t=128),
                rep2(s_idbf), ALU.add)),
                waits=[st["np0"]] + g0)
            st["qq"] = ("DV", v)

        def mkprod(j):
            def fn():
                slot, bw = fb.acquire()
                pe = None
                for p in range(2):
                    pe = P.op("tensor", (lambda e, b=slot["t"], p=p, j=j:
                                         e.matmul(b[:, p * 128:(p + 1) * 128],
                                                  hs["TPS"][j][:, p * 128:(p + 1) * 128],
                                                  hs["QQ"][:, p * 128:(p + 1) * 128],
                                                  start=(p == 0), stop=(p == 1))),
                              waits=([st["qq"], st[f"tps{j}"]] + list(bw)) if p == 0 else [])
                v = P.op("vector", (lambda e, b=slot["t"]:
                                    e.tensor_tensor(hs["QQ"][:, :], hs["QQ"][:, :],
                                                    b[:, 0:256], ALU.add)),
                         waits=[("PE", pe)])
                Banks.release(slot, "DV", v)
                st["qq"] = ("DV", v)
            return fn

        def stage_prep():
            w = [ld_done[buf]] + s["done"] + g0
            P.op("vector", (lambda e: e.tensor_tensor(
                hs["WN"][:, 0:128], KT[:, h * 128:(h + 1) * 128],
                colrep(s["EBC"], vh0, 128), ALU.mult)), waits=w)
            P.op("vector", (lambda e: e.tensor_tensor(
                hs["WN"][:, 128:256], KT[:, h * 128:(h + 1) * 128],
                colrep(s["EBC"], vh0 + 1, 128), ALU.mult)))
            P.op("vector", (lambda e: e.tensor_tensor(
                hs["BV"][:, 0:128], VT[:, vh0 * 128:vh0 * 128 + 128],
                colrep(s["BB"], vh0, 128), ALU.mult)))
            P.op("vector", (lambda e: e.tensor_tensor(
                hs["BV"][:, 128:256], VT[:, vh0 * 128 + 128:vh0 * 128 + 256],
                colrep(s["BB"], vh0 + 1, 128), ALU.mult)))
            P.op("vector", (lambda e: e.tensor_tensor(
                hs["KP"][:, 0:128], KT[:, h * 128:(h + 1) * 128],
                colrep(s["ESCC"], vh0, 128), ALU.mult)))
            v3 = P.op("vector", (lambda e: e.tensor_tensor(
                hs["KP"][:, 128:256], KT[:, h * 128:(h + 1) * 128],
                colrep(s["ESCC"], vh0 + 1, 128), ALU.mult)))
            st["prep"] = ("DV", v3)

        def mkfinal(name, lhs_key, rhs_key, wkeys, outcol):
            def fn():
                lhs, rhs = hs[lhs_key], hs[rhs_key]
                wk = [st[k] for k in wkeys]
                slot, bw = fb.acquire()
                pe = None
                for p in range(2):
                    pe = P.op("tensor", (lambda e, b=slot["t"], p=p:
                                         e.matmul(b[:, p * 128:(p + 1) * 128],
                                                  lhs[:, p * 128:(p + 1) * 128],
                                                  rhs[:, p * 128:(p + 1) * 128],
                                                  start=(p == 0), stop=(p == 1))),
                              waits=(wk + list(bw)) if p == 0 else [])
                if outcol is None:
                    dst = hs["TW"] if name == "tw" else hs["U0"]
                    ev = P.op("scalar", (lambda e, b=slot["t"], d=dst:
                                         _copy(e, d[:, :], b[:, 0:256])),
                              waits=[("PE", pe)] + g0, sem="AC")
                else:
                    ev = P.op("scalar", (lambda e, b=slot["t"], oc=outcol:
                                         _copy(e, hs["REC"][:, oc:oc + 256],
                                               b[:, 0:256])),
                              waits=[("PE", pe)] + g0 + hs["rec_guard"],
                              sem="AC",
                              fence=(lambda e, oc=outcol:
                                     e.copy(hs["FDB"][:, 0:1],
                                            hs["REC"][:, oc + 255:oc + 256])))
                st[f"{name}v"] = ("AC", ev)
                Banks.release(slot, "AC", ev)
            return fn

        def stage_store():
            w = [st[k] for k in ("motv", "mstv", "oindv", "kindv")]
            if g.get("dbg_watch") == (c, h):
                dbgt = g["dbg_tensor"]
                for i, key in enumerate(("AT", "TW", "U0", "KP", "QQ", "EXP2",
                                         "MKQ", "WN")):
                    wid = min(256, hs[key].shape[1])
                    P.dma("gpsimd", "SG", dbgt[i * 128:(i + 1) * 128, 0:wid],
                          hs[key][:, 0:wid], waits=w)
                P.dma("gpsimd", "SG", dbgt[8 * 128:9 * 128, 0:64],
                      s["TRIN"][:, 0:64], waits=w)
                P.dma("gpsimd", "SG", dbgt[9 * 128:9 * 128 + 1, 0:512],
                      hs["ROWT"][0:1, 0:512], waits=w)
                P.dma("gpsimd", "SG", dbgt[10 * 128:11 * 128, 0:32],
                      s["EBC"][:, 0:32], waits=w)
                P.dma("gpsimd", "SG", dbgt[11 * 128:12 * 128, 0:256],
                      hs["EG"][:, 0:256], waits=w)
            r = (c * NK + h) * 128
            v = P.dma("sync", "SW", B0M[r:r + 128, :], hs["REC"][:, :], waits=w)
            hs["rec_guard"] = [("SW", v)]
            hs["guard"] = [("PE", P.val("PE")), ("DV", P.val("DV")),
                           ("AC", P.val("AC")), ("PO", P.val("PO"))]
            if g.get("dbg_watch") == (c, h):
                hs["guard"].append(("SG", P.val("SG")))

        stages = [stage_grams, stage_rows, mkexp(1), mkexp(2), stage_ntat]
        stages += [mklevel(j) for j in range(1, 8)]
        stages += [stage_qq_init]
        stages += [mkprod(j) for j in range(1, 7)]
        stages += [stage_prep]
        stages += [mkfinal("tw", "QQ", "WN", ["qq", "prep"], None)]
        stages += [mkfinal("u0", "QQ", "BV", ["qq", "prep"], None)]
        stages += [mkfinal("mst", "TW", "KP", ["twv", "prep"], 256)]
        stages += [mkfinal("oind", "AT", "U0", ["at", "u0v"], 512)]
        stages += [mkfinal("kind", "KP", "U0", ["prep", "u0v"], 768)]
        stages += [mkfinal("mot", "TW", "AT", ["twv", "at"], 0)]
        stages += [stage_store]
        return stages

    if B0_SERIAL:
        head_guard = []
        for c in range(NCH):
            P.barrier()
            load_chunk(c, [])
            flat_done = chunk_smalls(c, [])
            for h in range(NK):
                st0 = head_stages(c, h, HS[0], flat_done)
                for i in range(len(st0)):
                    st0[i]()
        P.barrier()
        if g["debug"]:
            ECUMD = nc.dram_tensor("ECUMD", [128, NCH * 32], F32, kind="ExternalOutput")
            ECCD = nc.dram_tensor("ECCD", [128, NCH * 32], F32, kind="ExternalOutput")
            P.dma("sync", "SW", ECUMD[:, :], ECUMR[:, :])
            P.dma("sync", "SW", ECCD[:, :], ECCR[:, :])
        esB.close()
        return dict(ECUMR=ECUMR, ECCR=ECCR)

    load_chunk(0, [])
    head_guard = []
    for c in range(NCH):
        if c + 1 < NCH:
            load_chunk(c + 1, head_guard)
        flat_done = chunk_smalls(c, head_guard)
        head_guard = []
        if B0_PAIRING:
            for hp in range(NK // 2):
                st0 = head_stages(c, 2 * hp, HS[0], flat_done)
                st1 = head_stages(c, 2 * hp + 1, HS[1], flat_done)
                for i in range(len(st0)):
                    st0[i]()
                    st1[i]()
        else:
            order = range(NK - 1, -1, -1) if B0_REVERSE else range(NK)
            for h in order:
                st0 = head_stages(c, h, HS[0], flat_done)
                for i in range(len(st0)):
                    st0[i]()
        head_guard = [("PE", P.val("PE")), ("DV", P.val("DV")),
                      ("AC", P.val("AC")), ("PO", P.val("PO"))]

    P.barrier()
    if g["debug"]:
        ECUMD = nc.dram_tensor("ECUMD", [128, NCH * 32], F32, kind="ExternalOutput")
        ECCD = nc.dram_tensor("ECCD", [128, NCH * 32], F32, kind="ExternalOutput")
        P.dma("sync", "SW", ECUMD[:, :], ECUMR[:, :])
        P.dma("sync", "SW", ECCD[:, :], ECCR[:, :])
    esB.close()
    return dict(ECUMR=ECUMR, ECCR=ECCR)


def emit_phase_b1(nc, P, es, g, st):
    PB = g["PB"]
    QS, B0M, OS = g["QS"], g["B0M"], g["OS"]
    ECUMR, ECCR = st["ECUMR"], st["ECCR"]

    P.barrier()
    esB1 = contextlib.ExitStack()
    sb = lambda name, shape, dt: esB1.enter_context(nc.sbuf_tensor(name, shape, dt))

    S0ALL = sb("S0ALL", [128, NK * 256], F32)
    S0B = sb("S0B", [128, NK * 256], BF16)
    FD1 = sb("FD1", [128, 8], F32)
    RB = [sb(f"RB{i}", [128, NK * 1024], BF16) for i in range(2)]
    QF1 = [sb(f"QF1_{i}", [128, 2048], BF16) for i in range(2)]
    QSCT = sb("QSCT", [128, 256], F32)
    OT1 = sb("OT1", [128, 256], F32)
    STMP = sb("STMP", [128, 256], F32)
    DMP = sb("DMP", [128, 512], F32)
    ECU32 = sb("ECU32", [128, 32], F32)
    ECC32 = sb("ECC32", [128, 32], F32)
    OSTG = [sb(f"OSTG{i}", [128, NK * 256], BF16) for i in range(2)]

    fb = Banks(PB)
    qs_r = _r(QS, "(h p) t -> p h t", p=128)
    b0_r = _r(B0M, "(r p) m -> p r m", p=128)

    def colrep(tile, j, n):
        return bass.AP(tile, j, [[tile.shape[1], 128], [0, n]])

    ld_done = [None, None]
    ld_guard = [[], []]

    def load_chunk(c):
        buf = c % 2
        P.dma("sync", "LW", _r(RB[buf], "p (h m) -> p h m", m=1024),
              b0_r[:, c * NK:(c + 1) * NK, :], waits=ld_guard[buf])
        v2 = P.dma("sync", "LW", _r(QF1[buf], "p (h t) -> p h t", t=128),
                   qs_r[:, :, c * 128:(c + 1) * 128])
        ld_done[buf] = ("LW", v2)

    load_chunk(0)
    ost_guard = [[], []]
    sup_dv = 0
    for c in range(NCH):
        buf = c % 2
        if c + 1 < NCH:
            load_chunk(c + 1)
        if c % CHSEQ == 0:
            sup_dv = P.op("vector", lambda e: e.memset(S0ALL[:, :], 0.0),
                          waits=[("AC", P.val("AC"))])
        sbc = None
        sb_w = [("DV", sup_dv), ("PE", P.val("PE"))]
        for i in range(4):
            sbc = P.op("scalar", (lambda e, i=i: e.copy(
                S0B[:, i * 1024:(i + 1) * 1024], S0ALL[:, i * 1024:(i + 1) * 1024])),
                waits=sb_w if i == 0 else [],
                fence=(lambda e: e.copy(FD1[:, 0:1], S0B[:, 4095:4096]))
                if i == 3 else False)
        P.op("vector", (lambda e, c=c: e.tensor_copy(ECU32[:, :],
                                                     ECUMR[:, c * 32:(c + 1) * 32])),
             waits=[("DV", P.val("DV"))])
        P.op("vector", (lambda e, c=c: e.tensor_copy(ECC32[:, :],
                                                     ECCR[:, c * 32:(c + 1) * 32])))
        ovals = []
        if B1_MODE in ("oindpure", "oindmm", "oindq", "oinds"):
            for h in range(NK):
                rbase = h * 1024
                if B1_MODE in ("oindmm", "oindq", "oinds"):
                    slotQ, bwQ = fb.acquire()
                    pq = P.op("tensor", (lambda e, b=slotQ["t"], h=h, buf=buf:
                                         e.matmul(b[:, 0:256],
                                                  QF1[buf][:, h * 128:(h + 1) * 128],
                                                  S0B[:, h * 256:(h + 1) * 256],
                                                  start=True, stop=True)),
                              waits=[ld_done[buf], ("AC", sbc)] + list(bwQ))
                    slotO, bwO = fb.acquire()
                    slotS, bwS = fb.acquire()
                    po = ps_ = None
                    for p in range(2):
                        po = P.op("tensor", (lambda e, b=slotO["t"], h=h, p=p, buf=buf, rbase=rbase:
                                             e.matmul(b[:, p * 128:(p + 1) * 128],
                                                      RB[buf][:, rbase + p * 128:rbase + (p + 1) * 128],
                                                      S0B[:, h * 256 + p * 128:h * 256 + (p + 1) * 128],
                                                      start=(p == 0), stop=(p == 1))),
                                  waits=list(bwO) if p == 0 else [])
                    for p in range(2):
                        ps_ = P.op("tensor", (lambda e, b=slotS["t"], h=h, p=p, buf=buf, rbase=rbase:
                                              e.matmul(b[:, p * 128:(p + 1) * 128],
                                                       RB[buf][:, rbase + 256 + p * 128:rbase + 256 + (p + 1) * 128],
                                                       S0B[:, h * 256 + p * 128:h * 256 + (p + 1) * 128],
                                                       start=(p == 0), stop=(p == 1))),
                                   waits=list(bwS) if p == 0 else [])
                    if B1_MODE in ("oindq", "oinds"):
                        vq = None
                        for p in range(2):
                            vq = P.op("vector", (lambda e, b=slotQ["t"], p=p, h=h, c=c:
                                                 e.tensor_tensor(QSCT[:, p * 128:(p + 1) * 128],
                                                                 b[:, p * 128:(p + 1) * 128],
                                                                 colrep(ECU32, 2 * h + p, 128),
                                                                 ALU.mult)),
                                      waits=[("PE", pq)])
                        Banks.release(slotQ, "DV", vq)
                        v1 = P.op("vector", (lambda e, b=slotO["t"], h=h, buf=buf, rbase=rbase:
                                             e.tensor_tensor(OT1[:, :],
                                                             RB[buf][:, rbase + 512:rbase + 768],
                                                             b[:, 0:256], ALU.add)),
                                  waits=[("PE", po)])
                        Banks.release(slotO, "DV", v1)
                    else:
                        vq = P.op("vector", (lambda e, b=slotQ["t"]:
                                             e.tensor_copy(QSCT[:, :], b[:, 0:256])),
                                  waits=[("PE", pq)])
                        Banks.release(slotQ, "DV", vq)
                        v1 = P.op("vector", (lambda e, b=slotO["t"]:
                                             e.tensor_copy(OT1[:, :], b[:, 0:256])),
                                  waits=[("PE", po)])
                        Banks.release(slotO, "DV", v1)
                    if B1_MODE == "oinds":
                        vs1 = None
                        for p in range(2):
                            vs1 = P.op("vector", (lambda e, p=p, h=h, c=c:
                                                  e.tensor_tensor(STMP[:, p * 128:(p + 1) * 128],
                                                                  S0ALL[:, h * 256 + p * 128:h * 256 + (p + 1) * 128],
                                                                  colrep(ECC32, 2 * h + p, 128),
                                                                  ALU.mult)))
                        vs2 = P.op("vector", (lambda e, b=slotS["t"]:
                                              e.tensor_tensor(STMP[:, :], STMP[:, :],
                                                              b[:, 0:256], ALU.add)),
                                   waits=[("PE", ps_)])
                        Banks.release(slotS, "DV", vs2)
                        vs3 = P.op("vector", (lambda e, h=h, buf=buf, rbase=rbase:
                                              e.tensor_tensor(S0ALL[:, h * 256:(h + 1) * 256],
                                                              STMP[:, :],
                                                              RB[buf][:, rbase + 768:rbase + 1024],
                                                              ALU.add)))
                        sup_dv = vs3
                    else:
                        vs2 = P.op("vector", (lambda e, b=slotS["t"]:
                                              e.tensor_copy(STMP[:, :], b[:, 0:256])),
                                   waits=[("PE", ps_)])
                        Banks.release(slotS, "DV", vs2)
                v2 = P.op("vector", (lambda e, h=h, buf=buf, rbase=rbase:
                                     e.tensor_copy(OSTG[buf][:, h * 256:(h + 1) * 256],
                                                   RB[buf][:, rbase + 512:rbase + 768])),
                          waits=(ost_guard[buf] + [ld_done[buf]]) if h == 0 else [])
                ovals.append(("DV", v2))
            vfence = P.op("vector", (lambda e, buf=buf: e.tensor_copy(
                FD1[:, 1:2], OSTG[buf][:, 4095:4096])))
            stv = P.dma("sync", "SW", OS[c * 128:(c + 1) * 128, :], OSTG[buf][:, :],
                        waits=ovals + [("DV", vfence)])
            ost_guard[buf] = [("SW", stv)]
            ld_guard[buf] = [("PE", P.val("PE")), ("DV", P.val("DV"))]
            continue
        for h in range(NK):
            rbase = h * 1024
            slotQ, bwQ = fb.acquire()
            pq = P.op("tensor", (lambda e, b=slotQ["t"], h=h, buf=buf:
                                 e.matmul(b[:, 0:256], QF1[buf][:, h * 128:(h + 1) * 128],
                                          S0B[:, h * 256:(h + 1) * 256],
                                          start=True, stop=True)),
                      waits=[ld_done[buf], ("AC", sbc)] + list(bwQ))
            slotO, bwO = fb.acquire()
            slotS, bwS = fb.acquire()
            po = ps_ = None
            for p in range(2):
                po = P.op("tensor", (lambda e, b=slotO["t"], h=h, p=p, buf=buf, rbase=rbase:
                                     e.matmul(b[:, p * 128:(p + 1) * 128],
                                              RB[buf][:, rbase + p * 128:rbase + (p + 1) * 128],
                                              S0B[:, h * 256 + p * 128:h * 256 + (p + 1) * 128],
                                              start=(p == 0), stop=(p == 1))),
                          waits=list(bwO) if p == 0 else [])
            for p in range(2):
                ps_ = P.op("tensor", (lambda e, b=slotS["t"], h=h, p=p, buf=buf, rbase=rbase:
                                      e.matmul(b[:, p * 128:(p + 1) * 128],
                                               RB[buf][:, rbase + 256 + p * 128:rbase + 256 + (p + 1) * 128],
                                               S0B[:, h * 256 + p * 128:h * 256 + (p + 1) * 128],
                                               start=(p == 0), stop=(p == 1))),
                           waits=list(bwS) if p == 0 else [])
            vq = None
            for p in range(2):
                vq = P.op("vector", (lambda e, b=slotQ["t"], p=p, h=h, c=c:
                                     e.tensor_tensor(QSCT[:, p * 128:(p + 1) * 128],
                                                     b[:, p * 128:(p + 1) * 128],
                                                     colrep(ECU32, 2 * h + p, 128),
                                                     ALU.mult)),
                          waits=[("PE", pq)])
            Banks.release(slotQ, "DV", vq)
            v1 = P.op("vector", (lambda e, b=slotO["t"], h=h, buf=buf, rbase=rbase:
                                 e.tensor_tensor(OT1[:, :],
                                                 RB[buf][:, rbase + 512:rbase + 768],
                                                 b[:, 0:256], ALU.add)),
                      waits=[("PE", po)])
            Banks.release(slotO, "DV", v1)
            if B1_MODE == "oind":
                v2 = P.op("vector", (lambda e, h=h, buf=buf, rbase=rbase:
                                     e.tensor_copy(OSTG[buf][:, h * 256:(h + 1) * 256],
                                                   RB[buf][:, rbase + 512:rbase + 768])),
                          waits=(ost_guard[buf] + [ld_done[buf]]) if h == 0 else [])
            elif B1_MODE == "noq":
                v2 = P.op("vector", (lambda e, h=h, buf=buf:
                                     e.tensor_copy(OSTG[buf][:, h * 256:(h + 1) * 256],
                                                   OT1[:, :])),
                          waits=ost_guard[buf] if h == 0 else [])
            else:
                v2 = P.op("vector", (lambda e, h=h, buf=buf:
                                     e.tensor_tensor(OSTG[buf][:, h * 256:(h + 1) * 256],
                                                     OT1[:, :], QSCT[:, :], ALU.add)),
                          waits=ost_guard[buf] if h == 0 else [])
            ovals.append(("DV", v2))
            if g["debug"] and (c, h) == g.get("dbg_b1w", (None, None)):
                dt_ = g["dbg_tensor"]
                vdmp = P.op("vector", lambda e: e.tensor_copy(DMP[:, 0:256], QSCT[:, :]))
                vdmp = P.op("vector", lambda e: e.tensor_copy(DMP[:, 256:512], OT1[:, :]))
                P.dma("gpsimd", "SG", dt_[0:128, 0:512], DMP[:, :],
                      waits=[("DV", vdmp)])
                vdmp = P.op("vector", (lambda e, h=h: e.tensor_copy(
                    DMP[:, 0:256], S0B[:, h * 256:(h + 1) * 256])))
                vdmp = P.op("vector", (lambda e, buf=buf, rbase=rbase: e.tensor_copy(
                    DMP[:, 256:512], RB[buf][:, rbase + 512:rbase + 768])),
                    waits=[("SG", P.val("SG"))])
                P.dma("gpsimd", "SG", dt_[128:256, 0:512], DMP[:, :],
                      waits=[("DV", vdmp)])
            if B1_MODE == "fullq":
                vs2 = P.op("vector", (lambda e, b=slotS["t"]:
                                      e.tensor_copy(STMP[:, :], b[:, 0:256])),
                           waits=[("PE", ps_)])
                Banks.release(slotS, "DV", vs2)
                sup_dv = vs2
                continue
            vs1 = None
            for p in range(2):
                vs1 = P.op("vector", (lambda e, p=p, h=h, c=c:
                                      e.tensor_tensor(STMP[:, p * 128:(p + 1) * 128],
                                                      S0ALL[:, h * 256 + p * 128:h * 256 + (p + 1) * 128],
                                                      colrep(ECC32, 2 * h + p, 128),
                                                      ALU.mult)))
            vs2 = P.op("vector", (lambda e, b=slotS["t"]:
                                  e.tensor_tensor(STMP[:, :], STMP[:, :],
                                                  b[:, 0:256], ALU.add)),
                       waits=[("PE", ps_)])
            Banks.release(slotS, "DV", vs2)
            vs3 = P.op("vector", (lambda e, h=h, buf=buf, rbase=rbase:
                                  e.tensor_tensor(S0ALL[:, h * 256:(h + 1) * 256],
                                                  STMP[:, :],
                                                  RB[buf][:, rbase + 768:rbase + 1024],
                                                  ALU.add)))
            sup_dv = vs3
        vfence = P.op("vector", (lambda e, buf=buf: e.tensor_copy(
            FD1[:, 1:2], OSTG[buf][:, 4095:4096])))
        stv = P.dma("sync", "SW", OS[c * 128:(c + 1) * 128, :], OSTG[buf][:, :],
                    waits=ovals + [("DV", vfence)])
        ost_guard[buf] = [("SW", stv)]
        ld_guard[buf] = [("PE", P.val("PE")), ("DV", P.val("DV"))]

    P.barrier()
    esB1.close()


def emit_precast(nc, P, es, g):
    woutt, wgut, wdot = g["woutt"], g["wgut"], g["wdot"]
    woutb, wgutb, wdotb = g["woutb"], g["wgutb"], g["wdotb"]
    PCS = [es.enter_context(nc.sbuf_tensor(f"PCS{i}", [128, 4096], BF16))
           for i in range(2)]
    pieces = []
    for rc in range(16):
        for fc in range(4):
            pieces.append((wgut, wgutb, rc * 128, fc * 4096, 4096))
    for rc in range(64):
        pieces.append((wdot, wdotb, rc * 128, 0, 2048))
    for rc in range(32):
        pieces.append((woutt, woutb, rc * 128, 0, 2048))
    st_done = [0, 0]
    for i, (src, dst, r0, c0, wid) in enumerate(pieces):
        buf = i % 2
        ld = P.dma("gpsimd", "PL", PCS[buf][:, 0:wid],
                   src[r0:r0 + 128, c0:c0 + wid],
                   waits=[("SC", st_done[buf])] if st_done[buf] else ())
        st_done[buf] = P.dma("gpsimd", "SC", dst[r0:r0 + 128, c0:c0 + wid],
                             PCS[buf][:, 0:wid], waits=[("PL", ld)])
    P.pc_done = P.val("SC")


def emit_phase_c(nc, P, es, g):
    PB = g["PB"]
    OS, ZS, H1F, H1B = g["OS"], g["ZS"], g["H1F"], g["H1B"]
    xt, woutb = g["xt"], g["woutb"]
    s_onesc, s_ones1, s_idf, s_vec = (g["s_onesc"], g["s_ones1"], g["s_idf"],
                                      g["s_vec"])

    P.barrier()
    esC = contextlib.ExitStack()
    sb = lambda name, shape, dt: esC.enter_context(nc.sbuf_tensor(name, shape, dt))

    TT = 512
    OTB = sb("OTB", [128, NV * TT], BF16)
    ZTB = sb("ZTB", [128, NV * TT], BF16)
    WOC = [sb(f"WOC{i}", [128, NV * 128], BF16) for i in range(2)]
    HC = sb("HC", [128, DCH * TT], F32)
    HBC = sb("HBC", [128, DCH * TT], BF16)
    SQC = sb("SQC", [128, TT], F32)
    RR = sb("RR", [1, TT], F32)
    LR = sb("LR", [1, 4 * TT], F32)
    NWC = sb("NWC", [128, 1], F32)
    FCC = sb("FCC", [128, 4], F32)
    DMPC = sb("DMPC", [128, 3 * TT], F32)

    fb = Banks(PB)

    def colrep(tile, j, n):
        return bass.AP(tile, j, [[tile.shape[1], 128], [0, n]])

    slot, bw = fb.acquire()
    pe = P.op("tensor", (lambda e, b=slot["t"]:
                         e.transpose(b[:, 0:1], s_vec[0:1, 64:192],
                                     s_idf[0:1, 0:1])),
              waits=list(bw))
    nwv = P.op("vector", (lambda e, b=slot["t"]: e.tensor_copy(NWC[:, :], b[:, 0:1])),
               waits=[("PE", pe)])
    Banks.release(slot, "DV", nwv)

    Gd = {}
    for tt in range(TOK // TT):
        t0 = tt * TT
        otld = None
        w0 = Gd.pop("OTZT", [])
        for v in range(NV):
            P.dma("sync", "LW", OTB[:, v * TT:(v + 1) * TT],
                  OS[t0:t0 + TT, v * 128:(v + 1) * 128], transpose=True,
                  waits=w0 if v == 0 else ())
            otld = P.dma("sync", "LW", ZTB[:, v * TT:(v + 1) * TT],
                         ZS[t0:t0 + TT, v * 128:(v + 1) * 128], transpose=True)
        ld_done = ("LW", otld)
        gvals = []
        sq_guard = Gd.pop("SQC", [])
        rr_guard = Gd.pop("RR", [])
        for v in range(NV):
            sq = P.op("vector", (lambda e, v=v: e.tensor_tensor(
                SQC[:, :], OTB[:, v * TT:(v + 1) * TT],
                OTB[:, v * TT:(v + 1) * TT], ALU.mult)),
                waits=[ld_done] + (sq_guard if v == 0 else []),
                fence=lambda e: e.tensor_copy(FCC[:, 0:1], SQC[:, TT - 1:TT]))
            slot, bw = fb.acquire()
            pe = P.op("tensor", (lambda e, b=slot["t"]:
                                 e.matmul(b[0:1, 0:TT], s_onesc[:, :], SQC[:, :],
                                          start=True, stop=True)),
                      waits=[("DV", sq)] + list(bw))
            rp = P.op("vector", (lambda e, b=slot["t"]:
                                 e.tensor_scalar(RR[0:1, :], b[0:1, 0:TT],
                                                 1.0 / 128.0, EPS_RMS,
                                                 ALU.mult, ALU.add)),
                      waits=[("PE", pe)] + (rr_guard if v == 0 else []),
                      fence=lambda e: e.tensor_copy(FCC[0:1, 0:1],
                                                    RR[0:1, TT - 1:TT]))
            Banks.release(slot, "DV", rp)
            ac = P.op("scalar", lambda e: e.activation(RR[0:1, :], RR[0:1, :],
                                                       AF.Sqrt),
                      waits=[("DV", rp)],
                      fence=lambda e: e.copy(FCC[0:1, 1:2], RR[0:1, TT - 1:TT]))
            rv = P.op("vector", lambda e: e.reciprocal(RR[0:1, :], RR[0:1, :]),
                      waits=[("AC", ac)],
                      fence=lambda e: e.tensor_copy(FCC[0:1, 0:1], RR[0:1, TT - 1:TT]))
            bslot, bbw = fb.acquire()
            pe2 = P.op("tensor", (lambda e, b=bslot["t"]:
                                  e.matmul(b[:, 0:TT], s_ones1[:, :], RR[0:1, :],
                                           start=True, stop=True)),
                       waits=[("DV", rv)] + list(bbw))
            if g["debug"] and tt == 0 and v == NV - 1:
                P.op("vector", (lambda e, b=bslot["t"]: e.tensor_copy(
                    DMPC[:, 0:TT], b[:, 0:TT])), waits=[("PE", pe2)])
            g1 = P.op("vector", (lambda e, b=bslot["t"], v=v: e.tensor_tensor(
                OTB[:, v * TT:(v + 1) * TT], OTB[:, v * TT:(v + 1) * TT],
                b[:, 0:TT], ALU.mult)), waits=[("PE", pe2)])
            Banks.release(bslot, "DV", g1)
            if g["debug"] and tt == 0 and v == NV - 1:
                P.op("vector", (lambda e, v=v: e.tensor_copy(
                    DMPC[:, TT:2 * TT], OTB[:, v * TT:(v + 1) * TT])))
            g2 = P.op("vector", (lambda e, v=v: e.tensor_tensor(
                OTB[:, v * TT:(v + 1) * TT], OTB[:, v * TT:(v + 1) * TT],
                colrep(NWC, 0, TT), ALU.mult)))
            if g["debug"] and tt == 0 and v == NV - 1:
                P.op("vector", (lambda e, v=v: e.tensor_copy(
                    DMPC[:, 2 * TT:3 * TT], OTB[:, v * TT:(v + 1) * TT])))
            g3 = P.op("vector", (lambda e, v=v: e.tensor_tensor(
                OTB[:, v * TT:(v + 1) * TT], OTB[:, v * TT:(v + 1) * TT],
                ZTB[:, v * TT:(v + 1) * TT], ALU.mult)),
                fence=(lambda e, v=v: e.tensor_copy(
                    FCC[:, 1:2], OTB[:, (v + 1) * TT - 1:(v + 1) * TT])))
            gvals.append(("DV", g3))
        dbgw = []
        if g["debug"] and tt == 0 and g.get("dbg_b1") is not None:
            P.dma("gpsimd", "SG", g["dbg_b1"][0:128, :], OTB[:, 0:4096],
                  waits=gvals)
            P.dma("gpsimd", "SG", g["dbg_b1"][128:129, 0:512], RR[0:1, :],
                  waits=gvals)
            P.dma("gpsimd", "SG", g["dbg_b1"][129:130, 0:512],
                  DMPC[0:1, 0:TT])
            P.dma("gpsimd", "SG", g["dbg_b1"][130:131, 0:512],
                  DMPC[0:1, TT:2 * TT])
            P.dma("gpsimd", "SG", g["dbg_b1"][131:132, 0:512],
                  OTB[0:1, 31 * TT:32 * TT])
            P.dma("gpsimd", "SG", g["dbg_b1"][132:133, 0:512],
                  DMPC[0:1, 2 * TT:3 * TT])
            dbgw = [("SG", P.val("SG"))]
        Gd["SQC"] = [("PE", P.val("PE"))]
        Gd["RR"] = [("PE", P.val("PE")), ("DV", P.val("DV"))] + dbgw
        wo_done = [0, 0]
        hvals = []
        hc_guard = Gd.pop("HC", [])
        for c in range(DCH):
            wb = c % 2
            wo_w = ([("SC", getattr(P, "pc_done", 0))]
                    if (tt == 0 and c == 0) else [])
            wo_done[wb] = P.dma("sync", "LW",
                                _r(WOC[wb], "p (v m) -> p v m", m=128),
                                _r(woutb, "(v p) m -> p v m", p=128)
                                [:, :, c * 128:(c + 1) * 128],
                                waits=wo_w + ([("PE", wo_done[wb])]
                                              if wo_done[wb] else []))
            xl = P.dma("gpsimd", "LG", HC[:, c * TT:(c + 1) * TT],
                       xt[c * 128:(c + 1) * 128, t0:t0 + TT],
                       waits=hc_guard if c < 2 else ())
            slot, bw = fb.acquire()
            pe = None
            for v in range(NV):
                w = list(bw) + (gvals + [("LW", wo_done[wb])] if v == 0 else [])
                pe = P.op("tensor", (lambda e, b=slot["t"], v=v, wb=wb:
                                     e.matmul(b[:, 0:TT],
                                              WOC[wb][:, v * 128:(v + 1) * 128],
                                              OTB[:, v * TT:(v + 1) * TT],
                                              start=(v == 0), stop=(v == NV - 1))),
                          waits=w)
                bw = ()
            wo_done[wb] = pe
            hv = P.op("vector", (lambda e, b=slot["t"], c=c: e.tensor_tensor(
                HC[:, c * TT:(c + 1) * TT], HC[:, c * TT:(c + 1) * TT],
                b[:, 0:TT], ALU.add)), waits=[("PE", pe), ("LG", xl)],
                fence=(lambda e, c=c: e.tensor_copy(
                    FCC[:, 3:4], HC[:, (c + 1) * TT - 1:(c + 1) * TT]))
                if c == DCH - 1 else False)
            Banks.release(slot, "DV", hv)
            hvals.append(("DV", hv))
        Gd["OTZT"] = [("PE", P.val("PE")), ("DV", P.val("DV"))] + dbgw
        sslot, sbw = fb.acquire()
        pe = None
        for c in range(DCH):
            pe = P.op("tensor", (lambda e, b=sslot["t"], c=c:
                                 e.matmul(b[0:1, 0:TT], s_onesc[:, :],
                                          HC[:, c * TT:(c + 1) * TT],
                                          start=(c == 0), stop=(c == DCH - 1))),
                      waits=(hvals + list(sbw)) if c == 0 else [])
        sum_pe = pe
        qslot, qbw = fb.acquire()
        sq_pe = None
        sqv = None
        for c in range(DCH):
            sqv = P.op("vector", (lambda e, c=c: e.tensor_tensor(
                SQC[:, :], HC[:, c * TT:(c + 1) * TT],
                HC[:, c * TT:(c + 1) * TT], ALU.mult)),
                waits=([("PE", sum_pe)] if c == 0 else []) +
                      ([("PE", sq_pe)] if sq_pe else []),
                fence=lambda e: e.tensor_copy(FCC[:, 0:1], SQC[:, TT - 1:TT]))
            sq_pe = P.op("tensor", (lambda e, b=qslot["t"], c=c:
                                    e.matmul(b[0:1, 0:TT], s_onesc[:, :], SQC[:, :],
                                             start=(c == 0), stop=(c == DCH - 1))),
                         waits=[("DV", sqv)] + (list(qbw) if c == 0 else []))
        Gd["SQC"].append(("PE", sq_pe))
        r1 = P.op("vector", (lambda e, b=sslot["t"]: e.tensor_scalar_mul(
            LR[0:1, 0:TT], b[0:1, 0:TT], 1.0 / D)), waits=[("PE", sum_pe)])
        Banks.release(sslot, "DV", r1)
        r2 = P.op("vector", (lambda e, b=qslot["t"]: e.tensor_scalar(
            LR[0:1, TT:2 * TT], b[0:1, 0:TT], 1.0 / D, EPS_LN,
            ALU.mult, ALU.add)), waits=[("PE", sq_pe)])
        Banks.release(qslot, "DV", r2)
        r3 = P.op("vector", lambda e: e.tensor_tensor(
            LR[0:1, 2 * TT:3 * TT], LR[0:1, 0:TT], LR[0:1, 0:TT], ALU.mult))
        r4 = P.op("vector", lambda e: e.tensor_tensor(
            LR[0:1, TT:2 * TT], LR[0:1, TT:2 * TT], LR[0:1, 2 * TT:3 * TT],
            ALU.subtract),
            fence=lambda e: e.tensor_copy(FCC[0:1, 0:1], LR[0:1, 2 * TT - 1:2 * TT]))
        r5 = P.op("scalar", lambda e: e.activation(
            LR[0:1, TT:2 * TT], LR[0:1, TT:2 * TT], AF.Sqrt),
            waits=[("DV", r4)],
            fence=lambda e: e.copy(FCC[0:1, 1:2], LR[0:1, 2 * TT - 1:2 * TT]))
        r6 = P.op("vector", lambda e: e.reciprocal(
            LR[0:1, TT:2 * TT], LR[0:1, TT:2 * TT]), waits=[("AC", r5)],
            fence=lambda e: e.tensor_copy(FCC[0:1, 0:1], LR[0:1, 2 * TT - 1:2 * TT]))
        mslot, mbw = fb.acquire()
        mpe = P.op("tensor", (lambda e, b=mslot["t"]:
                              e.matmul(b[:, 0:TT], s_ones1[:, :], LR[0:1, 0:TT],
                                       start=True, stop=True)),
                   waits=[("DV", r6)] + list(mbw))
        rslot, rbw = fb.acquire()
        rpe = P.op("tensor", (lambda e, b=rslot["t"]:
                              e.matmul(b[:, 0:TT], s_ones1[:, :],
                                       LR[0:1, TT:2 * TT], start=True, stop=True)),
                   waits=list(rbw))
        h1v = None
        hb_guard = Gd.pop("HBC", [])
        for c in range(DCH):
            n1 = P.op("vector", (lambda e, b=mslot["t"], c=c: e.tensor_tensor(
                HC[:, c * TT:(c + 1) * TT], HC[:, c * TT:(c + 1) * TT],
                b[:, 0:TT], ALU.subtract)), waits=[("PE", mpe), ("PE", rpe)]
                if c == 0 else [])
            n2 = P.op("vector", (lambda e, b=rslot["t"], c=c: e.tensor_tensor(
                HC[:, c * TT:(c + 1) * TT], HC[:, c * TT:(c + 1) * TT],
                b[:, 0:TT], ALU.mult)),
                fence=(lambda e, c=c: e.tensor_copy(
                    FCC[:, 1:2], HC[:, (c + 1) * TT - 1:(c + 1) * TT])))
            hb = P.op("scalar", (lambda e, c=c: e.copy(
                HBC[:, c * TT:(c + 1) * TT], HC[:, c * TT:(c + 1) * TT])),
                waits=[("DV", n2)] + (hb_guard if c == 0 else []),
                fence=(lambda e, c=c: e.copy(
                    FCC[:, 2:3], HBC[:, (c + 1) * TT - 1:(c + 1) * TT])))
            P.dma("sync", "SW", H1F[c * 128:(c + 1) * 128, t0:t0 + TT],
                  HC[:, c * TT:(c + 1) * TT], waits=[("DV", n2)])
            h1v = P.dma("scalar", "SS", H1B[c * 128:(c + 1) * 128, t0:t0 + TT],
                        HBC[:, c * TT:(c + 1) * TT], waits=[("AC", hb)])
        Banks.release(mslot, "DV", P.val("DV"))
        Banks.release(rslot, "DV", P.val("DV"))
        Gd["HC"] = [("SW", P.val("SW")), ("AC", P.val("AC"))]
        Gd["HBC"] = [("SS", P.val("SS"))]
    P.barrier()
    esC.close()


def emit_phase_d(nc, P, es, g):
    PB = g["PB"]
    H1F, H1B, FFS, out = g["H1F"], g["H1B"], g["FFS"], g["out"]
    wgutb, wdotb = g["wgutb"], g["wdotb"]
    s_onesc, s_ones1, s_idf = g["s_onesc"], g["s_ones1"], g["s_idf"]

    P.barrier()
    esD = contextlib.ExitStack()
    sb = lambda name, shape, dt: esD.enter_context(nc.sbuf_tensor(name, shape, dt))
    MT = 1024
    H1S = sb("H1S", [128, DCH * MT], BF16)
    WG = [sb(f"WG{i}", [128, DCH * 256], BF16) for i in range(2)]
    FFO = [sb(f"FFO{i}", [128, MT], BF16) for i in range(2)]
    FCD = sb("FCD", [128, 4], F32)
    fb = Banks(PB)
    NJ = INTER // 128

    wg_r = _r(wgutb, "(k p) f -> p k f", p=128)
    h1_r = _r(H1B, "(k p) t -> p k t", p=128)
    wg_done = [0, 0]
    ffo_guard = [[], []]
    for mt in range(TOK // MT):
        t0 = mt * MT
        h1ld = P.dma("sync", "LW", _r(H1S, "p (k t) -> p k t", t=MT),
                     h1_r[:, :, t0:t0 + MT],
                     waits=[("PE", P.val("PE"))] if mt else ())
        for j in range(NJ):
            wb = j % 2
            wg_w = ([("SC", getattr(P, "pc_done", 0))]
                    if (mt == 0 and j == 0) else [])
            wld = P.dma("sync", "LW",
                        _r(WG[wb], "p (k f) -> p k f", f=256)[:, :, 0:128],
                        wg_r[:, :, j * 128:(j + 1) * 128],
                        waits=wg_w + ([("PE", wg_done[wb])]
                                      if wg_done[wb] else []))
            wld = P.dma("sync", "LW",
                        _r(WG[wb], "p (k f) -> p k f", f=256)[:, :, 128:256],
                        wg_r[:, :, INTER + j * 128:INTER + (j + 1) * 128])
            gsl = [fb.acquire() for _ in range(2)]
            usl = [fb.acquire() for _ in range(2)]
            upe = None
            for half in range(2):
                slot, bw = gsl[half]
                for kk in range(DCH):
                    w = list(bw) + ([("LW", max(wld, h1ld))] if kk == 0 else [])
                    P.op("tensor", (lambda e, b=slot["t"], kk=kk, half=half, wb=wb:
                                    e.matmul(b[:, 0:512],
                                             WG[wb][:, kk * 256:kk * 256 + 128],
                                             H1S[:, kk * MT + half * 512:
                                                 kk * MT + (half + 1) * 512],
                                             start=(kk == 0),
                                             stop=(kk == DCH - 1))),
                         waits=w)
                    bw = ()
            for half in range(2):
                slot, bw = usl[half]
                for kk in range(DCH):
                    w = list(bw)
                    upe = P.op("tensor", (lambda e, b=slot["t"], kk=kk, half=half,
                                          wb=wb:
                                          e.matmul(b[:, 0:512],
                                                   WG[wb][:, kk * 256 + 128:
                                                          (kk + 1) * 256],
                                                   H1S[:, kk * MT + half * 512:
                                                       kk * MT + (half + 1) * 512],
                                                   start=(kk == 0),
                                                   stop=(kk == DCH - 1))),
                               waits=w)
                    bw = ()
            wg_done[wb] = upe
            fo = j % 2
            dv = None
            for half in range(2):
                ac = P.op("scalar", (lambda e, b=gsl[half][0]["t"], fo=fo,
                                     half=half:
                                     e.activation(FFO[fo][:, half * 512:
                                                          (half + 1) * 512],
                                                  b[:, 0:512], AF.Silu)),
                          waits=[("PE", upe)] + (ffo_guard[fo] if half == 0
                                                 else []),
                          fence=(lambda e, fo=fo, half=half: e.copy(
                              FCD[:, 2:3],
                              FFO[fo][:, (half + 1) * 512 - 1:(half + 1) * 512])))
                Banks.release(gsl[half][0], "AC", ac)
                dv = P.op("vector", (lambda e, b=usl[half][0]["t"], fo=fo,
                                     half=half:
                                     e.tensor_tensor(FFO[fo][:, half * 512:
                                                             (half + 1) * 512],
                                                     FFO[fo][:, half * 512:
                                                             (half + 1) * 512],
                                                     b[:, 0:512], ALU.mult)),
                          waits=[("AC", ac)],
                          fence=(lambda e, fo=fo, half=half: e.tensor_copy(
                              FCD[:, 0:1],
                              FFO[fo][:, (half + 1) * 512 - 1:(half + 1) * 512])))
                Banks.release(usl[half][0], "DV", dv)
            stv = P.dma("sync", "SW", FFS[j * 128:(j + 1) * 128, t0:t0 + MT],
                        FFO[fo][:, :], waits=[("DV", dv)])
            ffo_guard[fo] = [("SW", stv)]
    P.barrier()
    esD.close()

    esD2 = contextlib.ExitStack()
    sb = lambda name, shape, dt: esD2.enter_context(nc.sbuf_tensor(name, shape, dt))
    TT = 512
    NI = INTER // 128
    FFSB = sb("FFSB", [128, NI * TT], BF16)
    WDC = [sb(f"WDC{i}", [128, NI * 128], BF16) for i in range(2)]
    HD = sb("HD", [128, DCH * TT], F32)
    SQD = sb("SQD", [128, TT], F32)
    LRD = sb("LRD", [1, 3 * TT], F32)
    OTS = sb("OTS", [128, TT], F32)
    FCD2 = sb("FCD2", [128, 4], F32)
    fb = Banks(PB)
    ffs_r = _r(FFS, "(i p) t -> p i t", p=128)
    wd_r = _r(wdotb, "(i p) m -> p i m", p=128)
    wd_done = [0, 0]
    Gd = {}
    ovals = []
    for tt in range(TOK // TT):
        t0 = tt * TT
        fld = P.dma("sync", "LW", _r(FFSB, "p (i t) -> p i t", t=TT),
                    ffs_r[:, :, t0:t0 + TT],
                    waits=Gd.pop("FFSB", []))
        hvals = []
        hd_guard = Gd.pop("HD", [])
        for c in range(DCH):
            wb = c % 2
            wd_done[wb] = P.dma("sync", "LW",
                                _r(WDC[wb], "p (i m) -> p i m", m=128),
                                wd_r[:, :, c * 128:(c + 1) * 128],
                                waits=[("PE", wd_done[wb])] if wd_done[wb] else ())
            xl = P.dma("gpsimd", "LG", HD[:, c * TT:(c + 1) * TT],
                       H1F[c * 128:(c + 1) * 128, t0:t0 + TT],
                       waits=hd_guard if c < 2 else ())
            slot, bw = fb.acquire()
            pe = None
            for i in range(NI):
                w = (list(bw) + [("LW", max(wd_done[wb], fld))]) if i == 0 else []
                pe = P.op("tensor", (lambda e, b=slot["t"], i=i, wb=wb:
                                     e.matmul(b[:, 0:TT],
                                              WDC[wb][:, i * 128:(i + 1) * 128],
                                              FFSB[:, i * TT:(i + 1) * TT],
                                              start=(i == 0), stop=(i == NI - 1))),
                          waits=w)
                bw = ()
            wd_done[wb] = pe
            hv = P.op("vector", (lambda e, b=slot["t"], c=c: e.tensor_tensor(
                HD[:, c * TT:(c + 1) * TT], HD[:, c * TT:(c + 1) * TT],
                b[:, 0:TT], ALU.add)), waits=[("PE", pe), ("LG", xl)],
                fence=(lambda e, c=c: e.tensor_copy(
                    FCD2[:, 3:4], HD[:, (c + 1) * TT - 1:(c + 1) * TT]))
                if c == DCH - 1 else False)
            Banks.release(slot, "DV", hv)
            hvals.append(("DV", hv))
        Gd["FFSB"] = [("PE", P.val("PE"))]
        sslot, sbw = fb.acquire()
        pe = None
        for c in range(DCH):
            pe = P.op("tensor", (lambda e, b=sslot["t"], c=c:
                                 e.matmul(b[0:1, 0:TT], s_onesc[:, :],
                                          HD[:, c * TT:(c + 1) * TT],
                                          start=(c == 0), stop=(c == DCH - 1))),
                      waits=(hvals + list(sbw)) if c == 0 else [])
        sum_pe = pe
        qslot, qbw = fb.acquire()
        sq_pe = None
        for c in range(DCH):
            sqv = P.op("vector", (lambda e, c=c: e.tensor_tensor(
                SQD[:, :], HD[:, c * TT:(c + 1) * TT],
                HD[:, c * TT:(c + 1) * TT], ALU.mult)),
                waits=([("PE", sum_pe)] if c == 0 else []) +
                      ([("PE", sq_pe)] if sq_pe else []),
                fence=lambda e: e.tensor_copy(FCD2[:, 0:1], SQD[:, TT - 1:TT]))
            sq_pe = P.op("tensor", (lambda e, b=qslot["t"], c=c:
                                    e.matmul(b[0:1, 0:TT], s_onesc[:, :], SQD[:, :],
                                             start=(c == 0), stop=(c == DCH - 1))),
                         waits=[("DV", sqv)] + (list(qbw) if c == 0 else []))
        r1 = P.op("vector", (lambda e, b=sslot["t"]: e.tensor_scalar_mul(
            LRD[0:1, 0:TT], b[0:1, 0:TT], 1.0 / D)), waits=[("PE", sum_pe)])
        Banks.release(sslot, "DV", r1)
        r2 = P.op("vector", (lambda e, b=qslot["t"]: e.tensor_scalar(
            LRD[0:1, TT:2 * TT], b[0:1, 0:TT], 1.0 / D, EPS_LN,
            ALU.mult, ALU.add)), waits=[("PE", sq_pe)])
        Banks.release(qslot, "DV", r2)
        r3 = P.op("vector", lambda e: e.tensor_tensor(
            LRD[0:1, 2 * TT:3 * TT], LRD[0:1, 0:TT], LRD[0:1, 0:TT], ALU.mult))
        r4 = P.op("vector", lambda e: e.tensor_tensor(
            LRD[0:1, TT:2 * TT], LRD[0:1, TT:2 * TT], LRD[0:1, 2 * TT:3 * TT],
            ALU.subtract),
            fence=lambda e: e.tensor_copy(FCD2[0:1, 0:1],
                                          LRD[0:1, 2 * TT - 1:2 * TT]))
        r5 = P.op("scalar", lambda e: e.activation(
            LRD[0:1, TT:2 * TT], LRD[0:1, TT:2 * TT], AF.Sqrt),
            waits=[("DV", r4)],
            fence=lambda e: e.copy(FCD2[0:1, 1:2], LRD[0:1, 2 * TT - 1:2 * TT]))
        r6 = P.op("vector", lambda e: e.reciprocal(
            LRD[0:1, TT:2 * TT], LRD[0:1, TT:2 * TT]), waits=[("AC", r5)],
            fence=lambda e: e.tensor_copy(FCD2[0:1, 0:1],
                                          LRD[0:1, 2 * TT - 1:2 * TT]))
        mslot, mbw = fb.acquire()
        mpe = P.op("tensor", (lambda e, b=mslot["t"]:
                              e.matmul(b[:, 0:TT], s_ones1[:, :], LRD[0:1, 0:TT],
                                       start=True, stop=True)),
                   waits=[("DV", r6)] + list(mbw))
        rslot, rbw = fb.acquire()
        rpe = P.op("tensor", (lambda e, b=rslot["t"]:
                              e.matmul(b[:, 0:TT], s_ones1[:, :],
                                       LRD[0:1, TT:2 * TT], start=True, stop=True)),
                   waits=list(rbw))
        n2v = []
        for c in range(DCH):
            n1 = P.op("vector", (lambda e, b=mslot["t"], c=c: e.tensor_tensor(
                HD[:, c * TT:(c + 1) * TT], HD[:, c * TT:(c + 1) * TT],
                b[:, 0:TT], ALU.subtract)), waits=[("PE", mpe), ("PE", rpe)]
                if c == 0 else [])
            n2 = P.op("vector", (lambda e, b=rslot["t"], c=c: e.tensor_tensor(
                HD[:, c * TT:(c + 1) * TT], HD[:, c * TT:(c + 1) * TT],
                b[:, 0:TT], ALU.mult)),
                fence=(lambda e, c=c: e.tensor_copy(
                    FCD2[:, 1:2], HD[:, (c + 1) * TT - 1:(c + 1) * TT])))
            n2v.append(("DV", n2))
        Banks.release(mslot, "DV", P.val("DV"))
        Banks.release(rslot, "DV", P.val("DV"))
        for c in range(DCH):
            tslot, tbw = fb.acquire()
            tpe = None
            for tq in range(TT // 128):
                tpe = P.op("tensor", (lambda e, b=tslot["t"], c=c, tq=tq:
                                      e.transpose(b[:, tq * 128:(tq + 1) * 128],
                                                  HD[:, c * TT + tq * 128:
                                                     c * TT + (tq + 1) * 128],
                                                  s_idf[:, :])),
                           waits=([n2v[c]] + list(tbw) + ovals[-1:])
                           if tq == 0 else [])
            ev = P.op("vector", (lambda e, b=tslot["t"]:
                                 e.tensor_copy(OTS[:, :], b[:, 0:TT])),
                      waits=[("PE", tpe)],
                      fence=lambda e: e.tensor_copy(FCD2[:, 2:3], OTS[:, TT - 1:TT]))
            Banks.release(tslot, "DV", ev)
            sv = None
            for tq in range(TT // 128):
                sv = P.dma("sync", "SW",
                           out[t0 + tq * 128:t0 + (tq + 1) * 128,
                               c * 128:(c + 1) * 128],
                           OTS[:, tq * 128:(tq + 1) * 128],
                           waits=[("DV", ev)] if tq == 0 else ())
            ovals = [("SW", sv)]
        Gd["HD"] = [("PE", P.val("PE")), ("DV", P.val("DV")), ("SW", P.val("SW"))]
    P.barrier()
    esD2.close()


def _prep_inputs(ins):
    w = {}
    w["wqkvt"] = np.ascontiguousarray(ins["L0_attn_W_qkv"].T)
    w["wzt"] = np.ascontiguousarray(ins["L0_attn_W_z"].T)
    w["wbat"] = np.ascontiguousarray(
        np.concatenate([ins["L0_attn_W_b"], ins["L0_attn_W_a"]], axis=0).T)
    w["convw"] = np.ascontiguousarray(ins["L0_attn_conv_w"])
    vec = np.zeros((1, 192), dtype=np.float32)
    vec[0, 0:32] = ins["L0_attn_dt_bias"]
    vec[0, 32:64] = ins["L0_attn_A_log"]
    vec[0, 64:192] = ins["L0_attn_norm_w"]
    w["vecs"] = vec
    w["woutt"] = np.ascontiguousarray(ins["L0_attn_W_out"].T)
    w["wgut"] = np.ascontiguousarray(ins["L0_ffn_w_gate_up"].T)
    w["wdot"] = np.ascontiguousarray(ins["L0_ffn_w_down"].T)
    return w


def _core_inputs(x, c, w):
    xs = np.asarray(x[2 * c:2 * c + 2], dtype=np.float32)
    m = dict(w)
    m["xt"] = np.ascontiguousarray(xs.transpose(2, 0, 1).reshape(D, TOK))
    m["xr"] = np.ascontiguousarray(xs.reshape(TOK, D))
    return m


def kernel(**inputs):
    x = np.asarray(inputs["L0_attn_x"], dtype=np.float32)
    w = _prep_inputs(inputs)
    nc = build_program()
    in_maps = [_core_inputs(x, c, w) for c in range(8)]
    res = run_bass_kernel_spmd(nc, in_maps, core_ids=list(range(8)))
    outs = [np.asarray(r["out"], dtype=np.float32).reshape(2, TSEQ, D)
            for r in res.results]
    return np.concatenate(outs, axis=0)
